# version 02
# baseline (speedup 1.0000x reference)
import numpy as np
import ml_dtypes
import concourse.bass as bass
import concourse.mybir as mybir
from concourse.bass_utils import run_bass_kernel_spmd

F32 = mybir.dt.float32
BF16 = mybir.dt.bfloat16
AF = mybir.ActivationFunctionType
ALU = mybir.AluOpType
AX = mybir.AxisListType

P = 128
C = 1024
T = 4096
HID = 4096
TC = 256
NCH = T // TC
NCC = C // P
NHC = HID // P
NTT = TC // P
MAGIC = 12582912.0
ARENA_F32 = 52224

ENGINES = ("tensor", "vector", "scalar", "gpsimd", "sync")
DMAQ = ("sync", "scalar", "gpsimd")
NLANES = 3


class Em:

    def __init__(self):
        self.ops = {e: [] for e in ENGINES}
        self.ticks = {e: 0 for e in ENGINES}
        self.dmas = {q: [0] * NLANES for q in DMAQ}
        self.rr = {q: 0 for q in DMAQ}
        self.pending = {e: [] for e in ENGINES}

    def op(self, eng, fn, waits=(), tick=False, fence=False):
        kind = "tick" if tick else ("fence" if fence else None)
        w = self.pending[eng] + list(waits)
        self.pending[eng] = []
        self.ops[eng].append([fn, w, kind])
        if tick:
            self.ticks[eng] += 1
            return (eng, self.ticks[eng])
        return None

    def dma(self, q, out, in_, waits=(), lane=None, slow=False):
        if lane is None:
            lane = self.rr[q]
            self.rr[q] = (lane + 1) % NLANES
        def fn(h):
            return h.dma_start(out=out, in_=in_,
                               allow_slow_non_contiguous=slow)
        w = self.pending[q] + list(waits)
        self.pending[q] = []
        prev = self.dmas[q][lane]
        if prev:
            w.append((f"dma_{q}{lane}", 16 * prev))
        self.ops[q].append([fn, w, ("dma", lane)])
        self.dmas[q][lane] = prev + 1
        return (f"dma_{q}{lane}", 16 * (prev + 1))

    def stage_barrier(self):
        toks = [(e, self.ticks[e]) for e in ENGINES if self.ticks[e] > 0]
        toks += [(f"dma_{q}{l}", 16 * self.dmas[q][l])
                 for q in DMAQ for l in range(NLANES) if self.dmas[q][l]]
        for e in ENGINES:
            self.pending[e] = [t for t in toks
                               if not (t[0] == e and self.ticks[e] == t[1])]

    def barrier_token(self, eng):
        if self.ticks[eng] == 0:
            return None
        return (eng, self.ticks[eng])

    def check_deadlock(self):
        idx = {e: 0 for e in ENGINES}
        done_ticks = {e: 0 for e in ENGINES}
        done_dmas = {f"{q}{l}": 0 for q in DMAQ for l in range(NLANES)}

        def ok(w):
            if w is None:
                return True
            name, val = w
            if name.startswith("dma_"):
                return 16 * done_dmas[name[4:]] >= val
            return done_ticks[name] >= val

        progress = True
        while progress:
            progress = False
            for e in ENGINES:
                while idx[e] < len(self.ops[e]):
                    fn, waits, kind = self.ops[e][idx[e]]
                    if not all(ok(w) for w in waits):
                        break
                    if kind == "tick":
                        done_ticks[e] += 1
                    elif isinstance(kind, tuple) and kind[0] == "dma":
                        done_dmas[f"{e}{kind[1]}"] += 1
                    idx[e] += 1
                    progress = True
        stuck = {e: (idx[e], len(self.ops[e])) for e in ENGINES
                 if idx[e] < len(self.ops[e])}
        if stuck:
            detail = {}
            for e in stuck:
                fn, waits, kind = self.ops[e][idx[e]]
                detail[e] = (idx[e], waits)
            raise RuntimeError(f"deadlock: stuck at {detail}")

    def emit(self, nc, handles, sems, dma_sems):
        for e in ENGINES:
            h = handles[e]
            last_wait = {}
            for fn, waits, kind in self.ops[e]:
                for w in waits:
                    if w is None:
                        continue
                    name, val = w
                    if last_wait.get(name, -1) >= val:
                        continue
                    last_wait[name] = val
                    sem = dma_sems[name[4:]] if name.startswith("dma_") else sems[name]
                    h.wait_ge(sem, val)
                ins = fn(h)
                if kind == "tick":
                    ins.then_inc(sems[e], 1)
                elif kind == "dma":
                    ins.then_inc(dma_sems[e], 16)


def ceil8(n):
    return (n + 7) & ~7


class Arena:
    def __init__(self, handle):
        self.h = handle
        self.off = 0
        self.marks = []

    def alloc(self, n_f32):
        n_f32 = ceil8(n_f32)
        assert self.off + n_f32 <= ARENA_F32, f"arena overflow {self.off}+{n_f32}"
        off = self.off
        self.off += n_f32
        return off

    def f32(self, n, shape=None):
        off = self.alloc(n)
        ap = self.h.ap()[:, off:off + n]
        if shape is not None:
            ap = ap.rearrange("p (" + " ".join(f"d{i}" for i in range(len(shape))) + ") -> p " +
                              " ".join(f"d{i}" for i in range(len(shape))),
                              **{f"d{i}": s for i, s in enumerate(shape)})
        return ap

    def bf16(self, n, shape=None):
        off = self.alloc((n + 1) // 2)
        ap = self.h.ap().bitcast(BF16)[:, 2 * off:2 * off + n]
        if shape is not None:
            ap = ap.rearrange("p (" + " ".join(f"d{i}" for i in range(len(shape))) + ") -> p " +
                              " ".join(f"d{i}" for i in range(len(shape))),
                              **{f"d{i}": s for i, s in enumerate(shape)})
        return ap

    def mark(self):
        self.marks.append(self.off)

    def reset(self):
        self.off = self.marks.pop()


VEC8 = ["td", "tf", "tmk", "tmv", "tmr", "g1", "b1", "g2", "b2", "ftmk", "ftmr"]


def host_prep(inputs):
    F = np.float32
    g = {k: np.ascontiguousarray(np.asarray(v)) for k, v in inputs.items()}
    C_ = C

    def t2(a):
        return np.ascontiguousarray(np.asarray(a).T)

    def v8(a):
        return np.ascontiguousarray(np.asarray(a).reshape(NCC, P).T)

    shared = {
        "WkT": t2(g["L0_attn_Wk"]), "WvT": t2(g["L0_attn_Wv"]),
        "WrT": t2(g["L0_attn_Wr"]), "WoT": t2(g["L0_attn_Wo"]),
        "fWkT": t2(g["L0_attn_fWk"]), "fWrT": t2(g["L0_attn_fWr"]),
        "fWvT": t2(g["L0_attn_fWv"]),
        "w1T": t2(g["L0_ffn_w1"]), "w2T": t2(g["L0_ffn_w2"]),
        "vecs": np.ascontiguousarray(np.stack([
            v8(g["L0_attn_time_decay"]), v8(g["L0_attn_time_first"]),
            v8(g["L0_attn_tmk"].reshape(C_)), v8(g["L0_attn_tmv"].reshape(C_)),
            v8(g["L0_attn_tmr"].reshape(C_)),
            v8(g["L0_attn_ln1_g"]), v8(g["L0_attn_ln1_b"]),
            v8(g["L0_attn_ln2_g"]), v8(g["L0_attn_ln2_b"]),
            v8(g["L0_attn_f_tmk"].reshape(C_)), v8(g["L0_attn_f_tmr"].reshape(C_)),
        ], axis=1).reshape(P, len(VEC8) * NCC)),
        "ident": np.eye(P, dtype=F),
        "ident16": np.eye(P, dtype=F).astype(ml_dtypes.bfloat16),
        "ones1": np.ones((1, P), dtype=F),
        "convw": np.ascontiguousarray(
            g["L0_ffn_conv_w"].reshape(NHC, P, 3).transpose(1, 0, 2).reshape(P, NHC * 3)),
        "convb": np.ascontiguousarray(
            g["L0_ffn_conv_b"].reshape(NHC, P).T),
    }
    xs = [np.ascontiguousarray(g["L0_attn_x"][b]) for b in range(8)]
    return shared, xs


def build(stages=("s1", "s2a", "s2b", "s3a", "s3b"), debug=()):
    nc = bass.Bass("TRN2", target_bir_lowering=False, debug=False,
                   detect_race_conditions=False)
    em = Em()
    d = {}

    def dram(name, shape, dt=F32, kind="ExternalInput"):
        t = nc.dram_tensor(name, shape, dt, kind=kind).ap()
        d[name] = t
        return t

    def scratch(name, shape, dt=F32):
        kind = "ExternalOutput" if name in debug else "Internal"
        return dram(name, shape, dt, kind=kind)

    x_h = dram("x", [T, C])
    for n, shape in [("WkT", [C, C]), ("WvT", [C, C]), ("WrT", [C, C]),
                     ("WoT", [C, C]), ("fWkT", [C, HID]), ("fWrT", [C, C]),
                     ("fWvT", [HID, C]), ("w1T", [C, HID]), ("w2T", [HID, C])]:
        dram(n, shape)
    dram("vecs", [P, len(VEC8) * NCC])
    dram("ident", [P, P])
    dram("ident16", [P, P], BF16)
    dram("ones1", [1, P])
    dram("convw", [P, NHC * 3])
    dram("convb", [P, NHC])
    x1_h = scratch("x1s", [T, C])
    kk_h = scratch("kks", [NCH, P, NHC * TC], BF16)
    xr2_h = scratch("xr2s", [NCH, P, NCC * TC], BF16)
    sr2_h = scratch("sr2s", [T, C])
    xA_h = scratch("xAs", [T, C])
    gq_h = scratch("gqs", [NCH, P, NHC * TC], BF16)
    inv2_h = scratch("inv2s", [T // P, P])
    w2q_h = scratch("w2qs", [P, NHC * 1024], BF16)
    out_h = dram("out", [T, C], kind="ExternalOutput")

    sb_ctx = nc.sbuf_tensor("arena", [P, ARENA_F32], F32)
    arena_h = sb_ctx.__enter__()
    ar = Arena(arena_h)

    ps_ctx = []
    pbank = []
    for i in range(8):
        c = nc.psum_tensor(f"pbank{i}", [P, 512], F32)
        ps_ctx.append(c)
        pbank.append(c.__enter__())
    pb = pbank[:4]

    identb = ar.f32(P)
    identb16 = ar.bf16(P)

    onesb = ar.f32(P)
    vecsb = ar.f32(len(VEC8) * NCC)
    derv = ar.f32(80)
    state = ar.f32(80)
    statb = ar.f32(32)
    convwb = ar.f32(NHC * 3)
    convbb = ar.f32(NHC)
    scl = ar.f32(32)

    def vcol(name, i=0):
        j = VEC8.index(name) * NCC
        return vecsb[:, j + i:j + i + 1]

    def vgrp(name):
        j = VEC8.index(name) * NCC
        return vecsb[:, j:j + NCC]

    ident_ap = identb

    tok_ident = em.dma("sync", identb, d["ident"][:, :])
    tok_ident16 = em.dma("sync", identb16, d["ident16"][:, :])
    tok_ones = em.dma("sync", onesb[0:1, :], d["ones1"][:, :])
    tok_vecs = em.dma("sync", vecsb, d["vecs"][:, :])
    tok_convw = em.dma("sync", convwb, d["convw"][:, :])
    tok_convb = em.dma("sync", convbb, d["convb"][:, :])

    ew = derv[:, 0:8]; wneg = derv[:, 8:16]; lam = derv[:, 16:24]
    eu = derv[:, 24:32]; allow = derv[:, 32:40]
    SA = state[:, 0:8]; SB = state[:, 8:16]; mrun = state[:, 16:24]
    tmpa = state[:, 24:32]; mnew = state[:, 32:40]; mdelta = state[:, 40:48]
    fac = state[:, 48:56]; negm = state[:, 56:64]; ck = state[:, 64:72]

    t_ew = em.op("scalar", lambda h: h.activation(out=ew, in_=vgrp("td"), func=AF.Exp),
                 waits=[tok_vecs], tick=True)
    t_wneg = em.op("vector", lambda h: h.tensor_scalar_mul(wneg, ew, -1.0),
                   waits=[t_ew], tick=True)
    t_lam = em.op("scalar", lambda h: h.activation(out=lam, in_=wneg, func=AF.Exp),
                  waits=[t_wneg], tick=True)
    t_eu = em.op("scalar", lambda h: h.activation(out=eu, in_=vgrp("tf"), func=AF.Exp),
                 waits=[tok_vecs], tick=True)
    t_allow = em.op("vector", lambda h: h.tensor_scalar(allow, ew, float(TC), 60.0,
                                                        ALU.mult, ALU.min),
                    waits=[t_ew], tick=True)
    em.op("vector", lambda h: h.memset(SA, 0.0))
    em.op("vector", lambda h: h.memset(SB, 0.0))
    t_state0 = em.op("vector", lambda h: h.memset(mrun, -1e30), tick=True)

    ar.mark()
    wbuf = {}
    tok_w = {}
    for n in ["WkT", "WvT", "WrT", "WoT"]:
        wbuf[n] = ar.bf16(NCC * C, shape=[NCC, C])
        srcw = d[n].rearrange("(cc p) co -> p cc co", p=P)
        tok_w[n] = em.dma("gpsimd", wbuf[n], srcw)

    xin = ar.f32(2 * 2048)
    z1tm = ar.f32(2048)
    z1fm = ar.f32(NCC * 257 + 7)
    dbuf = ar.f32(2048)
    xkb = ar.bf16(NCC * TC, shape=[NCC, TC])
    xvb = ar.bf16(NCC * TC, shape=[NCC, TC])
    xrb = ar.bf16(NCC * TC, shape=[NCC, TC])
    ekbuf = ar.f32(2048)
    ekvbuf = ar.f32(2048)
    AAbuf = ar.f32(NCC * 264)
    BBbuf = ar.f32(NCC * 264)
    numbuf = ar.f32(2048)
    denbuf = ar.f32(2048)
    srbuf = ar.f32(2048)
    sryb_f = ar.bf16(NCC * TC)
    x1buf = ar.f32(2 * 2048)

    def view3(ap, a, b):
        return ap.rearrange("p (a b) -> p a b", a=a, b=b)

    xin3 = view3(xin, 4, 1024)
    z1tm3 = view3(z1tm, 2, 1024)
    z1fm3 = view3(z1fm[:, :NCC * 257], NCC, 257)
    dbuf3 = view3(dbuf, NCC, TC)
    ek3 = view3(ekbuf, NCC, TC)
    ekv3 = view3(ekvbuf, NCC, TC)
    AA3 = view3(AAbuf, NCC, 264)
    BB3 = view3(BBbuf, NCC, 264)
    num3 = view3(numbuf, NCC, TC)
    den3 = view3(denbuf, NCC, TC)
    sr3 = view3(srbuf, NCC, TC)
    x13 = view3(x1buf, 4, 1024)
    sryb = view3(sryb_f, NCC, TC)

    t_z1fm0 = em.op("vector", lambda h: h.memset(z1fm3[:, :, 0], 0.0), tick=True)

    if "s1" in stages:
        PH = 99
        SKIP = set()
        toks = {"sr": None, "x1evac": [None, None], "evacz1": None, "ek": None,
                "ekv": None, "x1dma": [None, None], "y": None, "sry": None,
                "mixr": None, "var": None, "den": None, "vd": None}
        for j in range(NCH):
            pj = j % 2
            xa = [xin3[:, pj * 2 + tt, :] for tt in range(2)]
            x1a = [x13[:, pj * 2 + tt, :] for tt in range(2)]
            xin_t = lambda tt, xa=xa: xa[tt]
            x1_t = lambda tt, x1a=x1a: x1a[tt]
            src = x_h[j * TC:(j + 1) * TC, :].rearrange("(tt p) c -> p tt c", p=P)
            wl = [toks["x1evac"][pj]] if toks["x1evac"][pj] else []
            tok_x = em.dma("sync", view3(xin, 4, 1024)[:, pj * 2:pj * 2 + 2, :], src,
                           waits=wl)
            for tt in (range(2) if "1" not in SKIP else []):
                em.op("vector", lambda h, tt=tt, xin_t=xin_t: h.reduce_sum(
                    statb[:, 0 + tt:1 + tt], xin_t(tt), axis=AX.X), waits=[tok_x])
            t_sum = em.op("vector", lambda h: h.nop(), tick=True)
            sq_wl = [tok_x] + ([toks["var"]] if toks["var"] else [])
            for tt in (range(2) if "1" not in SKIP else []):
                em.op("scalar", lambda h, tt=tt, xin_t=xin_t: h.activation(
                    out=z1tm3[:, tt, :], in_=xin_t(tt), func=AF.Square,
                    accum_out=statb[:, 2 + tt:3 + tt]), waits=sq_wl if tt == 0 else [])
            t_sq = em.op("scalar", lambda h: h.nop(), tick=True)
            if PH < 2:
                continue
            if "2" in SKIP:
                t_var = t_rstd = None
            if "2" not in SKIP:
              em.op("vector", lambda h: h.tensor_scalar(
                statb[:, 4:6], statb[:, 2:4], 1.0 / C, 1e-5, ALU.mult, ALU.add),
                waits=[t_sq])
            if "2" not in SKIP:
              em.op("vector", lambda h: h.tensor_scalar_mul(
                statb[:, 6:8], statb[:, 0:2], 1.0 / C), waits=[t_sum])
            if "2" not in SKIP:
              t_m = em.op("vector", lambda h: h.nop(), tick=True)
              em.op("scalar", lambda h: h.activation(
                out=statb[:, 8:10], in_=statb[:, 6:8], func=AF.Square),
                waits=[t_m])
            if "2" not in SKIP:
              t_m2s = em.op("scalar", lambda h: h.nop(), tick=True)
              toks["var"] = t_var = em.op("vector", lambda h: h.tensor_tensor(
                statb[:, 10:12], statb[:, 4:6], statb[:, 8:10], ALU.subtract),
                waits=[t_m2s], tick=True)
            if "2" not in SKIP:
              em.op("scalar", lambda h: h.activation(
                out=statb[:, 14:16], in_=statb[:, 10:12], func=AF.Ln), waits=[t_var],
                fence=True)
            if "2" not in SKIP:
              t_rstd = em.op("scalar", lambda h: h.activation(
                out=statb[:, 12:14], in_=statb[:, 14:16], func=AF.Exp, scale=-0.5),
                tick=True)
            if PH < 3:
                continue
            for tt in (range(2) if "3" not in SKIP else []):
                em.op("vector", lambda h, tt=tt, xin_t=xin_t: h.tensor_scalar(
                    z1tm3[:, tt, :], xin_t(tt), statb[:, 6 + tt:7 + tt],
                    statb[:, 12 + tt:13 + tt], ALU.subtract, ALU.mult),
                    waits=[t_rstd] if t_rstd else [])
            t_apply = em.op("vector", lambda h: h.nop(), tick=True)
            if PH < 4:
                continue
            if j > 0 and "4" not in SKIP:
                em.op("vector", lambda h: h.tensor_copy(
                    z1fm3[:, :, 0], z1fm3[:, :, 256]), waits=[toks["evacz1"]])
            t_bnd = em.op("vector", lambda h: h.nop(), tick=True)
            if PH < 5:
                continue
            twl = [t_apply, t_bnd]
            if toks["sr"]:
                twl.append(toks["sr"])
            if j == 0:
                twl += [tok_ident, t_z1fm0]
            _first = True
            for bank in range(4):
                for cc in (bank, bank + 4):
                    for tt in range(2):
                        em.op("tensor", lambda h, cc=cc, tt=tt: h.transpose(
                            pb[cc % 4][:, (cc // 4) * 256 + tt * 128:
                                       (cc // 4) * 256 + tt * 128 + 128],
                            z1tm3[:, tt, cc * 128:(cc + 1) * 128], ident_ap),
                            waits=twl if _first else [])
                        _first = False
            t_tr = em.op("tensor", lambda h: h.nop(), tick=True)
            if PH < 6:
                continue
            evz_wl = [t_tr] + ([tok_vecs] if j == 0 else [])
            _plain = False
            for ci_, cc in enumerate(range(NCC)):
                if True:
                    em.op("scalar", lambda h, cc=cc: h.activation(
                        out=z1fm3[:, cc, 1:257],
                        in_=pb[cc % 4][:, (cc // 4) * 256:(cc // 4) * 256 + 256],
                        func=AF.Identity, scale=vcol("g1", cc), bias=vcol("b1", cc)),
                        waits=evz_wl if ci_ == 0 else [])
            toks["evacz1"] = t_evz = em.op("scalar", lambda h: h.nop(), tick=True)
            if PH < 7:
                continue
            d_wl = [t_evz] + ([toks["den"]] if toks["den"] else [])
            if j == 0:
                d_wl.append(tok_vecs)
            toks["vd"] = t_d = em.op("vector", lambda h: h.tensor_tensor(
                dbuf3, z1fm3[:, :, 1:257], z1fm3[:, :, 0:256], ALU.subtract),
                waits=d_wl, tick=True)
            if PH < 8:
                continue
            for cc in range(NCC):
                em.op("vector", lambda h, cc=cc: h.scalar_tensor_tensor(
                    xkb[:, cc, :], dbuf3[:, cc, :], vcol("tmk", cc),
                    z1fm3[:, cc, 0:256], ALU.mult, ALU.add),
                    waits=[t_d] if cc == 0 else [])
                em.op("vector", lambda h, cc=cc: h.scalar_tensor_tensor(
                    xvb[:, cc, :], dbuf3[:, cc, :], vcol("tmv", cc),
                    z1fm3[:, cc, 0:256], ALU.mult, ALU.add))
                em.op("vector", lambda h, cc=cc: h.scalar_tensor_tensor(
                    xrb[:, cc, :], dbuf3[:, cc, :], vcol("tmr", cc),
                    z1fm3[:, cc, 0:256], ALU.mult, ALU.add))
            t_mixkv = t_mixr = em.op("vector", lambda h: h.nop(), tick=True)
            toks["mixr"] = t_mixr
            if PH < 9:
                continue
            kwl = [t_mixkv, tok_w["WkT"], t_evz]
            kwl = [w for w in kwl if w]
            if toks["ek"]:
                kwl.append(toks["ek"])
            for co in range(NCC):
                for ci in range(NCC):
                    em.op("tensor", lambda h, co=co, ci=ci: h.matmul(
                        pb[co // 2][:, (co % 2) * 256:(co % 2) * 256 + 256],
                        wbuf["WkT"][:, ci, co * 128:(co + 1) * 128],
                        xkb[:, ci, :], start=(ci == 0), stop=(ci == NCC - 1)),
                        waits=kwl if (co == 0 and ci == 0) else [])
            t_k = em.op("tensor", lambda h: h.nop(), tick=True)
            if PH < 10:
                continue
            for co in range(NCC):
                em.op("vector", lambda h, co=co: h.reduce_max(
                    ck[:, co:co + 1],
                    pb[co // 2][:, (co % 2) * 256:(co % 2) * 256 + 256], axis=AX.X),
                    waits=[t_k, t_allow, t_state0] if co == 0 else [],
                    fence=(co == NCC - 1))
            em.op("vector", lambda h: h.tensor_tensor(tmpa, mrun, allow, ALU.subtract), fence=True)
            em.op("vector", lambda h: h.tensor_tensor(mnew, tmpa, ck, ALU.max), fence=True)
            em.op("vector", lambda h: h.tensor_tensor(mdelta, mrun, mnew, ALU.subtract), fence=True)
            em.op("vector", lambda h: h.tensor_scalar_mul(negm, mnew, -1.0), fence=True)
            t_st1 = em.op("vector", lambda h: h.tensor_copy(mrun, mnew), tick=True)
            t_fac = em.op("scalar", lambda h: h.activation(
                out=fac, in_=mdelta, func=AF.Exp), waits=[t_st1], tick=True)
            em.op("vector", lambda h: h.tensor_tensor(SA, SA, fac, ALU.mult),
                  waits=[t_fac], fence=True)
            t_st2 = em.op("vector", lambda h: h.tensor_tensor(SB, SB, fac, ALU.mult),
                          tick=True)
            if PH < 11:
                continue
            vwl = [t_mixkv, tok_w["WvT"]]
            vwl = [w for w in vwl if w]
            if toks["x1evac"][1 - pj]:
                vwl.append(toks["x1evac"][1 - pj])
            for co in range(NCC):
                for ci in range(NCC):
                    em.op("tensor", lambda h, co=co, ci=ci: h.matmul(
                        pbank[4 + co // 2][:, (co % 2) * 256:(co % 2) * 256 + 256],
                        wbuf["WvT"][:, ci, co * 128:(co + 1) * 128],
                        xvb[:, ci, :], start=(ci == 0), stop=(ci == NCC - 1)),
                        waits=vwl if (co == 0 and ci == 0) else [])
            t_v = em.op("tensor", lambda h: h.nop(), tick=True)
            if PH < 12:
                continue
            ek_wl = [t_st1, t_k] + ([toks["y"]] if toks["y"] else [])
            for co in range(NCC):
                em.op("scalar", lambda h, co=co: h.activation(
                    out=ek3[:, co, :],
                    in_=pb[co // 2][:, (co % 2) * 256:(co % 2) * 256 + 256],
                    func=AF.Exp, bias=negm[:, co:co + 1]),
                    waits=ek_wl if co == 0 else [])
            toks["ek"] = t_ek = em.op("scalar", lambda h: h.nop(), tick=True)
            if PH < 13:
                continue
            for co in range(NCC):
                em.op("vector", lambda h, co=co: h.tensor_tensor(
                    ekv3[:, co, :], ek3[:, co, :],
                    pbank[4 + co // 2][:, (co % 2) * 256:(co % 2) * 256 + 256], ALU.mult),
                    waits=[t_ek, t_v] if co == 0 else [])
            toks["ekv"] = t_ekv = em.op("vector", lambda h: h.nop(), tick=True)
            if PH < 14:
                continue
            em.op("vector", lambda h: h.tensor_copy(AA3[:, :, 0], SA),
                  waits=[t_st2, t_ekv])
            for co in range(NCC):
                em.op("vector", lambda h, co=co: h.tensor_tensor_scan(
                    AA3[:, co, 1:257], lam[:, co:co + 1].broadcast_to([P, TC]),
                    ekv3[:, co, :], SA[:, co:co + 1], ALU.mult, ALU.add),
                    waits=[t_lam] if co == 0 else [])
            t_scanA = em.op("vector", lambda h: h.nop(), tick=True)
            scb_wl = [t_st2, t_ek] + ([toks["den"]] if toks["den"] else [])
            em.op("vector", lambda h: h.tensor_copy(BB3[:, :, 0], SB),
                  waits=scb_wl)
            for co in range(NCC):
                em.op("vector", lambda h, co=co: h.tensor_tensor_scan(
                    BB3[:, co, 1:257], lam[:, co:co + 1].broadcast_to([P, TC]),
                    ek3[:, co, :], SB[:, co:co + 1], ALU.mult, ALU.add),
                    waits=[t_lam] if co == 0 else [])
            t_scanB = em.op("vector", lambda h: h.nop(), tick=True)
            if PH < 15:
                continue
            em.op("vector", lambda h: h.tensor_copy(SA, AA3[:, :, 256]),
                  waits=[t_scanA])
            t_carA = em.op("vector", lambda h: h.nop(), tick=True)
            em.op("vector", lambda h: h.tensor_copy(SB, BB3[:, :, 256]),
                  waits=[t_scanB])
            t_carB = em.op("vector", lambda h: h.nop(), tick=True)
            if PH < 16:
                continue
            num_wl = [t_eu, t_scanA] + ([toks["y"]] if toks["y"] else [])
            for co in range(NCC):
                em.op("vector", lambda h, co=co: h.scalar_tensor_tensor(
                    num3[:, co, :], ekv3[:, co, :], eu[:, co:co + 1],
                    AA3[:, co, 0:256], ALU.mult, ALU.add),
                    waits=num_wl if co == 0 else [])
            t_num = em.op("vector", lambda h: h.nop(), tick=True)
            et_wl = [t_eu, t_ek, t_mixkv] + ([toks["vd"]] if toks["vd"] else [])
            for co in range(NCC):
                em.op("gpsimd", lambda h, co=co: h.tensor_tensor(
                    dbuf3[:, co, :], ek3[:, co, :],
                    eu[:, co:co + 1].broadcast_to([P, TC]), ALU.mult),
                    waits=et_wl if co == 0 else [])
            for co in range(NCC):
                em.op("gpsimd", lambda h, co=co: h.tensor_tensor(
                    den3[:, co, :], dbuf3[:, co, :], BB3[:, co, 0:256], ALU.add),
                    waits=[t_scanB] if co == 0 else [])
            toks["den"] = t_den = em.op("gpsimd", lambda h: h.nop(), tick=True)
            if PH < 17:
                continue
            em.op("scalar", lambda h: h.activation(
                out=ekbuf, in_=denbuf, func=AF.Ln), waits=[t_den], fence=True)
            t_rden = em.op("scalar", lambda h: h.activation(
                out=denbuf, in_=ekbuf, func=AF.Exp, scale=-1.0), tick=True)
            if PH < 18:
                continue
            rwl = [t_mixr, tok_w["WrT"], t_ek]
            rwl = [w for w in rwl if w]
            for co in range(NCC):
                for ci in range(NCC):
                    em.op("tensor", lambda h, co=co, ci=ci: h.matmul(
                        pb[co // 2][:, (co % 2) * 256:(co % 2) * 256 + 256],
                        wbuf["WrT"][:, ci, co * 128:(co + 1) * 128],
                        xrb[:, ci, :], start=(ci == 0), stop=(ci == NCC - 1)),
                        waits=rwl if (co == 0 and ci == 0) else [])
            t_r = em.op("tensor", lambda h: h.nop(), tick=True)
            sr_wl = [t_r] + ([toks["sry"]] if toks["sry"] else [])
            for co in range(NCC):
                em.op("scalar", lambda h, co=co: h.activation(
                    out=sr3[:, co, :],
                    in_=pb[co // 2][:, (co % 2) * 256:(co % 2) * 256 + 256],
                    func=AF.Sigmoid), waits=sr_wl if co == 0 else [])
            toks["sr"] = t_sr = em.op("scalar", lambda h: h.nop(), tick=True)
            if PH < 19:
                continue
            toks["y"] = t_y = em.op("gpsimd", lambda h: h.tensor_tensor(
                ekbuf, numbuf, denbuf, ALU.mult), waits=[t_num, t_rden], tick=True)
            toks["sry"] = t_sry = em.op("gpsimd", lambda h: h.tensor_tensor(
                sryb_f, ekbuf, srbuf, ALU.mult),
                waits=[t_sr], tick=True)
            if PH < 20:
                continue
            wwl = [t_sry, tok_w["WoT"], t_ekv]
            wwl = [w for w in wwl if w]
            first = True
            for tt in range(2):
                for nh in range(2):
                    for ci in range(NCC):
                        em.op("tensor", lambda h, tt=tt, nh=nh, ci=ci: h.matmul(
                            pbank[4 + 2 * tt + nh][:, :],
                            sryb[:, ci, tt * 128:tt * 128 + 128],
                            wbuf["WoT"][:, ci, nh * 512:nh * 512 + 512],
                            start=(ci == 0), stop=(ci == NCC - 1)),
                            waits=wwl if first else [])
                        first = False
            t_wo = em.op("tensor", lambda h: h.nop(), tick=True)
            if PH < 21:
                continue
            x1_wl = [t_wo] + ([toks["x1dma"][pj]] if toks["x1dma"][pj] else [])
            for tt in range(2):
                for nh in range(2):
                    em.op("vector", lambda h, tt=tt, nh=nh, x1_t=x1_t, xin_t=xin_t:
                          h.tensor_tensor(
                              x1_t(tt)[:, nh * 512:nh * 512 + 512],
                              xin_t(tt)[:, nh * 512:nh * 512 + 512],
                              pbank[4 + 2 * tt + nh][:, :], ALU.add),
                          waits=x1_wl if (tt == 0 and nh == 0) else [])
            toks["x1evac"][pj] = t_x1 = em.op(
                "vector", lambda h: h.nop(), tick=True)
            dst = x1_h[j * TC:(j + 1) * TC, :].rearrange("(tt p) c -> p tt c", p=P)
            toks["x1dma"][pj] = em.dma(
                "sync", dst, x13[:, pj * 2:pj * 2 + 2, :], waits=[t_x1])


    if "s2a" in stages:
        em.stage_barrier()
        ar.reset(); ar.mark()
        fwk = ar.bf16(NCC * HID, shape=[NCC, HID])
        tok_fwk = em.dma("gpsimd", fwk, d["fWkT"].rearrange("(cc p) ho -> p cc ho", p=P))
        x1in = ar.f32(2 * 2048)
        x1in3 = view3(x1in, 4, 1024)
        z2tm = ar.f32(2048); z2tm3 = view3(z2tm, 2, 1024)
        z2fm = ar.f32(NCC * 257 + 7); z2fm3 = view3(z2fm[:, :NCC * 257], NCC, 257)
        dbuf2 = ar.f32(2048); dbuf23 = view3(dbuf2, NCC, TC)
        xk2b = ar.bf16(NCC * TC, shape=[NCC, TC])
        xr2b_f = ar.bf16(NCC * TC)
        xr2b = view3(xr2b_f, NCC, TC)
        kkb_f = ar.bf16(NHC * TC)
        kkb = view3(kkb_f, NHC, TC)
        rbuf = ar.f32(8 * TC)
        rbuf3 = view3(rbuf, 8, TC)

        t_z2fm0 = em.op("vector", lambda h: h.memset(z2fm3[:, :, 0], 0.0), tick=True)
        t2 = {"evac": None, "mix": None, "kk": [None] * 4, "x1dma": [None, None],
              "var": None, "kkdma": None, "xrdma": None}
        for j in range(NCH):
            pj = j % 2
            x1a = [x1in3[:, pj * 2 + tt, :] for tt in range(2)]
            srcx1 = x1_h[j * TC:(j + 1) * TC, :].rearrange("(tt p) c -> p tt c", p=P)
            wl = [t2["mix"]] if j >= 2 and t2["mix"] else []
            tok_x1 = em.dma("sync", x1in3[:, pj * 2:pj * 2 + 2, :], srcx1, waits=wl)
            for tt in range(2):
                em.op("vector", lambda h, tt=tt, x1a=x1a: h.reduce_sum(
                    statb[:, 0 + tt:1 + tt], x1a[tt], axis=AX.X),
                    waits=[tok_x1] if tt == 0 else [], tick=(tt == 1))
            t_sum = ("vector", em.ticks["vector"])
            sq_wl = [tok_x1] + ([t2["var"]] if t2["var"] else [])
            for tt in range(2):
                em.op("scalar", lambda h, tt=tt, x1a=x1a: h.activation(
                    out=z2tm3[:, tt, :], in_=x1a[tt], func=AF.Square,
                    accum_out=statb[:, 2 + tt:3 + tt]),
                    waits=sq_wl if tt == 0 else [], tick=(tt == 1))
            t_sq = ("scalar", em.ticks["scalar"])
            em.op("vector", lambda h: h.tensor_scalar(
                statb[:, 4:6], statb[:, 2:4], 1.0 / C, 1e-5, ALU.mult, ALU.add),
                waits=[t_sq], fence=True)
            t_m = em.op("vector", lambda h: h.tensor_scalar_mul(
                statb[:, 6:8], statb[:, 0:2], 1.0 / C), waits=[t_sum],
                fence=True, tick=True)
            em.op("scalar", lambda h: h.activation(
                out=statb[:, 8:10], in_=statb[:, 6:8], func=AF.Square),
                waits=[t_m], fence=True, tick=True)
            t_m2s = ("scalar", em.ticks["scalar"])
            t2["var"] = t_var = em.op("vector", lambda h: h.tensor_tensor(
                statb[:, 10:12], statb[:, 4:6], statb[:, 8:10], ALU.subtract),
                waits=[t_m2s], tick=True)
            em.op("scalar", lambda h: h.activation(
                out=statb[:, 14:16], in_=statb[:, 10:12], func=AF.Ln),
                waits=[t_var], fence=True)
            t_rstd = em.op("scalar", lambda h: h.activation(
                out=statb[:, 12:14], in_=statb[:, 14:16], func=AF.Exp, scale=-0.5),
                tick=True)
            for tt in range(2):
                em.op("vector", lambda h, tt=tt, x1a=x1a: h.tensor_scalar(
                    z2tm3[:, tt, :], x1a[tt], statb[:, 6 + tt:7 + tt],
                    statb[:, 12 + tt:13 + tt], ALU.subtract, ALU.mult),
                    waits=[t_rstd], tick=(tt == 1))
            t_apply = ("vector", em.ticks["vector"])
            if j > 0:
                em.op("vector", lambda h: h.tensor_copy(
                    z2fm3[:, :, 0], z2fm3[:, :, 256]), waits=[t2["evac"]],
                    fence=True, tick=True)
            else:
                em.op("vector", lambda h: h.nop(), waits=[t_z2fm0], tick=True)
            t_bnd = ("vector", em.ticks["vector"])
            twl = [t_apply, t_bnd]
            if t2["kk"][3]:
                twl.append(t2["kk"][3])
            if j == 0:
                twl.append(tok_ident)
            first = True
            for bank in range(4):
                for cc in (bank, bank + 4):
                    for tt in range(2):
                        em.op("tensor", lambda h, cc=cc, tt=tt: h.transpose(
                            pb[cc % 4][:, (cc // 4) * 256 + tt * 128:
                                       (cc // 4) * 256 + tt * 128 + 128],
                            z2tm3[:, tt, cc * 128:(cc + 1) * 128], ident_ap),
                            waits=twl if first else [],
                            tick=(bank == 3 and cc == 7 and tt == 1))
                        first = False
            t_tr = ("tensor", em.ticks["tensor"])
            evz_wl = [t_tr] + ([tok_vecs] if j == 0 else [])
            for cc in range(NCC):
                em.op("scalar", lambda h, cc=cc: h.activation(
                    out=z2fm3[:, cc, 1:257],
                    in_=pb[cc % 4][:, (cc // 4) * 256:(cc // 4) * 256 + 256],
                    func=AF.Identity, scale=vcol("g2", cc), bias=vcol("b2", cc)),
                    waits=evz_wl if cc == 0 else [], tick=(cc == NCC - 1))
            t2["evac"] = t_evz = ("scalar", em.ticks["scalar"])
            dwl = [t_evz] + ([t2["xrdma"]] if t2["xrdma"] else [])
            em.op("vector", lambda h: h.tensor_tensor(
                dbuf23, z2fm3[:, :, 1:257], z2fm3[:, :, 0:256], ALU.subtract),
                waits=dwl, tick=True)
            t_d = ("vector", em.ticks["vector"])
            for cc in range(NCC):
                em.op("vector", lambda h, cc=cc: h.scalar_tensor_tensor(
                    xk2b[:, cc, :], dbuf23[:, cc, :], vcol("ftmk", cc),
                    z2fm3[:, cc, 0:256], ALU.mult, ALU.add),
                    waits=[t_d] if cc == 0 else [])
                em.op("vector", lambda h, cc=cc: h.scalar_tensor_tensor(
                    xr2b[:, cc, :], dbuf23[:, cc, :], vcol("ftmr", cc),
                    z2fm3[:, cc, 0:256], ALU.mult, ALU.add),
                    tick=(cc == NCC - 1))
            t2["mix"] = t_mix = ("vector", em.ticks["vector"])
            wl = [t_mix] + ([t2["xrdma"]] if t2["xrdma"] else [])
            t2["xrdma"] = em.dma("scalar", xr2_h[j, :, :], xr2b_f, waits=[t_mix])
            for hg in range(4):
                first = True
                for hs in range(8):
                    hc = hg * 8 + hs
                    mwl = [t_mix, tok_fwk]
                    if t2["kk"][(hg - 1) % 4]:
                        mwl.append(t2["kk"][(hg - 1) % 4])
                    if hg == 0:
                        mwl.append(t_evz)
                    for cc in range(NCC):
                        em.op("tensor", lambda h, hc=hc, hs=hs, cc=cc: h.matmul(
                            pb[hs // 2][:, (hs % 2) * 256:(hs % 2) * 256 + 256],
                            fwk[:, cc, hc * 128:(hc + 1) * 128],
                            xk2b[:, cc, :], start=(cc == 0), stop=(cc == NCC - 1)),
                            waits=mwl if first else [],
                            tick=(hs == 7 and cc == NCC - 1))
                        first = False
                t_fk = ("tensor", em.ticks["tensor"])
                for hs in range(8):
                    em.op("scalar", lambda h, hs=hs: h.activation(
                        out=rbuf3[:, hs, :],
                        in_=pb[hs // 2][:, (hs % 2) * 256:(hs % 2) * 256 + 256],
                        func=AF.Relu), waits=[t_fk] if hs == 0 else [],
                        tick=(hs == 7))
                t_rl = ("scalar", em.ticks["scalar"])
                kwl2 = [t_rl] + ([t2["kkdma"]] if (hg == 0 and t2["kkdma"]) else [])
                for hs in range(8):
                    hc = hg * 8 + hs
                    em.op("vector", lambda h, hc=hc, hs=hs: h.tensor_tensor(
                        kkb[:, hc, :],
                        pb[hs // 2][:, (hs % 2) * 256:(hs % 2) * 256 + 256],
                        rbuf3[:, hs, :], ALU.mult),
                        waits=kwl2 if hs == 0 else [], tick=(hs == 7))
                t2["kk"][hg] = ("vector", em.ticks["vector"])
            wl = [t2["kk"][3]] + ([t2["kkdma"]] if t2["kkdma"] else [])
            t2["kkdma"] = em.dma("sync", kk_h[j, :, :], kkb_f, waits=wl)

    if "s2b" in stages:
        em.stage_barrier()
        ar.reset(); ar.mark()
        fwv = ar.bf16(NHC * C, shape=[NHC, C])
        tok_fwv = em.dma("gpsimd", fwv, d["fWvT"].rearrange("(hc p) co -> p hc co", p=P))
        fwr = ar.bf16(NCC * C, shape=[NCC, C])
        tok_fwr = em.dma("gpsimd", fwr, d["fWrT"].rearrange("(cc p) co -> p cc co", p=P))
        kkin_f = [ar.bf16(NHC * TC) for _ in range(2)]
        kkin = [view3(b, NHC, TC) for b in kkin_f]
        xrin_f = [ar.bf16(NCC * TC) for _ in range(2)]
        xrin = [view3(b, NCC, TC) for b in xrin_f]
        xin2 = ar.f32(2048); xin23 = view3(xin2, 2, 1024)
        x1in2 = ar.f32(2048); x1in23 = view3(x1in2, 2, 1024)
        sr2buf = ar.f32(2048); sr2b3 = view3(sr2buf, 2, 1024)
        rkvbuf = ar.f32(2048); rkvb3 = view3(rkvbuf, 2, 1024)
        accbuf = ar.f32(2048); accb3 = view3(accbuf, 2, 1024)
        xAbuf = ar.f32(2 * 2048); xAb3 = view3(xAbuf, 4, 1024)

        t3 = {"r2": None, "kv": None, "sr2": None, "rkv": None, "var": None,
              "xAdma": [None, None], "acc": None, "apply": None}
        for j in range(NCH):
            pj = j % 2
            kwl = [t3["kv"]] if t3["kv"] else []
            tok_kk = em.dma("sync", kkin_f[pj], kk_h[j, :, :], waits=kwl)
            rwl = [t3["r2"]] if t3["r2"] else []
            tok_xr = em.dma("scalar", xrin_f[pj], xr2_h[j, :, :], waits=rwl)
            awl = [t3["acc"]] if t3["acc"] else []
            tok_x = em.dma("sync", xin23,
                           x_h[j * TC:(j + 1) * TC, :].rearrange("(tt p) c -> p tt c", p=P),
                           waits=awl)
            tok_x1 = em.dma("scalar", x1in23,
                            x1_h[j * TC:(j + 1) * TC, :].rearrange("(tt p) c -> p tt c", p=P),
                            waits=awl)
            rwl = [tok_xr, tok_fwr]
            if t3["sr2"]:
                rwl.append(t3["sr2"])
            first = True
            for tt in range(2):
                for nh in range(2):
                    for cc in range(NCC):
                        em.op("tensor", lambda h, tt=tt, nh=nh, cc=cc, xr_=xrin[pj]: h.matmul(
                            pbank[2 * tt + nh][:, :],
                            xr_[:, cc, tt * 128:tt * 128 + 128],
                            fwr[:, cc, nh * 512:nh * 512 + 512],
                            start=(cc == 0), stop=(cc == NCC - 1)),
                            waits=rwl if first else [],
                            tick=(tt == 1 and nh == 1 and cc == NCC - 1))
                        first = False
            t_r2 = ("tensor", em.ticks["tensor"])
            t3["r2"] = t_r2
            kvwl = [tok_kk, tok_fwv]
            if t3["rkv"]:
                kvwl.append(t3["rkv"])
            first = True
            for tt in range(2):
                for nh in range(2):
                    for hc in range(NHC):
                        em.op("tensor", lambda h, tt=tt, nh=nh, hc=hc, kk_=kkin[pj]: h.matmul(
                            pbank[4 + 2 * tt + nh][:, :],
                            kk_[:, hc, tt * 128:tt * 128 + 128],
                            fwv[:, hc, nh * 512:nh * 512 + 512],
                            start=(hc == 0), stop=(hc == NHC - 1)),
                            waits=kvwl if first else [],
                            tick=(tt == 1 and nh == 1 and hc == NHC - 1))
                        first = False
            t_kv = ("tensor", em.ticks["tensor"])
            t3["kv"] = t_kv
            for tt in range(2):
                for nh in range(2):
                    em.op("scalar", lambda h, tt=tt, nh=nh: h.activation(
                        out=sr2b3[:, tt, nh * 512:nh * 512 + 512],
                        in_=pbank[2 * tt + nh][:, :], func=AF.Sigmoid),
                        waits=[t_r2] if (tt == 0 and nh == 0) else [],
                        tick=(tt == 1 and nh == 1))
            t3["sr2"] = t_sr2 = ("scalar", em.ticks["scalar"])
            for tt in range(2):
                for nh in range(2):
                    em.op("vector", lambda h, tt=tt, nh=nh: h.tensor_tensor(
                        rkvb3[:, tt, nh * 512:nh * 512 + 512],
                        sr2b3[:, tt, nh * 512:nh * 512 + 512],
                        pbank[4 + 2 * tt + nh][:, :], ALU.mult),
                        waits=[t_sr2, t_kv] if (tt == 0 and nh == 0) else [],
                        tick=(tt == 1 and nh == 1))
            t3["rkv"] = ("vector", em.ticks["vector"])
            for tt in range(2):
                em.op("vector", lambda h, tt=tt: h.tensor_tensor(
                    accb3[:, tt, :], xin23[:, tt, :], x1in23[:, tt, :], ALU.add),
                    waits=[tok_x, tok_x1] if tt == 0 else [])
            sq_wl2 = [t3["var"]] if t3["var"] else []
            for tt in range(2):
                em.op("vector", lambda h, tt=tt: h.scalar_tensor_tensor(
                    accb3[:, tt, :], rkvb3[:, tt, :], 1.0, accb3[:, tt, :],
                    ALU.mult, ALU.add, accum_out=statb[:, 0 + tt:1 + tt]),
                    waits=sq_wl2 if tt == 0 else [], tick=(tt == 1))
            t3["acc"] = t_acc = ("vector", em.ticks["vector"])
            for tt in range(2):
                em.op("scalar", lambda h, tt=tt: h.activation(
                    out=rkvb3[:, tt, :], in_=accb3[:, tt, :], func=AF.Square,
                    accum_out=statb[:, 2 + tt:3 + tt]),
                    waits=[t_acc] if tt == 0 else [], tick=(tt == 1))
            t_sq = ("scalar", em.ticks["scalar"])
            em.op("vector", lambda h: h.tensor_scalar(
                statb[:, 4:6], statb[:, 2:4], 1.0 / C, 1e-5, ALU.mult, ALU.add),
                waits=[t_sq], fence=True)
            t_m = em.op("vector", lambda h: h.tensor_scalar_mul(
                statb[:, 6:8], statb[:, 0:2], 1.0 / C), fence=True, tick=True)
            em.op("scalar", lambda h: h.activation(
                out=statb[:, 8:10], in_=statb[:, 6:8], func=AF.Square),
                waits=[t_m], fence=True, tick=True)
            t_m2s = ("scalar", em.ticks["scalar"])
            t3["var"] = t_var = em.op("vector", lambda h: h.tensor_tensor(
                statb[:, 10:12], statb[:, 4:6], statb[:, 8:10], ALU.subtract),
                waits=[t_m2s], tick=True)
            em.op("scalar", lambda h: h.activation(
                out=statb[:, 14:16], in_=statb[:, 10:12], func=AF.Ln),
                waits=[t_var], fence=True)
            t_rstd = em.op("scalar", lambda h: h.activation(
                out=statb[:, 12:14], in_=statb[:, 14:16], func=AF.Exp, scale=-0.5),
                tick=True)
            xa_wl = [t_rstd] + ([t3["xAdma"][pj]] if t3["xAdma"][pj] else [])
            for tt in range(2):
                em.op("vector", lambda h, tt=tt, pj=pj: h.tensor_scalar(
                    xAb3[:, pj * 2 + tt, :], accb3[:, tt, :], statb[:, 6 + tt:7 + tt],
                    statb[:, 12 + tt:13 + tt], ALU.subtract, ALU.mult),
                    waits=xa_wl if tt == 0 else [], tick=(tt == 1))
            t3["apply"] = t_ap = ("vector", em.ticks["vector"])
            dst = xA_h[j * TC:(j + 1) * TC, :].rearrange("(tt p) c -> p tt c", p=P)
            t3["xAdma"][pj] = em.dma("scalar", dst, xAb3[:, pj * 2:pj * 2 + 2, :],
                                     waits=[t_ap])


    if "s3a" in stages or "s3b" in stages:
        em.stage_barrier()
        ar.reset(); ar.mark()
        w1q = ar.bf16(NCC * HID, shape=[NCC, HID])
        hbuf = ar.f32(NHC * 258 + 2)
        hbuf3 = view3(hbuf[:, :NHC * 258], NHC, 258)
        cbuf = ar.f32(NHC * 256)
        cbuf3 = view3(cbuf, NHC, 256)
        stg = [hbuf[:, 0:4096], hbuf[:, 4096:8192]]
        abscols = cbuf[:, 0:40]
        tot = cbuf[:, 40:48]
        row1 = cbuf[0:1, 48:56]
        swc = scl

        def prep_weight(tag, src_ap, ntile, tile_n, qdst, store_dma,
                        prev_done=None):
            abst = []
            toks_ld = []
            for i in range(ntile):
                wl = [abst[i - 2]] if i >= 2 else (
                    [prev_done] if prev_done else [])
                toks_ld.append(em.dma("sync", stg[i % 2][:, 0:tile_n],
                                      src_ap[:, i, :], waits=wl))
                em.op("scalar", lambda h, i=i: h.activation(
                    out=absdst[:, 0:tile_n], in_=stg[i % 2][:, 0:tile_n],
                    func=AF.Abs, accum_out=abscols[:, i:i + 1]),
                    waits=[toks_ld[i]], tick=True)
                abst.append(("scalar", em.ticks["scalar"]))
            t_abs = abst[-1]
            em.op("vector", lambda h: h.reduce_sum(
                tot[:, 0:1], abscols[:, 0:ntile], axis=AX.X), waits=[t_abs],
                fence=True, tick=True)
            t_tot = ("vector", em.ticks["vector"])
            em.op("tensor", lambda h: h.transpose(
                pbank[0][0:1, 0:128], tot[:, 0:1], ident_ap),
                waits=[t_tot, tok_ident], tick=True)
            t_tr = ("tensor", em.ticks["tensor"])
            em.op("vector", lambda h: h.reduce_sum(
                row1[:, 0:1], pbank[0][0:1, 0:128], axis=AX.X), waits=[t_tr],
                fence=True)
            em.op("vector", lambda h: h.tensor_scalar(
                row1[:, 1:2], row1[:, 0:1], 1.0 / (ntile * tile_n * 128), 1e-5,
                ALU.mult, ALU.max), fence=True)
            em.op("vector", lambda h: h.reciprocal(
                row1[:, 2:3], row1[:, 1:2]), fence=True, tick=True)
            t_sw = ("vector", em.ticks["vector"])
            for (srccol, dstcol) in ((2, 0), (1, 1)):
                em.op("tensor", lambda h, srccol=srccol, dstcol=dstcol: h.matmul(
                    pbank[1][:, dstcol:dstcol + 1], onesb[0:1, 0:128],
                    row1[:, srccol:srccol + 1], start=True, stop=True),
                    waits=[t_sw, tok_ones], tick=True)
            t_bc = ("tensor", em.ticks["tensor"])
            base = 0 if tag == "w1" else 2
            em.op("vector", lambda h, base=base: h.tensor_copy(
                swc[:, base:base + 2], pbank[1][:, 0:2]), waits=[t_bc],
                fence=True, tick=True)
            t_swc = ("vector", em.ticks["vector"])
            toks_ld2 = []
            qt = []
            t_q = None
            for i in range(ntile):
                wl = [qt[i - 2]] if i >= 2 else [t_abs]
                toks_ld2.append(em.dma("sync", stg[i % 2][:, 0:tile_n],
                                       src_ap[:, i, :], waits=wl))
                qwl = [toks_ld2[i], t_swc]
                if store_dma is not None and i >= 2:
                    qwl.append(stok[i % 2])
                em.op("vector", lambda h, i=i, base=base: h.tensor_scalar(
                    stg[i % 2][:, 0:tile_n], stg[i % 2][:, 0:tile_n],
                    swc[:, base:base + 1], MAGIC, ALU.mult, ALU.add),
                    waits=qwl)
                em.op("vector", lambda h, i=i: h.tensor_scalar(
                    stg[i % 2][:, 0:tile_n], stg[i % 2][:, 0:tile_n],
                    MAGIC, -1.0, ALU.subtract, ALU.max))
                em.op("vector", lambda h, i=i, qd=qdst(i): h.tensor_scalar_min(
                    qd, stg[i % 2][:, 0:tile_n], 1.0), tick=True)
                qt.append(("vector", em.ticks["vector"]))
                t_q = qt[-1]
                if store_dma is not None:
                    stok[i % 2] = store_dma(i, t_q)
            return t_q

        stok = [None, None]
        def w2_store(i, tok):
            return em.dma("scalar", w2q_h[:, i * 1024:(i + 1) * 1024],
                          w2qstg[i % 2], waits=[tok])
        w2qstg = [ar.bf16(1024) for _ in range(2)]
        absdst = cbuf[:, 1024:5120]

        t_q1 = prep_weight("w1", d["w1T"].rearrange("(cc p) ho -> p cc ho", p=P),
                           NCC, HID, lambda i: w1q[:, i, :], None)
        t_q2 = prep_weight("w2", d["w2T"].rearrange("(hc p) co -> p hc co", p=P),
                           NHC, C, lambda i: w2qstg[i % 2], w2_store,
                           prev_done=t_q1)

    if "s3a" in stages:
        em.stage_barrier()
        xAin = ar.f32(2 * 2048)
        xAin3 = view3(xAin, 4, 1024)
        xqtm_f = ar.bf16(2048)
        xqtm = view3(xqtm_f, 2, 1024)
        xqfm = [ar.bf16(NCC * 260, shape=[NCC, 260]) for _ in range(2)]
        gqb_f = ar.bf16(NHC * 256)
        gqb = view3(gqb_f, NHC, 256)
        s1row = ar.f32(4104)
        s2row = ar.f32(256)
        hscale = ar.f32(264)
        b2scale = ar.f32(256)
        xqscr = ar.f32(1024)

        em.op("vector", lambda h: h.memset(s1row[0:1, :], 0.0))
        em.op("vector", lambda h: h.memset(xqfm[0][:, :, 0], 0.0))
        t_init3 = em.op("vector", lambda h: h.memset(xqfm[1][:, :, 0], 0.0),
                        tick=True)

        t4 = {"qdone": [None, None], "hburst": [None] * 8, "deq": [None] * 8,
              "gqdma": None, "evacs": [None, None], "hlast": None,
              "convdone": None, "i2dma": None, "si": None}

        def emit_quant(q):
            pq = q % 2
            wl = [t4["qdone"][pq]] if t4["qdone"][pq] else []
            tok_xa = em.dma("sync", xAin3[:, pq * 2:pq * 2 + 2, :],
                            xA_h[q * TC:(q + 1) * TC, :].rearrange(
                                "(tt p) c -> p tt c", p=P), waits=wl)
            for tt in range(2):
                em.op("vector", lambda h, tt=tt, pq=pq: h.reduce_max(
                    scl[:, 8 + tt:9 + tt], xAin3[:, pq * 2 + tt, :], axis=AX.X,
                    apply_absolute_value=True),
                    waits=[tok_xa] if tt == 0 else [], fence=(tt == 1))
            em.op("vector", lambda h: h.tensor_scalar_max(
                scl[:, 10:12], scl[:, 8:10], 1e-5), fence=True)
            em.op("vector", lambda h: h.reciprocal(
                scl[:, 12:14], scl[:, 10:12]), fence=True)
            em.op("vector", lambda h: h.tensor_scalar_mul(
                scl[:, 14:16], scl[:, 12:14], 127.0), fence=True)
            t_sc = em.op("vector", lambda h: h.tensor_scalar(
                scl[:, 16:18], scl[:, 10:12], 1.0 / 127, swc[:, 1:2],
                ALU.mult, ALU.mult), fence=True, tick=True)
            for tt in range(2):
                em.op("vector", lambda h, tt=tt, pq=pq: h.tensor_scalar(
                    xqscr, xAin3[:, pq * 2 + tt, :], scl[:, 14 + tt:15 + tt],
                    MAGIC, ALU.mult, ALU.add), waits=[t_sc] if tt == 0 else [])
                em.op("vector", lambda h, tt=tt: h.tensor_scalar_sub(
                    xqtm[:, tt, :], xqscr, MAGIC), tick=(tt == 1))
            t_xq = ("vector", em.ticks["vector"])
            stwl = [t_sc] + ([t4["qdone"][1 - pq]] if t4["qdone"][1 - pq] else [])
            for tt in range(2):
                em.op("tensor", lambda h, tt=tt: h.transpose(
                    pbank[4][0:1, tt * 128:tt * 128 + 128],
                    scl[:, 16 + tt:17 + tt], ident_ap),
                    waits=stwl if tt == 0 else [], tick=(tt == 1))
            t_rt = ("tensor", em.ticks["tensor"])
            em.op("vector", lambda h, q=q: h.tensor_copy(
                s1row[0:1, q * TC + 1:q * TC + 257], pbank[4][0:1, 0:256]),
                waits=[t_rt], fence=True, tick=True)
            t_s1r = ("vector", em.ticks["vector"])
            twl = [t_xq]
            if t4["hlast"]:
                twl.append(t4["hlast"])
            if t4["deq"][7]:
                twl.append(t4["deq"][7])
            first = True
            for bank in range(4):
                for cc in (bank, bank + 4):
                    for tt in range(2):
                        em.op("tensor", lambda h, cc=cc, tt=tt: h.transpose(
                            pb[cc % 4].bitcast(BF16)[:, (cc // 4) * 512 + tt * 128:
                                                     (cc // 4) * 512 + tt * 128 + 128],
                            xqtm[:, tt, cc * 128:(cc + 1) * 128], identb16),
                            waits=(twl + [tok_ident16]) if first else [],
                            tick=(bank == 3 and cc == 7 and tt == 1))
                        first = False
            t_tr = ("tensor", em.ticks["tensor"])
            ewl = [t_tr]
            if t4["hburst"][7]:
                ewl.append(t4["hburst"][7])
            for cc in range(NCC):
                em.op("scalar", lambda h, cc=cc, pq=pq: h.activation(
                    out=xqfm[pq][:, cc, 1:257],
                    in_=pb[cc % 4].bitcast(BF16)[:, (cc // 4) * 512:
                                                 (cc // 4) * 512 + 256],
                    func=AF.Identity), waits=ewl if cc == 0 else [],
                    tick=(cc == NCC - 1))
            t4["evacs"][pq] = ("scalar", em.ticks["scalar"])
            t4["qdone"][pq] = t_s1r
            return t4["evacs"][pq]

        emit_quant(0)
        for j in range(NCH):
            pj = j % 2
            if j < NCH - 1:
                emit_quant(j + 1)
                em.op("vector", lambda h, pj=pj: h.tensor_copy(
                    xqfm[1 - pj][:, :, 0], xqfm[pj][:, :, 256]),
                    waits=[t4["evacs"][pj], t4["evacs"][1 - pj]], fence=True)
                em.op("vector", lambda h, pj=pj: h.tensor_copy(
                    xqfm[pj][:, :, 257], xqfm[1 - pj][:, :, 1]),
                    fence=True, tick=True)
            else:
                em.op("vector", lambda h, pj=pj: h.memset(
                    xqfm[pj][:, :, 257], 0.0),
                    waits=[t4["evacs"][pj]], fence=True, tick=True)
            t_bnd = ("vector", em.ticks["vector"])
            em.op("tensor", lambda h, j=j: h.matmul(
                pbank[5][:, 0:258], onesb[0:1, 0:128],
                s1row[0:1, j * TC:j * TC + 258], start=True, stop=True),
                waits=[t4["qdone"][pj], tok_ones] +
                      ([t4["b2"]] if t4.get("b2") else []), tick=True)
            t_hb = ("tensor", em.ticks["tensor"])
            em.op("vector", lambda h: h.tensor_copy(
                hscale[:, 0:258], pbank[5][:, 0:258]), waits=[t_hb], tick=True)
            t_hs = ("vector", em.ticks["vector"])
            for hb in range(8):
                first = True
                for hs in range(4):
                    hc = hb * 4 + hs
                    mwl = [t_bnd, t_hs, t_q1]
                    if t4["deq"][(hb - 1) % 8]:
                        mwl.append(t4["deq"][(hb - 1) % 8])
                    if hb == 0:
                        mwl.append(t4["evacs"][1 - pj] if j < NCH - 1 else
                                   t4["evacs"][pj])
                    for cc in range(NCC):
                        em.op("tensor",
                              lambda h, hc=hc, hs=hs, cc=cc, sl=xqfm[pj]: h.matmul(
                                  pbank[hs][:, 0:258],
                                  w1q[:, cc, hc * 128:(hc + 1) * 128],
                                  sl[:, cc, 0:258],
                                  start=(cc == 0), stop=(cc == NCC - 1)),
                              waits=mwl if first else [],
                              tick=(hs == 3 and cc == NCC - 1))
                        first = False
                t4["hburst"][hb] = t_hm = ("tensor", em.ticks["tensor"])
                if hb == 7:
                    t4["hlast"] = t_hm
                for hs in range(4):
                    hc = hb * 4 + hs
                    em.op("vector", lambda h, hc=hc, hs=hs: h.tensor_tensor(
                        hbuf3[:, hc, :], pbank[hs][:, 0:258], hscale[:, 0:258],
                        ALU.mult), waits=[t_hm] if hs == 0 else [],
                        tick=(hs == 3))
                t4["deq"][hb] = ("vector", em.ticks["vector"])
            cwl = [t4["deq"][7]] + ([t4["si"]] if t4["si"] else [])
            for hc in range(NHC):
                em.op("vector", lambda h, hc=hc: h.tensor_scalar_mul(
                    cbuf3[:, hc, :], hbuf3[:, hc, 0:256],
                    convwb[:, hc * 3:hc * 3 + 1]),
                    waits=cwl if hc == 0 else [])
                em.op("vector", lambda h, hc=hc: h.scalar_tensor_tensor(
                    cbuf3[:, hc, :], hbuf3[:, hc, 1:257],
                    convwb[:, hc * 3 + 1:hc * 3 + 2], cbuf3[:, hc, :],
                    ALU.mult, ALU.add))
                em.op("vector", lambda h, hc=hc: h.scalar_tensor_tensor(
                    cbuf3[:, hc, :], hbuf3[:, hc, 2:258],
                    convwb[:, hc * 3 + 2:hc * 3 + 3], cbuf3[:, hc, :],
                    ALU.mult, ALU.add), tick=(hc == NHC - 1))
            t4["convdone"] = t_cv = ("vector", em.ticks["vector"])
            for hc in range(NHC):
                em.op("scalar", lambda h, hc=hc: h.activation(
                    out=hbuf3[:, hc, 0:256], in_=cbuf3[:, hc, :], func=AF.Silu,
                    bias=convbb[:, hc:hc + 1]), waits=[t_cv] if hc == 0 else [],
                    tick=(hc == NHC - 1))
            t4["si"] = t_si = ("scalar", em.ticks["scalar"])
            em.op("vector", lambda h: h.reduce_max(
                cbuf3[:, 30, :], hbuf3[:, :, 0:256].transpose([0, 2, 1]),
                axis=AX.X, apply_absolute_value=True), waits=[t_si],
                fence=True, tick=True)
            t_tm = ("vector", em.ticks["vector"])
            for tt in range(2):
                em.op("tensor", lambda h, tt=tt: h.transpose(
                    pbank[6][:, tt * 128:tt * 128 + 128],
                    cbuf3[:, 30, tt * 128:tt * 128 + 128], ident_ap),
                    waits=[t_tm] if tt == 0 else [], tick=(tt == 1))
            t_tt = ("tensor", em.ticks["tensor"])
            for tt in range(2):
                em.op("vector", lambda h, tt=tt: h.reduce_max(
                    scl[:, 20 + tt:21 + tt], pbank[6][:, tt * 128:tt * 128 + 128],
                    axis=AX.X), waits=[t_tt] if tt == 0 else [], fence=(tt == 1))
            em.op("vector", lambda h: h.tensor_scalar_max(
                scl[:, 22:24], scl[:, 20:22], 1e-5), fence=True)
            em.op("vector", lambda h: h.reciprocal(
                scl[:, 24:26], scl[:, 22:24]), fence=True)
            em.op("vector", lambda h: h.tensor_scalar_mul(
                scl[:, 26:28], scl[:, 24:26], 127.0), fence=True)
            t_i2 = em.op("vector", lambda h: h.tensor_scalar(
                scl[:, 28:30], scl[:, 22:24], 1.0 / 127, swc[:, 3:4],
                ALU.mult, ALU.mult), fence=True, tick=True)
            i2wl = [t_i2] + ([t4["i2dma"]] if t4["i2dma"] else [])
            t4["i2dma"] = em.dma("scalar",
                                 inv2_h[2 * j:2 * j + 2, :].rearrange("t p -> p t"),
                                 scl[:, 28:30], waits=i2wl, slow=True)
            for tt in range(2):
                em.op("tensor", lambda h, tt=tt: h.transpose(
                    pbank[7][0:1, tt * 128:tt * 128 + 128],
                    scl[:, 26 + tt:27 + tt], ident_ap),
                    waits=[t_i2] if tt == 0 else [], tick=(tt == 1))
            t_s2t = ("tensor", em.ticks["tensor"])
            em.op("vector", lambda h: h.tensor_copy(
                s2row[0:1, 0:256], pbank[7][0:1, 0:256]), waits=[t_s2t],
                fence=True, tick=True)
            t_s2r = ("vector", em.ticks["vector"])
            em.op("tensor", lambda h: h.matmul(
                pbank[7][:, 256:512], onesb[0:1, 0:128],
                s2row[0:1, 0:256], start=True, stop=True), waits=[t_s2r],
                tick=True)
            t_b2m = ("tensor", em.ticks["tensor"])
            em.op("vector", lambda h: h.tensor_copy(
                b2scale, pbank[7][:, 256:512]), waits=[t_b2m], tick=True)
            t_b2 = ("vector", em.ticks["vector"])
            t4["b2"] = t_b2
            gwl = [t_b2] + ([t4["gqdma"]] if t4["gqdma"] else [])
            for hc in range(NHC):
                em.op("vector", lambda h, hc=hc: h.tensor_tensor(
                    cbuf3[:, hc, :], hbuf3[:, hc, 0:256], b2scale, ALU.mult),
                    waits=gwl if hc == 0 else [], tick=(hc == NHC - 1))
            t_gm = ("vector", em.ticks["vector"])
            em.op("vector", lambda h: h.tensor_scalar(
                gqb_f, cbuf, MAGIC, MAGIC, ALU.add, ALU.subtract),
                waits=[t_gm], tick=True)
            t_gq = ("vector", em.ticks["vector"])
            t4["gqdma"] = em.dma("sync", gq_h[j, :, :], gqb_f, waits=[t_gq])

    if "s3b" in stages:
        em.stage_barrier()
        ar.reset(); ar.mark()
        w2qb = ar.bf16(NHC * C, shape=[NHC, C])
        tok_w2 = em.dma("gpsimd", w2qb, w2q_h[:, :].rearrange(
            "p (hc co) -> p hc co", hc=NHC))
        gqin_f = [ar.bf16(NHC * TC) for _ in range(2)]
        gqin = [view3(b, NHC, TC) for b in gqin_f]
        xAin2 = ar.f32(2048); xAin23 = view3(xAin2, 2, 1024)
        acc5b = ar.f32(2048); acc5b3 = view3(acc5b, 2, 1024)
        sqscr = ar.f32(2048); sqscr3 = view3(sqscr, 2, 1024)
        outb = ar.f32(2 * 2048); outb3 = view3(outb, 4, 1024)

        t5 = {"mm": None, "acc": None, "var": None, "odma": [None, None],
              "ap": None}
        for j in range(NCH):
            pj = j % 2
            gwl = [t5["mm"]] if t5["mm"] else []
            tok_gq = em.dma("sync", gqin_f[pj], gq_h[j, :, :], waits=gwl)
            awl = [t5["acc"]] if t5["acc"] else []
            tok_xa = em.dma("scalar", xAin23,
                            xA_h[j * TC:(j + 1) * TC, :].rearrange(
                                "(tt p) c -> p tt c", p=P), waits=awl)
            iwl = [t5["acc"]] if t5["acc"] else []
            tok_i2 = em.dma("scalar", scl[:, 0:2],
                            inv2_h[2 * j:2 * j + 2, :].rearrange("t p -> p t"),
                            waits=iwl, slow=True)
            first = True
            for tt in range(2):
                for nh in range(2):
                    for hc in range(NHC):
                        em.op("tensor",
                              lambda h, tt=tt, nh=nh, hc=hc, gq_=gqin[pj]: h.matmul(
                                  pbank[4 + 2 * tt + nh][:, :],
                                  gq_[:, hc, tt * 128:tt * 128 + 128],
                                  w2qb[:, hc, nh * 512:nh * 512 + 512],
                                  start=(hc == 0), stop=(hc == NHC - 1)),
                              waits=([tok_gq, tok_w2] +
                                     ([t5["acc"]] if t5["acc"] else []))
                              if first else [],
                              tick=(tt == 1 and nh == 1 and hc == NHC - 1))
                        first = False
            t5["mm"] = t_mm = ("tensor", em.ticks["tensor"])
            for tt in range(2):
                for nh in range(2):
                    em.op("vector", lambda h, tt=tt, nh=nh: h.scalar_tensor_tensor(
                        acc5b3[:, tt, nh * 512:nh * 512 + 512],
                        pbank[4 + 2 * tt + nh][:, :], scl[:, 0 + tt:1 + tt],
                        xAin23[:, tt, nh * 512:nh * 512 + 512],
                        ALU.mult, ALU.add,
                        accum_out=scl[:, 4 + 2 * tt + nh:5 + 2 * tt + nh]),
                        waits=[t_mm, tok_xa, tok_i2] if (tt == 0 and nh == 0)
                        else [], tick=(tt == 1 and nh == 1))
            t5["acc"] = t_acc = ("vector", em.ticks["vector"])
            em.op("vector", lambda h: h.tensor_tensor(
                statb[:, 0:1], scl[:, 4:5], scl[:, 5:6], ALU.add), waits=[t_acc],
                fence=True)
            em.op("vector", lambda h: h.tensor_tensor(
                statb[:, 1:2], scl[:, 6:7], scl[:, 7:8], ALU.add), fence=True,
                tick=True)
            t_sum = ("vector", em.ticks["vector"])
            sq_wl3 = [t_acc] + ([t5["var"]] if t5["var"] else [])
            for tt in range(2):
                em.op("scalar", lambda h, tt=tt: h.activation(
                    out=sqscr3[:, tt, :], in_=acc5b3[:, tt, :], func=AF.Square,
                    accum_out=statb[:, 2 + tt:3 + tt]),
                    waits=sq_wl3 if tt == 0 else [], tick=(tt == 1))
            t_sq = ("scalar", em.ticks["scalar"])
            em.op("vector", lambda h: h.tensor_scalar(
                statb[:, 4:6], statb[:, 2:4], 1.0 / C, 1e-5, ALU.mult, ALU.add),
                waits=[t_sq], fence=True)
            t_m = em.op("vector", lambda h: h.tensor_scalar_mul(
                statb[:, 6:8], statb[:, 0:2], 1.0 / C), waits=[t_sum],
                fence=True, tick=True)
            em.op("scalar", lambda h: h.activation(
                out=statb[:, 8:10], in_=statb[:, 6:8], func=AF.Square),
                waits=[t_m], fence=True, tick=True)
            t_m2s = ("scalar", em.ticks["scalar"])
            t5["var"] = t_var = em.op("vector", lambda h: h.tensor_tensor(
                statb[:, 10:12], statb[:, 4:6], statb[:, 8:10], ALU.subtract),
                waits=[t_m2s], tick=True)
            em.op("scalar", lambda h: h.activation(
                out=statb[:, 14:16], in_=statb[:, 10:12], func=AF.Ln),
                waits=[t_var], fence=True)
            t_rstd = em.op("scalar", lambda h: h.activation(
                out=statb[:, 12:14], in_=statb[:, 14:16], func=AF.Exp, scale=-0.5),
                tick=True)
            owl = [t_rstd] + ([t5["odma"][pj]] if t5["odma"][pj] else [])
            for tt in range(2):
                em.op("vector", lambda h, tt=tt, pj=pj: h.tensor_scalar(
                    outb3[:, pj * 2 + tt, :], acc5b3[:, tt, :],
                    statb[:, 6 + tt:7 + tt], statb[:, 12 + tt:13 + tt],
                    ALU.subtract, ALU.mult), waits=owl if tt == 0 else [],
                    tick=(tt == 1))
            t_ap = ("vector", em.ticks["vector"])
            dst = out_h[j * TC:(j + 1) * TC, :].rearrange("(tt p) c -> p tt c", p=P)
            t5["odma"][pj] = em.dma("sync", dst, outb3[:, pj * 2:pj * 2 + 2, :],
                                    waits=[t_ap])

    return nc, em, d, ar, (sb_ctx, ps_ctx)


def finalize2(nc, em):
    for q in DMAQ:
        fw = [(f"dma_{q}{l}", 16 * em.dmas[q][l])
              for l in range(NLANES) if em.dmas[q][l]]
        if fw:
            em.ops[q].append([lambda h: h.nop(), fw, None])
    em.check_deadlock()
    import contextlib
    stack = contextlib.ExitStack()
    sems = {e: stack.enter_context(nc.semaphore("sem_" + e)) for e in ENGINES}
    dma_sems = {f"{q}{l}": stack.enter_context(nc.semaphore(f"dsem_{q}{l}"))
                for q in DMAQ for l in range(NLANES)}
    with nc.Block() as block:
        def emit_engine(e):
            def inner(h):
                last_wait = {}
                for fn, waits, kind in em.ops[e]:
                    for w in waits:
                        if w is None:
                            continue
                        name, val = w
                        if last_wait.get(name, -1) >= val:
                            continue
                        last_wait[name] = val
                        sem = dma_sems[name[4:]] if name.startswith("dma_") else sems[name]
                        h.wait_ge(sem, val)
                    ins = fn(h)
                    if kind == "tick":
                        h.drain()
                        h.nop().then_inc(sems[e], 1)
                    elif kind == "fence":
                        h.drain()
                    elif isinstance(kind, tuple) and kind[0] == "dma":
                        ins.then_inc(dma_sems[f"{e}{kind[1]}"], 16)
            return inner

        block.tensor(emit_engine("tensor"))
        block.vector(emit_engine("vector"))
        block.scalar(emit_engine("scalar"))
        block.gpsimd(emit_engine("gpsimd"))
        block.sync(emit_engine("sync"))
    stack.close()
    return nc


_CACHE = {}


def kernel(**inputs):
    shared, xs = host_prep(inputs)
    if "nc" not in _CACHE:
        nc, em, d, ar, ctxs = build()
        finalize2(nc, em)
        _CACHE["nc"] = nc
    nc = _CACHE["nc"]
    in_maps = []
    for b in range(8):
        m = dict(shared)
        m["x"] = xs[b]
        in_maps.append(m)
    res = run_bass_kernel_spmd(nc, in_maps, core_ids=list(range(8)))
    out = np.stack([np.asarray(res.results[b]["out"]) for b in range(8)])
    return out.astype(np.float32, copy=False)


# version 03 (final)
# speedup vs baseline: 1.0794x; 1.0794x over previous
import numpy as np
import ml_dtypes
import concourse.bass as bass
import concourse.mybir as mybir
from concourse.bass_utils import run_bass_kernel_spmd

F32 = mybir.dt.float32
BF16 = mybir.dt.bfloat16
AF = mybir.ActivationFunctionType
ALU = mybir.AluOpType
AX = mybir.AxisListType

P = 128
C = 1024
T = 4096
HID = 4096
TC = 256
NCH = T // TC
NCC = C // P
NHC = HID // P
NTT = TC // P
MAGIC = 12582912.0
ARENA_F32 = 52224

ENGINES = ("tensor", "vector", "scalar", "gpsimd", "sync")
DMAQ = ("sync", "scalar", "gpsimd")
NLANES = 3


class Em:

    def __init__(self):
        self.ops = {e: [] for e in ENGINES}
        self.ticks = {e: 0 for e in ENGINES}
        self.dmas = {q: [0] * NLANES for q in DMAQ}
        self.rr = {q: 0 for q in DMAQ}
        self.pending = {e: [] for e in ENGINES}

    def op(self, eng, fn, waits=(), tick=False, fence=False):
        kind = "tick" if tick else ("fence" if fence else None)
        w = self.pending[eng] + list(waits)
        self.pending[eng] = []
        self.ops[eng].append([fn, w, kind])
        if tick:
            self.ticks[eng] += 1
            return (eng, self.ticks[eng])
        return None

    def dma(self, q, out, in_, waits=(), lane=None, slow=False):
        if lane is None:
            lane = self.rr[q]
            self.rr[q] = (lane + 1) % NLANES
        def fn(h):
            return h.dma_start(out=out, in_=in_,
                               allow_slow_non_contiguous=slow)
        w = self.pending[q] + list(waits)
        self.pending[q] = []
        prev = self.dmas[q][lane]
        if prev:
            w.append((f"dma_{q}{lane}", 16 * prev))
        self.ops[q].append([fn, w, ("dma", lane)])
        self.dmas[q][lane] = prev + 1
        return (f"dma_{q}{lane}", 16 * (prev + 1))

    def stage_barrier(self):
        toks = [(e, self.ticks[e]) for e in ENGINES if self.ticks[e] > 0]
        toks += [(f"dma_{q}{l}", 16 * self.dmas[q][l])
                 for q in DMAQ for l in range(NLANES) if self.dmas[q][l]]
        for e in ENGINES:
            self.pending[e] = [t for t in toks
                               if not (t[0] == e and self.ticks[e] == t[1])]

    def barrier_token(self, eng):
        if self.ticks[eng] == 0:
            return None
        return (eng, self.ticks[eng])

    def check_deadlock(self):
        idx = {e: 0 for e in ENGINES}
        done_ticks = {e: 0 for e in ENGINES}
        done_dmas = {f"{q}{l}": 0 for q in DMAQ for l in range(NLANES)}

        def ok(w):
            if w is None:
                return True
            name, val = w
            if name.startswith("dma_"):
                return 16 * done_dmas[name[4:]] >= val
            return done_ticks[name] >= val

        progress = True
        while progress:
            progress = False
            for e in ENGINES:
                while idx[e] < len(self.ops[e]):
                    fn, waits, kind = self.ops[e][idx[e]]
                    if not all(ok(w) for w in waits):
                        break
                    if kind == "tick":
                        done_ticks[e] += 1
                    elif isinstance(kind, tuple) and kind[0] == "dma":
                        done_dmas[f"{e}{kind[1]}"] += 1
                    idx[e] += 1
                    progress = True
        stuck = {e: (idx[e], len(self.ops[e])) for e in ENGINES
                 if idx[e] < len(self.ops[e])}
        if stuck:
            detail = {}
            for e in stuck:
                fn, waits, kind = self.ops[e][idx[e]]
                detail[e] = (idx[e], waits)
            raise RuntimeError(f"deadlock: stuck at {detail}")

    def emit(self, nc, handles, sems, dma_sems):
        for e in ENGINES:
            h = handles[e]
            last_wait = {}
            for fn, waits, kind in self.ops[e]:
                for w in waits:
                    if w is None:
                        continue
                    name, val = w
                    if last_wait.get(name, -1) >= val:
                        continue
                    last_wait[name] = val
                    sem = dma_sems[name[4:]] if name.startswith("dma_") else sems[name]
                    h.wait_ge(sem, val)
                ins = fn(h)
                if kind == "tick":
                    ins.then_inc(sems[e], 1)
                elif kind == "dma":
                    ins.then_inc(dma_sems[e], 16)


def ceil8(n):
    return (n + 7) & ~7


class Arena:
    def __init__(self, handle):
        self.h = handle
        self.off = 0
        self.marks = []

    def alloc(self, n_f32):
        n_f32 = ceil8(n_f32)
        assert self.off + n_f32 <= ARENA_F32, f"arena overflow {self.off}+{n_f32}"
        off = self.off
        self.off += n_f32
        return off

    def f32(self, n, shape=None):
        off = self.alloc(n)
        ap = self.h.ap()[:, off:off + n]
        if shape is not None:
            ap = ap.rearrange("p (" + " ".join(f"d{i}" for i in range(len(shape))) + ") -> p " +
                              " ".join(f"d{i}" for i in range(len(shape))),
                              **{f"d{i}": s for i, s in enumerate(shape)})
        return ap

    def bf16(self, n, shape=None):
        off = self.alloc((n + 1) // 2)
        ap = self.h.ap().bitcast(BF16)[:, 2 * off:2 * off + n]
        if shape is not None:
            ap = ap.rearrange("p (" + " ".join(f"d{i}" for i in range(len(shape))) + ") -> p " +
                              " ".join(f"d{i}" for i in range(len(shape))),
                              **{f"d{i}": s for i, s in enumerate(shape)})
        return ap

    def mark(self):
        self.marks.append(self.off)

    def reset(self):
        self.off = self.marks.pop()


VEC8 = ["td", "tf", "tmk", "tmv", "tmr", "g1", "b1", "g2", "b2", "ftmk", "ftmr"]


def host_prep(inputs):
    F = np.float32
    g = {k: np.ascontiguousarray(np.asarray(v)) for k, v in inputs.items()}
    C_ = C

    def t2(a):
        return np.ascontiguousarray(np.asarray(a).T)

    def v8(a):
        return np.ascontiguousarray(np.asarray(a).reshape(NCC, P).T)

    shared = {
        "WkT": t2(g["L0_attn_Wk"]), "WvT": t2(g["L0_attn_Wv"]),
        "WrT": t2(g["L0_attn_Wr"]), "WoT": t2(g["L0_attn_Wo"]),
        "fWkT": t2(g["L0_attn_fWk"]), "fWrT": t2(g["L0_attn_fWr"]),
        "fWvT": t2(g["L0_attn_fWv"]),
        "w1T": t2(g["L0_ffn_w1"]), "w2T": t2(g["L0_ffn_w2"]),
        "vecs": np.ascontiguousarray(np.stack([
            v8(g["L0_attn_time_decay"]), v8(g["L0_attn_time_first"]),
            v8(g["L0_attn_tmk"].reshape(C_)), v8(g["L0_attn_tmv"].reshape(C_)),
            v8(g["L0_attn_tmr"].reshape(C_)),
            v8(g["L0_attn_ln1_g"]), v8(g["L0_attn_ln1_b"]),
            v8(g["L0_attn_ln2_g"]), v8(g["L0_attn_ln2_b"]),
            v8(g["L0_attn_f_tmk"].reshape(C_)), v8(g["L0_attn_f_tmr"].reshape(C_)),
        ], axis=1).reshape(P, len(VEC8) * NCC)),
        "ident": np.eye(P, dtype=F),
        "ident16": np.eye(P, dtype=F).astype(ml_dtypes.bfloat16),
        "ones1": np.ones((1, P), dtype=F),
        "convw": np.ascontiguousarray(
            g["L0_ffn_conv_w"].reshape(NHC, P, 3).transpose(1, 0, 2).reshape(P, NHC * 3)),
        "convb": np.ascontiguousarray(
            g["L0_ffn_conv_b"].reshape(NHC, P).T),
    }
    xs = [np.ascontiguousarray(g["L0_attn_x"][b]) for b in range(8)]
    return shared, xs


def build(stages=("s1", "s2a", "s2b", "s3a", "s3b"), debug=()):
    nc = bass.Bass("TRN2", target_bir_lowering=False, debug=False,
                   detect_race_conditions=False)
    em = Em()
    d = {}

    def dram(name, shape, dt=F32, kind="ExternalInput"):
        t = nc.dram_tensor(name, shape, dt, kind=kind).ap()
        d[name] = t
        return t

    def scratch(name, shape, dt=F32):
        kind = "ExternalOutput" if name in debug else "Internal"
        return dram(name, shape, dt, kind=kind)

    x_h = dram("x", [T, C])
    for n, shape in [("WkT", [C, C]), ("WvT", [C, C]), ("WrT", [C, C]),
                     ("WoT", [C, C]), ("fWkT", [C, HID]), ("fWrT", [C, C]),
                     ("fWvT", [HID, C]), ("w1T", [C, HID]), ("w2T", [HID, C])]:
        dram(n, shape)
    dram("vecs", [P, len(VEC8) * NCC])
    dram("ident", [P, P])
    dram("ident16", [P, P], BF16)
    dram("ones1", [1, P])
    dram("convw", [P, NHC * 3])
    dram("convb", [P, NHC])
    x1_h = scratch("x1s", [T, C])
    kk_h = scratch("kks", [NCH, P, NHC * TC], BF16)
    xr2_h = scratch("xr2s", [NCH, P, NCC * TC], BF16)
    sr2_h = scratch("sr2s", [T, C])
    xA_h = scratch("xAs", [T, C])
    gq_h = scratch("gqs", [NCH, P, NHC * TC], BF16)
    inv2_h = scratch("inv2s", [T // P, P])
    w2q_h = scratch("w2qs", [P, NHC * 1024], BF16)
    out_h = dram("out", [T, C], kind="ExternalOutput")

    sb_ctx = nc.sbuf_tensor("arena", [P, ARENA_F32], F32)
    arena_h = sb_ctx.__enter__()
    ar = Arena(arena_h)

    ps_ctx = []
    pbank = []
    for i in range(8):
        c = nc.psum_tensor(f"pbank{i}", [P, 512], F32)
        ps_ctx.append(c)
        pbank.append(c.__enter__())
    pb = pbank[:4]

    identb = ar.f32(P)
    identb16 = ar.bf16(P)

    onesb = ar.f32(P)
    vecsb = ar.f32(len(VEC8) * NCC)
    derv = ar.f32(80)
    state = ar.f32(80)
    statb = ar.f32(32)
    convwb = ar.f32(NHC * 3)
    convbb = ar.f32(NHC)
    scl = ar.f32(32)

    def vcol(name, i=0):
        j = VEC8.index(name) * NCC
        return vecsb[:, j + i:j + i + 1]

    def vgrp(name):
        j = VEC8.index(name) * NCC
        return vecsb[:, j:j + NCC]

    ident_ap = identb

    tok_ident = em.dma("sync", identb, d["ident"][:, :])
    tok_ident16 = em.dma("sync", identb16, d["ident16"][:, :])
    tok_ones = em.dma("sync", onesb[0:1, :], d["ones1"][:, :])
    tok_vecs = em.dma("sync", vecsb, d["vecs"][:, :])
    tok_convw = em.dma("sync", convwb, d["convw"][:, :])
    tok_convb = em.dma("sync", convbb, d["convb"][:, :])

    ew = derv[:, 0:8]; wneg = derv[:, 8:16]; lam = derv[:, 16:24]
    eu = derv[:, 24:32]; allow = derv[:, 32:40]
    SA = state[:, 0:8]; SB = state[:, 8:16]; mrun = state[:, 16:24]
    tmpa = state[:, 24:32]; mnew = state[:, 32:40]; mdelta = state[:, 40:48]
    fac = state[:, 48:56]; negm = state[:, 56:64]; ck = state[:, 64:72]

    t_ew = em.op("scalar", lambda h: h.activation(out=ew, in_=vgrp("td"), func=AF.Exp),
                 waits=[tok_vecs], tick=True)
    t_wneg = em.op("vector", lambda h: h.tensor_scalar_mul(wneg, ew, -1.0),
                   waits=[t_ew], tick=True)
    t_lam = em.op("scalar", lambda h: h.activation(out=lam, in_=wneg, func=AF.Exp),
                  waits=[t_wneg], tick=True)
    t_eu = em.op("scalar", lambda h: h.activation(out=eu, in_=vgrp("tf"), func=AF.Exp),
                 waits=[tok_vecs], tick=True)
    t_allow = em.op("vector", lambda h: h.tensor_scalar(allow, ew, float(TC), 60.0,
                                                        ALU.mult, ALU.min),
                    waits=[t_ew], tick=True)
    em.op("vector", lambda h: h.memset(SA, 0.0))
    em.op("vector", lambda h: h.memset(SB, 0.0))
    t_state0 = em.op("vector", lambda h: h.memset(mrun, -1e30), tick=True)

    ar.mark()
    wbuf = {}
    tok_w = {}
    for n in ["WkT", "WvT", "WrT", "WoT"]:
        wbuf[n] = ar.bf16(NCC * C, shape=[NCC, C])
        srcw = d[n].rearrange("(cc p) co -> p cc co", p=P)
        tok_w[n] = em.dma("gpsimd", wbuf[n], srcw)

    xin = ar.f32(2 * 2048)
    z1tm = ar.f32(2048)
    z1fm = ar.f32(NCC * 257 + 7)
    dbuf = ar.f32(2048)
    xkb = ar.bf16(NCC * TC, shape=[NCC, TC])
    xvb = ar.bf16(NCC * TC, shape=[NCC, TC])
    xrb = ar.bf16(NCC * TC, shape=[NCC, TC])
    ekbuf = ar.f32(2048)
    ekvbuf = ar.f32(2048)
    AAbuf = ar.f32(NCC * 264)
    BBbuf = ar.f32(NCC * 264)
    numbuf = ar.f32(2048)
    denbuf = ar.f32(2048)
    srbuf = ar.f32(2048)
    sryb_f = ar.bf16(NCC * TC)
    x1buf = ar.f32(2 * 2048)

    def view3(ap, a, b):
        return ap.rearrange("p (a b) -> p a b", a=a, b=b)

    xin3 = view3(xin, 4, 1024)
    z1tm3 = view3(z1tm, 2, 1024)
    z1fm3 = view3(z1fm[:, :NCC * 257], NCC, 257)
    dbuf3 = view3(dbuf, NCC, TC)
    ek3 = view3(ekbuf, NCC, TC)
    ekv3 = view3(ekvbuf, NCC, TC)
    AA3 = view3(AAbuf, NCC, 264)
    BB3 = view3(BBbuf, NCC, 264)
    num3 = view3(numbuf, NCC, TC)
    den3 = view3(denbuf, NCC, TC)
    sr3 = view3(srbuf, NCC, TC)
    x13 = view3(x1buf, 4, 1024)
    sryb = view3(sryb_f, NCC, TC)

    t_z1fm0 = em.op("vector", lambda h: h.memset(z1fm3[:, :, 0], 0.0), tick=True)

    if "s1" in stages:
        PH = 99
        SKIP = set()
        toks = {"sr": None, "x1evac": [None, None], "evacz1": None, "ek": None,
                "ekv": None, "x1dma": [None, None], "y": None, "sry": None,
                "mixr": None, "var": None, "den": None, "vd": None}
        for j in range(NCH):
            pj = j % 2
            xa = [xin3[:, pj * 2 + tt, :] for tt in range(2)]
            x1a = [x13[:, pj * 2 + tt, :] for tt in range(2)]
            xin_t = lambda tt, xa=xa: xa[tt]
            x1_t = lambda tt, x1a=x1a: x1a[tt]
            src = x_h[j * TC:(j + 1) * TC, :].rearrange("(tt p) c -> p tt c", p=P)
            wl = [toks["x1evac"][pj]] if toks["x1evac"][pj] else []
            tok_x = em.dma("sync", view3(xin, 4, 1024)[:, pj * 2:pj * 2 + 2, :], src,
                           waits=wl)
            for tt in (range(2) if "1" not in SKIP else []):
                em.op("vector", lambda h, tt=tt, xin_t=xin_t: h.reduce_sum(
                    statb[:, 0 + tt:1 + tt], xin_t(tt), axis=AX.X), waits=[tok_x])
            t_sum = em.op("vector", lambda h: h.nop(), tick=True)
            sq_wl = [tok_x] + ([toks["var"]] if toks["var"] else [])
            for tt in (range(2) if "1" not in SKIP else []):
                em.op("scalar", lambda h, tt=tt, xin_t=xin_t: h.activation(
                    out=z1tm3[:, tt, :], in_=xin_t(tt), func=AF.Square,
                    accum_out=statb[:, 2 + tt:3 + tt]), waits=sq_wl if tt == 0 else [])
            t_sq = em.op("scalar", lambda h: h.nop(), tick=True)
            if PH < 2:
                continue
            if "2" in SKIP:
                t_var = t_rstd = None
            if "2" not in SKIP:
              em.op("vector", lambda h: h.tensor_scalar(
                statb[:, 4:6], statb[:, 2:4], 1.0 / C, 1e-5, ALU.mult, ALU.add),
                waits=[t_sq])
            if "2" not in SKIP:
              em.op("vector", lambda h: h.tensor_scalar_mul(
                statb[:, 6:8], statb[:, 0:2], 1.0 / C), waits=[t_sum])
            if "2" not in SKIP:
              t_m = em.op("vector", lambda h: h.nop(), tick=True)
              em.op("scalar", lambda h: h.activation(
                out=statb[:, 8:10], in_=statb[:, 6:8], func=AF.Square),
                waits=[t_m])
            if "2" not in SKIP:
              t_m2s = em.op("scalar", lambda h: h.nop(), tick=True)
              toks["var"] = t_var = em.op("vector", lambda h: h.tensor_tensor(
                statb[:, 10:12], statb[:, 4:6], statb[:, 8:10], ALU.subtract),
                waits=[t_m2s], tick=True)
            if "2" not in SKIP:
              em.op("scalar", lambda h: h.activation(
                out=statb[:, 14:16], in_=statb[:, 10:12], func=AF.Ln), waits=[t_var],
                fence=True)
            if "2" not in SKIP:
              t_rstd = em.op("scalar", lambda h: h.activation(
                out=statb[:, 12:14], in_=statb[:, 14:16], func=AF.Exp, scale=-0.5),
                tick=True)
            if PH < 3:
                continue
            for tt in (range(2) if "3" not in SKIP else []):
                em.op("vector", lambda h, tt=tt, xin_t=xin_t: h.tensor_scalar(
                    z1tm3[:, tt, :], xin_t(tt), statb[:, 6 + tt:7 + tt],
                    statb[:, 12 + tt:13 + tt], ALU.subtract, ALU.mult),
                    waits=[t_rstd] if t_rstd else [])
            t_apply = em.op("vector", lambda h: h.nop(), tick=True)
            if PH < 4:
                continue
            if j > 0 and "4" not in SKIP:
                em.op("vector", lambda h: h.tensor_copy(
                    z1fm3[:, :, 0], z1fm3[:, :, 256]), waits=[toks["evacz1"]])
            t_bnd = em.op("vector", lambda h: h.nop(), tick=True)
            if PH < 5:
                continue
            twl = [t_apply, t_bnd]
            if toks["sr"]:
                twl.append(toks["sr"])
            if j == 0:
                twl += [tok_ident, t_z1fm0]
            _first = True
            for bank in range(4):
                for cc in (bank, bank + 4):
                    for tt in range(2):
                        em.op("tensor", lambda h, cc=cc, tt=tt: h.transpose(
                            pb[cc % 4][:, (cc // 4) * 256 + tt * 128:
                                       (cc // 4) * 256 + tt * 128 + 128],
                            z1tm3[:, tt, cc * 128:(cc + 1) * 128], ident_ap),
                            waits=twl if _first else [])
                        _first = False
            t_tr = em.op("tensor", lambda h: h.nop(), tick=True)
            if PH < 6:
                continue
            evz_wl = [t_tr] + ([tok_vecs] if j == 0 else [])
            _plain = False
            for ci_, cc in enumerate(range(NCC)):
                if True:
                    em.op("scalar", lambda h, cc=cc: h.activation(
                        out=z1fm3[:, cc, 1:257],
                        in_=pb[cc % 4][:, (cc // 4) * 256:(cc // 4) * 256 + 256],
                        func=AF.Identity, scale=vcol("g1", cc), bias=vcol("b1", cc)),
                        waits=evz_wl if ci_ == 0 else [])
            toks["evacz1"] = t_evz = em.op("scalar", lambda h: h.nop(), tick=True)
            if PH < 7:
                continue
            d_wl = [t_evz] + ([toks["den"]] if toks["den"] else [])
            if j == 0:
                d_wl.append(tok_vecs)
            toks["vd"] = t_d = em.op("vector", lambda h: h.tensor_tensor(
                dbuf3, z1fm3[:, :, 1:257], z1fm3[:, :, 0:256], ALU.subtract),
                waits=d_wl, tick=True)
            if PH < 8:
                continue
            for cc in range(NCC):
                em.op("vector", lambda h, cc=cc: h.scalar_tensor_tensor(
                    xkb[:, cc, :], dbuf3[:, cc, :], vcol("tmk", cc),
                    z1fm3[:, cc, 0:256], ALU.mult, ALU.add),
                    waits=[t_d] if cc == 0 else [])
                em.op("vector", lambda h, cc=cc: h.scalar_tensor_tensor(
                    xvb[:, cc, :], dbuf3[:, cc, :], vcol("tmv", cc),
                    z1fm3[:, cc, 0:256], ALU.mult, ALU.add))
                em.op("vector", lambda h, cc=cc: h.scalar_tensor_tensor(
                    xrb[:, cc, :], dbuf3[:, cc, :], vcol("tmr", cc),
                    z1fm3[:, cc, 0:256], ALU.mult, ALU.add))
            t_mixkv = t_mixr = em.op("vector", lambda h: h.nop(), tick=True)
            toks["mixr"] = t_mixr
            if PH < 9:
                continue
            kwl = [t_mixkv, tok_w["WkT"], t_evz]
            kwl = [w for w in kwl if w]
            if toks["ek"]:
                kwl.append(toks["ek"])
            for co in range(NCC):
                for ci in range(NCC):
                    em.op("tensor", lambda h, co=co, ci=ci: h.matmul(
                        pb[co // 2][:, (co % 2) * 256:(co % 2) * 256 + 256],
                        wbuf["WkT"][:, ci, co * 128:(co + 1) * 128],
                        xkb[:, ci, :], start=(ci == 0), stop=(ci == NCC - 1)),
                        waits=kwl if (co == 0 and ci == 0) else [])
            t_k = em.op("tensor", lambda h: h.nop(), tick=True)
            if PH < 10:
                continue
            for co in range(NCC):
                em.op("vector", lambda h, co=co: h.reduce_max(
                    ck[:, co:co + 1],
                    pb[co // 2][:, (co % 2) * 256:(co % 2) * 256 + 256], axis=AX.X),
                    waits=[t_k, t_allow, t_state0] if co == 0 else [],
                    fence=(co == NCC - 1))
            em.op("vector", lambda h: h.tensor_tensor(tmpa, mrun, allow, ALU.subtract), fence=True)
            em.op("vector", lambda h: h.tensor_tensor(mnew, tmpa, ck, ALU.max), fence=True)
            em.op("vector", lambda h: h.tensor_tensor(mdelta, mrun, mnew, ALU.subtract), fence=True)
            em.op("vector", lambda h: h.tensor_scalar_mul(negm, mnew, -1.0), fence=True)
            t_st1 = em.op("vector", lambda h: h.tensor_copy(mrun, mnew), tick=True)
            t_fac = em.op("scalar", lambda h: h.activation(
                out=fac, in_=mdelta, func=AF.Exp), waits=[t_st1], tick=True)
            em.op("vector", lambda h: h.tensor_tensor(SA, SA, fac, ALU.mult),
                  waits=[t_fac], fence=True)
            t_st2 = em.op("vector", lambda h: h.tensor_tensor(SB, SB, fac, ALU.mult),
                          tick=True)
            if PH < 11:
                continue
            vwl = [t_mixkv, tok_w["WvT"]]
            vwl = [w for w in vwl if w]
            if toks["x1evac"][1 - pj]:
                vwl.append(toks["x1evac"][1 - pj])
            for co in range(NCC):
                for ci in range(NCC):
                    em.op("tensor", lambda h, co=co, ci=ci: h.matmul(
                        pbank[4 + co // 2][:, (co % 2) * 256:(co % 2) * 256 + 256],
                        wbuf["WvT"][:, ci, co * 128:(co + 1) * 128],
                        xvb[:, ci, :], start=(ci == 0), stop=(ci == NCC - 1)),
                        waits=vwl if (co == 0 and ci == 0) else [])
            t_v = em.op("tensor", lambda h: h.nop(), tick=True)
            if PH < 12:
                continue
            ek_wl = [t_st1, t_k] + ([toks["y"]] if toks["y"] else [])
            for co in range(NCC):
                em.op("scalar", lambda h, co=co: h.activation(
                    out=ek3[:, co, :],
                    in_=pb[co // 2][:, (co % 2) * 256:(co % 2) * 256 + 256],
                    func=AF.Exp, bias=negm[:, co:co + 1]),
                    waits=ek_wl if co == 0 else [])
            toks["ek"] = t_ek = em.op("scalar", lambda h: h.nop(), tick=True)
            if PH < 13:
                continue
            for co in range(NCC):
                em.op("vector", lambda h, co=co: h.tensor_tensor(
                    ekv3[:, co, :], ek3[:, co, :],
                    pbank[4 + co // 2][:, (co % 2) * 256:(co % 2) * 256 + 256], ALU.mult),
                    waits=[t_ek, t_v] if co == 0 else [])
            toks["ekv"] = t_ekv = em.op("vector", lambda h: h.nop(), tick=True)
            if PH < 14:
                continue
            em.op("vector", lambda h: h.tensor_copy(AA3[:, :, 0], SA),
                  waits=[t_st2, t_ekv])
            for co in range(NCC):
                em.op("vector", lambda h, co=co: h.tensor_tensor_scan(
                    AA3[:, co, 1:257], lam[:, co:co + 1].broadcast_to([P, TC]),
                    ekv3[:, co, :], SA[:, co:co + 1], ALU.mult, ALU.add),
                    waits=[t_lam] if co == 0 else [])
            t_scanA = em.op("vector", lambda h: h.nop(), tick=True)
            scb_wl = [t_st2, t_ek] + ([toks["den"]] if toks["den"] else [])
            em.op("vector", lambda h: h.tensor_copy(BB3[:, :, 0], SB),
                  waits=scb_wl)
            for co in range(NCC):
                em.op("vector", lambda h, co=co: h.tensor_tensor_scan(
                    BB3[:, co, 1:257], lam[:, co:co + 1].broadcast_to([P, TC]),
                    ek3[:, co, :], SB[:, co:co + 1], ALU.mult, ALU.add),
                    waits=[t_lam] if co == 0 else [])
            t_scanB = em.op("vector", lambda h: h.nop(), tick=True)
            if PH < 15:
                continue
            em.op("vector", lambda h: h.tensor_copy(SA, AA3[:, :, 256]),
                  waits=[t_scanA])
            t_carA = em.op("vector", lambda h: h.nop(), tick=True)
            em.op("vector", lambda h: h.tensor_copy(SB, BB3[:, :, 256]),
                  waits=[t_scanB])
            t_carB = em.op("vector", lambda h: h.nop(), tick=True)
            if PH < 16:
                continue
            num_wl = [t_eu, t_scanA] + ([toks["y"]] if toks["y"] else [])
            for co in range(NCC):
                em.op("vector", lambda h, co=co: h.scalar_tensor_tensor(
                    num3[:, co, :], ekv3[:, co, :], eu[:, co:co + 1],
                    AA3[:, co, 0:256], ALU.mult, ALU.add),
                    waits=num_wl if co == 0 else [])
            t_num = em.op("vector", lambda h: h.nop(), tick=True)
            et_wl = [t_eu, t_ek, t_mixkv] + ([toks["vd"]] if toks["vd"] else [])
            for co in range(NCC):
                em.op("gpsimd", lambda h, co=co: h.tensor_tensor(
                    dbuf3[:, co, :], ek3[:, co, :],
                    eu[:, co:co + 1].broadcast_to([P, TC]), ALU.mult),
                    waits=et_wl if co == 0 else [])
            for co in range(NCC):
                em.op("gpsimd", lambda h, co=co: h.tensor_tensor(
                    den3[:, co, :], dbuf3[:, co, :], BB3[:, co, 0:256], ALU.add),
                    waits=[t_scanB] if co == 0 else [])
            toks["den"] = t_den = em.op("gpsimd", lambda h: h.nop(), tick=True)
            if PH < 17:
                continue
            em.op("scalar", lambda h: h.activation(
                out=ekbuf, in_=denbuf, func=AF.Ln), waits=[t_den], fence=True)
            t_rden = em.op("scalar", lambda h: h.activation(
                out=denbuf, in_=ekbuf, func=AF.Exp, scale=-1.0), tick=True)
            if PH < 18:
                continue
            rwl = [t_mixr, tok_w["WrT"], t_ek]
            rwl = [w for w in rwl if w]
            for co in range(NCC):
                for ci in range(NCC):
                    em.op("tensor", lambda h, co=co, ci=ci: h.matmul(
                        pb[co // 2][:, (co % 2) * 256:(co % 2) * 256 + 256],
                        wbuf["WrT"][:, ci, co * 128:(co + 1) * 128],
                        xrb[:, ci, :], start=(ci == 0), stop=(ci == NCC - 1)),
                        waits=rwl if (co == 0 and ci == 0) else [])
            t_r = em.op("tensor", lambda h: h.nop(), tick=True)
            sr_wl = [t_r] + ([toks["sry"]] if toks["sry"] else [])
            for co in range(NCC):
                em.op("scalar", lambda h, co=co: h.activation(
                    out=sr3[:, co, :],
                    in_=pb[co // 2][:, (co % 2) * 256:(co % 2) * 256 + 256],
                    func=AF.Sigmoid), waits=sr_wl if co == 0 else [])
            toks["sr"] = t_sr = em.op("scalar", lambda h: h.nop(), tick=True)
            if PH < 19:
                continue
            toks["y"] = t_y = em.op("gpsimd", lambda h: h.tensor_tensor(
                ekbuf, numbuf, denbuf, ALU.mult), waits=[t_num, t_rden], tick=True)
            toks["sry"] = t_sry = em.op("gpsimd", lambda h: h.tensor_tensor(
                sryb_f, ekbuf, srbuf, ALU.mult),
                waits=[t_sr], tick=True)
            if PH < 20:
                continue
            wwl = [t_sry, tok_w["WoT"], t_ekv]
            wwl = [w for w in wwl if w]
            first = True
            for tt in range(2):
                for nh in range(2):
                    for ci in range(NCC):
                        em.op("tensor", lambda h, tt=tt, nh=nh, ci=ci: h.matmul(
                            pbank[4 + 2 * tt + nh][:, :],
                            sryb[:, ci, tt * 128:tt * 128 + 128],
                            wbuf["WoT"][:, ci, nh * 512:nh * 512 + 512],
                            start=(ci == 0), stop=(ci == NCC - 1)),
                            waits=wwl if first else [])
                        first = False
            t_wo = em.op("tensor", lambda h: h.nop(), tick=True)
            if PH < 21:
                continue
            x1_wl = [t_wo] + ([toks["x1dma"][pj]] if toks["x1dma"][pj] else [])
            for tt in range(2):
                for nh in range(2):
                    em.op("vector", lambda h, tt=tt, nh=nh, x1_t=x1_t, xin_t=xin_t:
                          h.tensor_tensor(
                              x1_t(tt)[:, nh * 512:nh * 512 + 512],
                              xin_t(tt)[:, nh * 512:nh * 512 + 512],
                              pbank[4 + 2 * tt + nh][:, :], ALU.add),
                          waits=x1_wl if (tt == 0 and nh == 0) else [])
            toks["x1evac"][pj] = t_x1 = em.op(
                "vector", lambda h: h.nop(), tick=True)
            dst = x1_h[j * TC:(j + 1) * TC, :].rearrange("(tt p) c -> p tt c", p=P)
            toks["x1dma"][pj] = em.dma(
                "sync", dst, x13[:, pj * 2:pj * 2 + 2, :], waits=[t_x1])


    if "s2a" in stages:
        em.stage_barrier()
        ar.reset(); ar.mark()
        fwk = ar.bf16(NCC * HID, shape=[NCC, HID])
        tok_fwk = em.dma("gpsimd", fwk, d["fWkT"].rearrange("(cc p) ho -> p cc ho", p=P))
        x1in = ar.f32(2 * 2048)
        x1in3 = view3(x1in, 4, 1024)
        z2tm = ar.f32(2048); z2tm3 = view3(z2tm, 2, 1024)
        z2fm = ar.f32(NCC * 257 + 7); z2fm3 = view3(z2fm[:, :NCC * 257], NCC, 257)
        dbuf2 = ar.f32(2048); dbuf23 = view3(dbuf2, NCC, TC)
        xk2b = ar.bf16(NCC * TC, shape=[NCC, TC])
        xr2b_f = ar.bf16(NCC * TC)
        xr2b = view3(xr2b_f, NCC, TC)
        kkb_f = ar.bf16(NHC * TC)
        kkb = view3(kkb_f, NHC, TC)
        rbuf = ar.f32(8 * TC)
        rbuf3 = view3(rbuf, 8, TC)

        t_z2fm0 = em.op("vector", lambda h: h.memset(z2fm3[:, :, 0], 0.0), tick=True)
        t2 = {"evac": None, "mix": None, "kk": [None] * 8, "x1dma": [None, None],
              "var": None, "kkdma": None, "xrdma": None}
        for j in range(NCH):
            pj = j % 2
            x1a = [x1in3[:, pj * 2 + tt, :] for tt in range(2)]
            srcx1 = x1_h[j * TC:(j + 1) * TC, :].rearrange("(tt p) c -> p tt c", p=P)
            wl = [t2["mix"]] if j >= 2 and t2["mix"] else []
            tok_x1 = em.dma("sync", x1in3[:, pj * 2:pj * 2 + 2, :], srcx1, waits=wl)
            for tt in range(2):
                em.op("vector", lambda h, tt=tt, x1a=x1a: h.reduce_sum(
                    statb[:, 0 + tt:1 + tt], x1a[tt], axis=AX.X),
                    waits=[tok_x1] if tt == 0 else [], tick=(tt == 1))
            t_sum = ("vector", em.ticks["vector"])
            sq_wl = [tok_x1] + ([t2["var"]] if t2["var"] else [])
            for tt in range(2):
                em.op("scalar", lambda h, tt=tt, x1a=x1a: h.activation(
                    out=z2tm3[:, tt, :], in_=x1a[tt], func=AF.Square,
                    accum_out=statb[:, 2 + tt:3 + tt]),
                    waits=sq_wl if tt == 0 else [], tick=(tt == 1))
            t_sq = ("scalar", em.ticks["scalar"])
            em.op("vector", lambda h: h.tensor_scalar(
                statb[:, 4:6], statb[:, 2:4], 1.0 / C, 1e-5, ALU.mult, ALU.add),
                waits=[t_sq], fence=True)
            t_m = em.op("vector", lambda h: h.tensor_scalar_mul(
                statb[:, 6:8], statb[:, 0:2], 1.0 / C), waits=[t_sum],
                fence=True, tick=True)
            em.op("scalar", lambda h: h.activation(
                out=statb[:, 8:10], in_=statb[:, 6:8], func=AF.Square),
                waits=[t_m], fence=True, tick=True)
            t_m2s = ("scalar", em.ticks["scalar"])
            t2["var"] = t_var = em.op("vector", lambda h: h.tensor_tensor(
                statb[:, 10:12], statb[:, 4:6], statb[:, 8:10], ALU.subtract),
                waits=[t_m2s], tick=True)
            em.op("scalar", lambda h: h.activation(
                out=statb[:, 14:16], in_=statb[:, 10:12], func=AF.Ln),
                waits=[t_var], fence=True)
            t_rstd = em.op("scalar", lambda h: h.activation(
                out=statb[:, 12:14], in_=statb[:, 14:16], func=AF.Exp, scale=-0.5),
                tick=True)
            for tt in range(2):
                em.op("vector", lambda h, tt=tt, x1a=x1a: h.tensor_scalar(
                    z2tm3[:, tt, :], x1a[tt], statb[:, 6 + tt:7 + tt],
                    statb[:, 12 + tt:13 + tt], ALU.subtract, ALU.mult),
                    waits=[t_rstd], tick=(tt == 1))
            t_apply = ("vector", em.ticks["vector"])
            if j > 0:
                em.op("vector", lambda h: h.tensor_copy(
                    z2fm3[:, :, 0], z2fm3[:, :, 256]), waits=[t2["evac"]],
                    fence=True, tick=True)
            else:
                em.op("vector", lambda h: h.nop(), waits=[t_z2fm0], tick=True)
            t_bnd = ("vector", em.ticks["vector"])
            twl = [t_apply, t_bnd]
            if t2["kk"][7]:
                twl.append(t2["kk"][7])
            if j == 0:
                twl.append(tok_ident)
            first = True
            for bank in range(4):
                for cc in (bank, bank + 4):
                    for tt in range(2):
                        em.op("tensor", lambda h, cc=cc, tt=tt: h.transpose(
                            pb[cc % 4][:, (cc // 4) * 256 + tt * 128:
                                       (cc // 4) * 256 + tt * 128 + 128],
                            z2tm3[:, tt, cc * 128:(cc + 1) * 128], ident_ap),
                            waits=twl if first else [],
                            tick=(bank == 3 and cc == 7 and tt == 1))
                        first = False
            t_tr = ("tensor", em.ticks["tensor"])
            evz_wl = [t_tr] + ([tok_vecs] if j == 0 else [])
            for cc in range(NCC):
                em.op("scalar", lambda h, cc=cc: h.activation(
                    out=z2fm3[:, cc, 1:257],
                    in_=pb[cc % 4][:, (cc // 4) * 256:(cc // 4) * 256 + 256],
                    func=AF.Identity, scale=vcol("g2", cc), bias=vcol("b2", cc)),
                    waits=evz_wl if cc == 0 else [], tick=(cc == NCC - 1))
            t2["evac"] = t_evz = ("scalar", em.ticks["scalar"])
            dwl = [t_evz] + ([t2["xrdma"]] if t2["xrdma"] else [])
            em.op("vector", lambda h: h.tensor_tensor(
                dbuf23, z2fm3[:, :, 1:257], z2fm3[:, :, 0:256], ALU.subtract),
                waits=dwl, tick=True)
            t_d = ("vector", em.ticks["vector"])
            for cc in range(NCC):
                em.op("vector", lambda h, cc=cc: h.scalar_tensor_tensor(
                    xk2b[:, cc, :], dbuf23[:, cc, :], vcol("ftmk", cc),
                    z2fm3[:, cc, 0:256], ALU.mult, ALU.add),
                    waits=[t_d] if cc == 0 else [])
                em.op("vector", lambda h, cc=cc: h.scalar_tensor_tensor(
                    xr2b[:, cc, :], dbuf23[:, cc, :], vcol("ftmr", cc),
                    z2fm3[:, cc, 0:256], ALU.mult, ALU.add),
                    tick=(cc == NCC - 1))
            t2["mix"] = t_mix = ("vector", em.ticks["vector"])
            wl = [t_mix] + ([t2["xrdma"]] if t2["xrdma"] else [])
            t2["xrdma"] = em.dma("scalar", xr2_h[j, :, :], xr2b_f, waits=[t_mix])
            for hg in range(8):
                first = True
                bb = (hg % 2) * 2
                for hs in range(4):
                    hc = hg * 4 + hs
                    mwl = [t_mix, tok_fwk]
                    if t2["kk"][(hg - 2) % 8]:
                        mwl.append(t2["kk"][(hg - 2) % 8])
                    if hg <= 1:
                        mwl.append(t_evz)
                    for cc in range(NCC):
                        em.op("tensor", lambda h, hc=hc, hs=hs, cc=cc, bb=bb: h.matmul(
                            pb[bb + hs // 2][:, (hs % 2) * 256:(hs % 2) * 256 + 256],
                            fwk[:, cc, hc * 128:(hc + 1) * 128],
                            xk2b[:, cc, :], start=(cc == 0), stop=(cc == NCC - 1)),
                            waits=mwl if first else [],
                            tick=(hs == 3 and cc == NCC - 1))
                        first = False
                t_fk = ("tensor", em.ticks["tensor"])
                for hs in range(4):
                    em.op("scalar", lambda h, hs=hs, hg=hg, bb=bb: h.activation(
                        out=rbuf3[:, (hg % 2) * 4 + hs, :],
                        in_=pb[bb + hs // 2][:, (hs % 2) * 256:(hs % 2) * 256 + 256],
                        func=AF.Relu), waits=[t_fk] if hs == 0 else [],
                        tick=(hs == 3))
                t_rl = ("scalar", em.ticks["scalar"])
                kwl2 = [t_rl] + ([t2["kkdma"]] if (hg == 0 and t2["kkdma"]) else [])
                for hs in range(4):
                    hc = hg * 4 + hs
                    em.op("vector", lambda h, hc=hc, hs=hs, hg=hg, bb=bb: h.tensor_tensor(
                        kkb[:, hc, :],
                        pb[bb + hs // 2][:, (hs % 2) * 256:(hs % 2) * 256 + 256],
                        rbuf3[:, (hg % 2) * 4 + hs, :], ALU.mult),
                        waits=kwl2 if hs == 0 else [], tick=(hs == 3))
                t2["kk"][hg % 8] = ("vector", em.ticks["vector"])
            wl = [t2["kk"][7]] + ([t2["kkdma"]] if t2["kkdma"] else [])
            t2["kkdma"] = em.dma("sync", kk_h[j, :, :], kkb_f, waits=wl)

    if "s2b" in stages:
        em.stage_barrier()
        ar.reset(); ar.mark()
        fwv = ar.bf16(NHC * C, shape=[NHC, C])
        tok_fwv = em.dma("gpsimd", fwv, d["fWvT"].rearrange("(hc p) co -> p hc co", p=P))
        fwr = ar.bf16(NCC * C, shape=[NCC, C])
        tok_fwr = em.dma("gpsimd", fwr, d["fWrT"].rearrange("(cc p) co -> p cc co", p=P))
        kkin_f = [ar.bf16(NHC * TC) for _ in range(2)]
        kkin = [view3(b, NHC, TC) for b in kkin_f]
        xrin_f = [ar.bf16(NCC * TC) for _ in range(2)]
        xrin = [view3(b, NCC, TC) for b in xrin_f]
        xin2 = ar.f32(2048); xin23 = view3(xin2, 2, 1024)
        x1in2 = ar.f32(2048); x1in23 = view3(x1in2, 2, 1024)
        sr2buf = ar.f32(2048); sr2b3 = view3(sr2buf, 2, 1024)
        rkvbuf = ar.f32(2048); rkvb3 = view3(rkvbuf, 2, 1024)
        accbuf = ar.f32(2048); accb3 = view3(accbuf, 2, 1024)
        xAbuf = ar.f32(2 * 2048); xAb3 = view3(xAbuf, 4, 1024)

        t3 = {"r2": None, "kv": None, "sr2": None, "rkv": None, "var": None,
              "xAdma": [None, None], "acc": None, "apply": None}
        for j in range(NCH):
            pj = j % 2
            kwl = [t3["kv"]] if t3["kv"] else []
            tok_kk = em.dma("sync", kkin_f[pj], kk_h[j, :, :], waits=kwl)
            rwl = [t3["r2"]] if t3["r2"] else []
            tok_xr = em.dma("scalar", xrin_f[pj], xr2_h[j, :, :], waits=rwl)
            awl = [t3["acc"]] if t3["acc"] else []
            tok_x = em.dma("sync", xin23,
                           x_h[j * TC:(j + 1) * TC, :].rearrange("(tt p) c -> p tt c", p=P),
                           waits=awl)
            tok_x1 = em.dma("scalar", x1in23,
                            x1_h[j * TC:(j + 1) * TC, :].rearrange("(tt p) c -> p tt c", p=P),
                            waits=awl)
            rwl = [tok_xr, tok_fwr]
            if t3["sr2"]:
                rwl.append(t3["sr2"])
            first = True
            for tt in range(2):
                for nh in range(2):
                    for cc in range(NCC):
                        em.op("tensor", lambda h, tt=tt, nh=nh, cc=cc, xr_=xrin[pj]: h.matmul(
                            pbank[2 * tt + nh][:, :],
                            xr_[:, cc, tt * 128:tt * 128 + 128],
                            fwr[:, cc, nh * 512:nh * 512 + 512],
                            start=(cc == 0), stop=(cc == NCC - 1)),
                            waits=rwl if first else [],
                            tick=(tt == 1 and nh == 1 and cc == NCC - 1))
                        first = False
            t_r2 = ("tensor", em.ticks["tensor"])
            t3["r2"] = t_r2
            kvwl = [tok_kk, tok_fwv]
            if t3["rkv"]:
                kvwl.append(t3["rkv"])
            first = True
            for tt in range(2):
                for nh in range(2):
                    for hc in range(NHC):
                        em.op("tensor", lambda h, tt=tt, nh=nh, hc=hc, kk_=kkin[pj]: h.matmul(
                            pbank[4 + 2 * tt + nh][:, :],
                            kk_[:, hc, tt * 128:tt * 128 + 128],
                            fwv[:, hc, nh * 512:nh * 512 + 512],
                            start=(hc == 0), stop=(hc == NHC - 1)),
                            waits=kvwl if first else [],
                            tick=(tt == 1 and nh == 1 and hc == NHC - 1))
                        first = False
            t_kv = ("tensor", em.ticks["tensor"])
            t3["kv"] = t_kv
            for tt in range(2):
                for nh in range(2):
                    em.op("scalar", lambda h, tt=tt, nh=nh: h.activation(
                        out=sr2b3[:, tt, nh * 512:nh * 512 + 512],
                        in_=pbank[2 * tt + nh][:, :], func=AF.Sigmoid),
                        waits=[t_r2] if (tt == 0 and nh == 0) else [],
                        tick=(tt == 1 and nh == 1))
            t3["sr2"] = t_sr2 = ("scalar", em.ticks["scalar"])
            for tt in range(2):
                for nh in range(2):
                    em.op("vector", lambda h, tt=tt, nh=nh: h.tensor_tensor(
                        rkvb3[:, tt, nh * 512:nh * 512 + 512],
                        sr2b3[:, tt, nh * 512:nh * 512 + 512],
                        pbank[4 + 2 * tt + nh][:, :], ALU.mult),
                        waits=[t_sr2, t_kv] if (tt == 0 and nh == 0) else [],
                        tick=(tt == 1 and nh == 1))
            t3["rkv"] = ("vector", em.ticks["vector"])
            for tt in range(2):
                em.op("vector", lambda h, tt=tt: h.tensor_tensor(
                    accb3[:, tt, :], xin23[:, tt, :], x1in23[:, tt, :], ALU.add),
                    waits=[tok_x, tok_x1] if tt == 0 else [])
            sq_wl2 = [t3["var"]] if t3["var"] else []
            for tt in range(2):
                em.op("vector", lambda h, tt=tt: h.scalar_tensor_tensor(
                    accb3[:, tt, :], rkvb3[:, tt, :], 1.0, accb3[:, tt, :],
                    ALU.mult, ALU.add, accum_out=statb[:, 0 + tt:1 + tt]),
                    waits=sq_wl2 if tt == 0 else [], tick=(tt == 1))
            t3["acc"] = t_acc = ("vector", em.ticks["vector"])
            for tt in range(2):
                em.op("scalar", lambda h, tt=tt: h.activation(
                    out=rkvb3[:, tt, :], in_=accb3[:, tt, :], func=AF.Square,
                    accum_out=statb[:, 2 + tt:3 + tt]),
                    waits=[t_acc] if tt == 0 else [], tick=(tt == 1))
            t_sq = ("scalar", em.ticks["scalar"])
            em.op("vector", lambda h: h.tensor_scalar(
                statb[:, 4:6], statb[:, 2:4], 1.0 / C, 1e-5, ALU.mult, ALU.add),
                waits=[t_sq], fence=True)
            t_m = em.op("vector", lambda h: h.tensor_scalar_mul(
                statb[:, 6:8], statb[:, 0:2], 1.0 / C), fence=True, tick=True)
            em.op("scalar", lambda h: h.activation(
                out=statb[:, 8:10], in_=statb[:, 6:8], func=AF.Square),
                waits=[t_m], fence=True, tick=True)
            t_m2s = ("scalar", em.ticks["scalar"])
            t3["var"] = t_var = em.op("vector", lambda h: h.tensor_tensor(
                statb[:, 10:12], statb[:, 4:6], statb[:, 8:10], ALU.subtract),
                waits=[t_m2s], tick=True)
            em.op("scalar", lambda h: h.activation(
                out=statb[:, 14:16], in_=statb[:, 10:12], func=AF.Ln),
                waits=[t_var], fence=True)
            t_rstd = em.op("scalar", lambda h: h.activation(
                out=statb[:, 12:14], in_=statb[:, 14:16], func=AF.Exp, scale=-0.5),
                tick=True)
            xa_wl = [t_rstd] + ([t3["xAdma"][pj]] if t3["xAdma"][pj] else [])
            for tt in range(2):
                em.op("vector", lambda h, tt=tt, pj=pj: h.tensor_scalar(
                    xAb3[:, pj * 2 + tt, :], accb3[:, tt, :], statb[:, 6 + tt:7 + tt],
                    statb[:, 12 + tt:13 + tt], ALU.subtract, ALU.mult),
                    waits=xa_wl if tt == 0 else [], tick=(tt == 1))
            t3["apply"] = t_ap = ("vector", em.ticks["vector"])
            dst = xA_h[j * TC:(j + 1) * TC, :].rearrange("(tt p) c -> p tt c", p=P)
            t3["xAdma"][pj] = em.dma("scalar", dst, xAb3[:, pj * 2:pj * 2 + 2, :],
                                     waits=[t_ap])


    if "s3a" in stages or "s3b" in stages:
        em.stage_barrier()
        ar.reset(); ar.mark()
        w1q = ar.bf16(NCC * HID, shape=[NCC, HID])
        hbuf = ar.f32(NHC * 258 + 2)
        hbuf3 = view3(hbuf[:, :NHC * 258], NHC, 258)
        cbuf = ar.f32(NHC * 256)
        cbuf3 = view3(cbuf, NHC, 256)
        stg = [hbuf[:, 0:4096], hbuf[:, 4096:8192]]
        abscols = cbuf[:, 0:40]
        tot = cbuf[:, 40:48]
        row1 = cbuf[0:1, 48:56]
        swc = scl

        def prep_weight(tag, src_ap, ntile, tile_n, qdst, store_dma,
                        prev_done=None):
            abst = []
            toks_ld = []
            for i in range(ntile):
                wl = [abst[i - 2]] if i >= 2 else (
                    [prev_done] if prev_done else [])
                toks_ld.append(em.dma("sync", stg[i % 2][:, 0:tile_n],
                                      src_ap[:, i, :], waits=wl))
                em.op("scalar", lambda h, i=i: h.activation(
                    out=absdst[:, 0:tile_n], in_=stg[i % 2][:, 0:tile_n],
                    func=AF.Abs, accum_out=abscols[:, i:i + 1]),
                    waits=[toks_ld[i]], tick=True)
                abst.append(("scalar", em.ticks["scalar"]))
            t_abs = abst[-1]
            em.op("vector", lambda h: h.reduce_sum(
                tot[:, 0:1], abscols[:, 0:ntile], axis=AX.X), waits=[t_abs],
                fence=True, tick=True)
            t_tot = ("vector", em.ticks["vector"])
            em.op("tensor", lambda h: h.transpose(
                pbank[0][0:1, 0:128], tot[:, 0:1], ident_ap),
                waits=[t_tot, tok_ident], tick=True)
            t_tr = ("tensor", em.ticks["tensor"])
            em.op("vector", lambda h: h.reduce_sum(
                row1[:, 0:1], pbank[0][0:1, 0:128], axis=AX.X), waits=[t_tr],
                fence=True)
            em.op("vector", lambda h: h.tensor_scalar(
                row1[:, 1:2], row1[:, 0:1], 1.0 / (ntile * tile_n * 128), 1e-5,
                ALU.mult, ALU.max), fence=True)
            em.op("vector", lambda h: h.reciprocal(
                row1[:, 2:3], row1[:, 1:2]), fence=True, tick=True)
            t_sw = ("vector", em.ticks["vector"])
            for (srccol, dstcol) in ((2, 0), (1, 1)):
                em.op("tensor", lambda h, srccol=srccol, dstcol=dstcol: h.matmul(
                    pbank[1][:, dstcol:dstcol + 1], onesb[0:1, 0:128],
                    row1[:, srccol:srccol + 1], start=True, stop=True),
                    waits=[t_sw, tok_ones], tick=True)
            t_bc = ("tensor", em.ticks["tensor"])
            base = 0 if tag == "w1" else 2
            em.op("vector", lambda h, base=base: h.tensor_copy(
                swc[:, base:base + 2], pbank[1][:, 0:2]), waits=[t_bc],
                fence=True, tick=True)
            t_swc = ("vector", em.ticks["vector"])
            toks_ld2 = []
            qt = []
            t_q = None
            for i in range(ntile):
                wl = [qt[i - 2]] if i >= 2 else [t_abs]
                toks_ld2.append(em.dma("sync", stg[i % 2][:, 0:tile_n],
                                       src_ap[:, i, :], waits=wl))
                qwl = [toks_ld2[i], t_swc]
                if store_dma is not None and i >= 2:
                    qwl.append(stok[i % 2])
                em.op("vector", lambda h, i=i, base=base: h.tensor_scalar(
                    stg[i % 2][:, 0:tile_n], stg[i % 2][:, 0:tile_n],
                    swc[:, base:base + 1], MAGIC, ALU.mult, ALU.add),
                    waits=qwl)
                em.op("vector", lambda h, i=i: h.tensor_scalar(
                    stg[i % 2][:, 0:tile_n], stg[i % 2][:, 0:tile_n],
                    MAGIC, -1.0, ALU.subtract, ALU.max))
                em.op("vector", lambda h, i=i, qd=qdst(i): h.tensor_scalar_min(
                    qd, stg[i % 2][:, 0:tile_n], 1.0), tick=True)
                qt.append(("vector", em.ticks["vector"]))
                t_q = qt[-1]
                if store_dma is not None:
                    stok[i % 2] = store_dma(i, t_q)
            return t_q

        stok = [None, None]
        def w2_store(i, tok):
            return em.dma("scalar", w2q_h[:, i * 1024:(i + 1) * 1024],
                          w2qstg[i % 2], waits=[tok])
        w2qstg = [ar.bf16(1024) for _ in range(2)]
        absdst = cbuf[:, 1024:5120]

        t_q1 = prep_weight("w1", d["w1T"].rearrange("(cc p) ho -> p cc ho", p=P),
                           NCC, HID, lambda i: w1q[:, i, :], None)
        t_q2 = prep_weight("w2", d["w2T"].rearrange("(hc p) co -> p hc co", p=P),
                           NHC, C, lambda i: w2qstg[i % 2], w2_store,
                           prev_done=t_q1)

    if "s3a" in stages:
        em.stage_barrier()
        xAin = ar.f32(2 * 2048)
        xAin3 = view3(xAin, 4, 1024)
        xqtm_f = ar.bf16(2048)
        xqtm = view3(xqtm_f, 2, 1024)
        xqfm = [ar.bf16(NCC * 260, shape=[NCC, 260]) for _ in range(2)]
        gqb_f = ar.bf16(NHC * 256)
        gqb = view3(gqb_f, NHC, 256)
        s1row = ar.f32(4104)
        s2row = ar.f32(256)
        hscale = ar.f32(264)
        b2scale = ar.f32(256)
        xqscr = ar.f32(1024)

        em.op("vector", lambda h: h.memset(s1row[0:1, :], 0.0))
        em.op("vector", lambda h: h.memset(xqfm[0][:, :, 0], 0.0))
        t_init3 = em.op("vector", lambda h: h.memset(xqfm[1][:, :, 0], 0.0),
                        tick=True)

        t4 = {"qdone": [None, None], "hburst": [None] * 8, "deq": [None] * 16,
              "gqdma": None, "evacs": [None, None], "hlast": None,
              "convdone": None, "i2dma": None, "si": None}

        def emit_quant(q):
            pq = q % 2
            wl = [t4["qdone"][pq]] if t4["qdone"][pq] else []
            tok_xa = em.dma("sync", xAin3[:, pq * 2:pq * 2 + 2, :],
                            xA_h[q * TC:(q + 1) * TC, :].rearrange(
                                "(tt p) c -> p tt c", p=P), waits=wl)
            for tt in range(2):
                em.op("vector", lambda h, tt=tt, pq=pq: h.reduce_max(
                    scl[:, 8 + tt:9 + tt], xAin3[:, pq * 2 + tt, :], axis=AX.X,
                    apply_absolute_value=True),
                    waits=[tok_xa] if tt == 0 else [], fence=(tt == 1))
            em.op("vector", lambda h: h.tensor_scalar_max(
                scl[:, 10:12], scl[:, 8:10], 1e-5), fence=True)
            em.op("vector", lambda h: h.reciprocal(
                scl[:, 12:14], scl[:, 10:12]), fence=True)
            em.op("vector", lambda h: h.tensor_scalar_mul(
                scl[:, 14:16], scl[:, 12:14], 127.0), fence=True)
            t_sc = em.op("vector", lambda h: h.tensor_scalar(
                scl[:, 16:18], scl[:, 10:12], 1.0 / 127, swc[:, 1:2],
                ALU.mult, ALU.mult), fence=True, tick=True)
            for tt in range(2):
                em.op("vector", lambda h, tt=tt, pq=pq: h.tensor_scalar(
                    xqscr, xAin3[:, pq * 2 + tt, :], scl[:, 14 + tt:15 + tt],
                    MAGIC, ALU.mult, ALU.add), waits=[t_sc] if tt == 0 else [])
                em.op("vector", lambda h, tt=tt: h.tensor_scalar_sub(
                    xqtm[:, tt, :], xqscr, MAGIC), tick=(tt == 1))
            t_xq = ("vector", em.ticks["vector"])
            stwl = [t_sc] + ([t4["qdone"][1 - pq]] if t4["qdone"][1 - pq] else [])
            for tt in range(2):
                em.op("tensor", lambda h, tt=tt: h.transpose(
                    pbank[4][0:1, tt * 128:tt * 128 + 128],
                    scl[:, 16 + tt:17 + tt], ident_ap),
                    waits=stwl if tt == 0 else [], tick=(tt == 1))
            t_rt = ("tensor", em.ticks["tensor"])
            em.op("vector", lambda h, q=q: h.tensor_copy(
                s1row[0:1, q * TC + 1:q * TC + 257], pbank[4][0:1, 0:256]),
                waits=[t_rt], fence=True, tick=True)
            t_s1r = ("vector", em.ticks["vector"])
            twl = [t_xq]
            if t4["hlast"]:
                twl.append(t4["hlast"])
            if t4["deq"][15]:
                twl.append(t4["deq"][15])
            first = True
            for bank in range(4):
                for cc in (bank, bank + 4):
                    for tt in range(2):
                        em.op("tensor", lambda h, cc=cc, tt=tt: h.transpose(
                            pb[cc % 4].bitcast(BF16)[:, (cc // 4) * 512 + tt * 128:
                                                     (cc // 4) * 512 + tt * 128 + 128],
                            xqtm[:, tt, cc * 128:(cc + 1) * 128], identb16),
                            waits=(twl + [tok_ident16]) if first else [],
                            tick=(bank == 3 and cc == 7 and tt == 1))
                        first = False
            t_tr = ("tensor", em.ticks["tensor"])
            ewl = [t_tr]
            if t4["hburst"][7]:
                ewl.append(t4["hburst"][7])
            for cc in range(NCC):
                em.op("scalar", lambda h, cc=cc, pq=pq: h.activation(
                    out=xqfm[pq][:, cc, 1:257],
                    in_=pb[cc % 4].bitcast(BF16)[:, (cc // 4) * 512:
                                                 (cc // 4) * 512 + 256],
                    func=AF.Identity), waits=ewl if cc == 0 else [],
                    tick=(cc == NCC - 1))
            t4["evacs"][pq] = ("scalar", em.ticks["scalar"])
            t4["qdone"][pq] = t_s1r
            return t4["evacs"][pq]

        emit_quant(0)
        for j in range(NCH):
            pj = j % 2
            if j < NCH - 1:
                emit_quant(j + 1)
                em.op("vector", lambda h, pj=pj: h.tensor_copy(
                    xqfm[1 - pj][:, :, 0], xqfm[pj][:, :, 256]),
                    waits=[t4["evacs"][pj], t4["evacs"][1 - pj]], fence=True)
                em.op("vector", lambda h, pj=pj: h.tensor_copy(
                    xqfm[pj][:, :, 257], xqfm[1 - pj][:, :, 1]),
                    fence=True, tick=True)
            else:
                em.op("vector", lambda h, pj=pj: h.memset(
                    xqfm[pj][:, :, 257], 0.0),
                    waits=[t4["evacs"][pj]], fence=True, tick=True)
            t_bnd = ("vector", em.ticks["vector"])
            em.op("tensor", lambda h, j=j: h.matmul(
                pbank[5][:, 0:258], onesb[0:1, 0:128],
                s1row[0:1, j * TC:j * TC + 258], start=True, stop=True),
                waits=[t4["qdone"][pj], tok_ones] +
                      ([t4["b2"]] if t4.get("b2") else []), tick=True)
            t_hb = ("tensor", em.ticks["tensor"])
            em.op("vector", lambda h: h.tensor_copy(
                hscale[:, 0:258], pbank[5][:, 0:258]), waits=[t_hb], tick=True)
            t_hs = ("vector", em.ticks["vector"])
            for hb in range(16):
                first = True
                bb = (hb % 2) * 2
                for hs in range(2):
                    hc = hb * 2 + hs
                    mwl = [t_bnd, t_hs, t_q1]
                    if t4["deq"][(hb - 2) % 16]:
                        mwl.append(t4["deq"][(hb - 2) % 16])
                    if hb <= 1:
                        mwl.append(t4["evacs"][1 - pj] if j < NCH - 1 else
                                   t4["evacs"][pj])
                    for cc in range(NCC):
                        em.op("tensor",
                              lambda h, hc=hc, hs=hs, cc=cc, bb=bb, sl=xqfm[pj]: h.matmul(
                                  pbank[bb + hs][:, 0:258],
                                  w1q[:, cc, hc * 128:(hc + 1) * 128],
                                  sl[:, cc, 0:258],
                                  start=(cc == 0), stop=(cc == NCC - 1)),
                              waits=mwl if first else [],
                              tick=(hs == 1 and cc == NCC - 1))
                        first = False
                t4["hburst"][hb % 8] = t_hm = ("tensor", em.ticks["tensor"])
                if hb == 15:
                    t4["hlast"] = t_hm
                for hs in range(2):
                    hc = hb * 2 + hs
                    em.op("vector", lambda h, hc=hc, hs=hs, bb=bb: h.tensor_tensor(
                        hbuf3[:, hc, :], pbank[bb + hs][:, 0:258], hscale[:, 0:258],
                        ALU.mult), waits=[t_hm] if hs == 0 else [],
                        tick=(hs == 1))
                t4["deq"][(hb) % 16] = ("vector", em.ticks["vector"])
            cwl = [t4["deq"][15]] + ([t4["si"]] if t4["si"] else [])
            for hc in range(NHC):
                em.op("vector", lambda h, hc=hc: h.tensor_scalar_mul(
                    cbuf3[:, hc, :], hbuf3[:, hc, 0:256],
                    convwb[:, hc * 3:hc * 3 + 1]),
                    waits=cwl if hc == 0 else [])
                em.op("vector", lambda h, hc=hc: h.scalar_tensor_tensor(
                    cbuf3[:, hc, :], hbuf3[:, hc, 1:257],
                    convwb[:, hc * 3 + 1:hc * 3 + 2], cbuf3[:, hc, :],
                    ALU.mult, ALU.add))
                em.op("vector", lambda h, hc=hc: h.scalar_tensor_tensor(
                    cbuf3[:, hc, :], hbuf3[:, hc, 2:258],
                    convwb[:, hc * 3 + 2:hc * 3 + 3], cbuf3[:, hc, :],
                    ALU.mult, ALU.add), tick=(hc == NHC - 1))
            t4["convdone"] = t_cv = ("vector", em.ticks["vector"])
            for hc in range(NHC):
                em.op("scalar", lambda h, hc=hc: h.activation(
                    out=hbuf3[:, hc, 0:256], in_=cbuf3[:, hc, :], func=AF.Silu,
                    bias=convbb[:, hc:hc + 1]), waits=[t_cv] if hc == 0 else [],
                    tick=(hc == NHC - 1))
            t4["si"] = t_si = ("scalar", em.ticks["scalar"])
            em.op("vector", lambda h: h.reduce_max(
                cbuf3[:, 30, :], hbuf3[:, :, 0:256].transpose([0, 2, 1]),
                axis=AX.X, apply_absolute_value=True), waits=[t_si],
                fence=True, tick=True)
            t_tm = ("vector", em.ticks["vector"])
            for tt in range(2):
                em.op("tensor", lambda h, tt=tt: h.transpose(
                    pbank[6][:, tt * 128:tt * 128 + 128],
                    cbuf3[:, 30, tt * 128:tt * 128 + 128], ident_ap),
                    waits=[t_tm] if tt == 0 else [], tick=(tt == 1))
            t_tt = ("tensor", em.ticks["tensor"])
            for tt in range(2):
                em.op("vector", lambda h, tt=tt: h.reduce_max(
                    scl[:, 20 + tt:21 + tt], pbank[6][:, tt * 128:tt * 128 + 128],
                    axis=AX.X), waits=[t_tt] if tt == 0 else [], fence=(tt == 1))
            em.op("vector", lambda h: h.tensor_scalar_max(
                scl[:, 22:24], scl[:, 20:22], 1e-5), fence=True)
            em.op("vector", lambda h: h.reciprocal(
                scl[:, 24:26], scl[:, 22:24]), fence=True)
            em.op("vector", lambda h: h.tensor_scalar_mul(
                scl[:, 26:28], scl[:, 24:26], 127.0), fence=True)
            t_i2 = em.op("vector", lambda h: h.tensor_scalar(
                scl[:, 28:30], scl[:, 22:24], 1.0 / 127, swc[:, 3:4],
                ALU.mult, ALU.mult), fence=True, tick=True)
            i2wl = [t_i2] + ([t4["i2dma"]] if t4["i2dma"] else [])
            t4["i2dma"] = em.dma("scalar",
                                 inv2_h[2 * j:2 * j + 2, :].rearrange("t p -> p t"),
                                 scl[:, 28:30], waits=i2wl, slow=True)
            for tt in range(2):
                em.op("tensor", lambda h, tt=tt: h.transpose(
                    pbank[7][0:1, tt * 128:tt * 128 + 128],
                    scl[:, 26 + tt:27 + tt], ident_ap),
                    waits=[t_i2] if tt == 0 else [], tick=(tt == 1))
            t_s2t = ("tensor", em.ticks["tensor"])
            em.op("vector", lambda h: h.tensor_copy(
                s2row[0:1, 0:256], pbank[7][0:1, 0:256]), waits=[t_s2t],
                fence=True, tick=True)
            t_s2r = ("vector", em.ticks["vector"])
            em.op("tensor", lambda h: h.matmul(
                pbank[7][:, 256:512], onesb[0:1, 0:128],
                s2row[0:1, 0:256], start=True, stop=True), waits=[t_s2r],
                tick=True)
            t_b2m = ("tensor", em.ticks["tensor"])
            em.op("vector", lambda h: h.tensor_copy(
                b2scale, pbank[7][:, 256:512]), waits=[t_b2m], tick=True)
            t_b2 = ("vector", em.ticks["vector"])
            t4["b2"] = t_b2
            gwl = [t_b2] + ([t4["gqdma"]] if t4["gqdma"] else [])
            for hc in range(NHC):
                em.op("vector", lambda h, hc=hc: h.tensor_tensor(
                    cbuf3[:, hc, :], hbuf3[:, hc, 0:256], b2scale, ALU.mult),
                    waits=gwl if hc == 0 else [], tick=(hc == NHC - 1))
            t_gm = ("vector", em.ticks["vector"])
            em.op("vector", lambda h: h.tensor_scalar(
                gqb_f, cbuf, MAGIC, MAGIC, ALU.add, ALU.subtract),
                waits=[t_gm], tick=True)
            t_gq = ("vector", em.ticks["vector"])
            t4["gqdma"] = em.dma("sync", gq_h[j, :, :], gqb_f, waits=[t_gq])

    if "s3b" in stages:
        em.stage_barrier()
        ar.reset(); ar.mark()
        w2qb = ar.bf16(NHC * C, shape=[NHC, C])
        tok_w2 = em.dma("gpsimd", w2qb, w2q_h[:, :].rearrange(
            "p (hc co) -> p hc co", hc=NHC))
        gqin_f = [ar.bf16(NHC * TC) for _ in range(2)]
        gqin = [view3(b, NHC, TC) for b in gqin_f]
        xAin2 = ar.f32(2048); xAin23 = view3(xAin2, 2, 1024)
        acc5b = ar.f32(2048); acc5b3 = view3(acc5b, 2, 1024)
        sqscr = ar.f32(2048); sqscr3 = view3(sqscr, 2, 1024)
        outb = ar.f32(2 * 2048); outb3 = view3(outb, 4, 1024)

        t5 = {"mm": None, "acc": None, "var": None, "odma": [None, None],
              "ap": None}
        for j in range(NCH):
            pj = j % 2
            gwl = [t5["mm"]] if t5["mm"] else []
            tok_gq = em.dma("sync", gqin_f[pj], gq_h[j, :, :], waits=gwl)
            awl = [t5["acc"]] if t5["acc"] else []
            tok_xa = em.dma("scalar", xAin23,
                            xA_h[j * TC:(j + 1) * TC, :].rearrange(
                                "(tt p) c -> p tt c", p=P), waits=awl)
            iwl = [t5["acc"]] if t5["acc"] else []
            tok_i2 = em.dma("scalar", scl[:, 0:2],
                            inv2_h[2 * j:2 * j + 2, :].rearrange("t p -> p t"),
                            waits=iwl, slow=True)
            first = True
            for tt in range(2):
                for nh in range(2):
                    for hc in range(NHC):
                        em.op("tensor",
                              lambda h, tt=tt, nh=nh, hc=hc, gq_=gqin[pj]: h.matmul(
                                  pbank[4 + 2 * tt + nh][:, :],
                                  gq_[:, hc, tt * 128:tt * 128 + 128],
                                  w2qb[:, hc, nh * 512:nh * 512 + 512],
                                  start=(hc == 0), stop=(hc == NHC - 1)),
                              waits=([tok_gq, tok_w2] +
                                     ([t5["acc"]] if t5["acc"] else []))
                              if first else [],
                              tick=(tt == 1 and nh == 1 and hc == NHC - 1))
                        first = False
            t5["mm"] = t_mm = ("tensor", em.ticks["tensor"])
            for tt in range(2):
                for nh in range(2):
                    em.op("vector", lambda h, tt=tt, nh=nh: h.scalar_tensor_tensor(
                        acc5b3[:, tt, nh * 512:nh * 512 + 512],
                        pbank[4 + 2 * tt + nh][:, :], scl[:, 0 + tt:1 + tt],
                        xAin23[:, tt, nh * 512:nh * 512 + 512],
                        ALU.mult, ALU.add,
                        accum_out=scl[:, 4 + 2 * tt + nh:5 + 2 * tt + nh]),
                        waits=[t_mm, tok_xa, tok_i2] if (tt == 0 and nh == 0)
                        else [], tick=(tt == 1 and nh == 1))
            t5["acc"] = t_acc = ("vector", em.ticks["vector"])
            em.op("vector", lambda h: h.tensor_tensor(
                statb[:, 0:1], scl[:, 4:5], scl[:, 5:6], ALU.add), waits=[t_acc],
                fence=True)
            em.op("vector", lambda h: h.tensor_tensor(
                statb[:, 1:2], scl[:, 6:7], scl[:, 7:8], ALU.add), fence=True,
                tick=True)
            t_sum = ("vector", em.ticks["vector"])
            sq_wl3 = [t_acc] + ([t5["var"]] if t5["var"] else [])
            for tt in range(2):
                em.op("scalar", lambda h, tt=tt: h.activation(
                    out=sqscr3[:, tt, :], in_=acc5b3[:, tt, :], func=AF.Square,
                    accum_out=statb[:, 2 + tt:3 + tt]),
                    waits=sq_wl3 if tt == 0 else [], tick=(tt == 1))
            t_sq = ("scalar", em.ticks["scalar"])
            em.op("vector", lambda h: h.tensor_scalar(
                statb[:, 4:6], statb[:, 2:4], 1.0 / C, 1e-5, ALU.mult, ALU.add),
                waits=[t_sq], fence=True)
            t_m = em.op("vector", lambda h: h.tensor_scalar_mul(
                statb[:, 6:8], statb[:, 0:2], 1.0 / C), waits=[t_sum],
                fence=True, tick=True)
            em.op("scalar", lambda h: h.activation(
                out=statb[:, 8:10], in_=statb[:, 6:8], func=AF.Square),
                waits=[t_m], fence=True, tick=True)
            t_m2s = ("scalar", em.ticks["scalar"])
            t5["var"] = t_var = em.op("vector", lambda h: h.tensor_tensor(
                statb[:, 10:12], statb[:, 4:6], statb[:, 8:10], ALU.subtract),
                waits=[t_m2s], tick=True)
            em.op("scalar", lambda h: h.activation(
                out=statb[:, 14:16], in_=statb[:, 10:12], func=AF.Ln),
                waits=[t_var], fence=True)
            t_rstd = em.op("scalar", lambda h: h.activation(
                out=statb[:, 12:14], in_=statb[:, 14:16], func=AF.Exp, scale=-0.5),
                tick=True)
            owl = [t_rstd] + ([t5["odma"][pj]] if t5["odma"][pj] else [])
            for tt in range(2):
                em.op("vector", lambda h, tt=tt, pj=pj: h.tensor_scalar(
                    outb3[:, pj * 2 + tt, :], acc5b3[:, tt, :],
                    statb[:, 6 + tt:7 + tt], statb[:, 12 + tt:13 + tt],
                    ALU.subtract, ALU.mult), waits=owl if tt == 0 else [],
                    tick=(tt == 1))
            t_ap = ("vector", em.ticks["vector"])
            dst = out_h[j * TC:(j + 1) * TC, :].rearrange("(tt p) c -> p tt c", p=P)
            t5["odma"][pj] = em.dma("sync", dst, outb3[:, pj * 2:pj * 2 + 2, :],
                                    waits=[t_ap])

    return nc, em, d, ar, (sb_ctx, ps_ctx)


def finalize2(nc, em):
    for q in DMAQ:
        fw = [(f"dma_{q}{l}", 16 * em.dmas[q][l])
              for l in range(NLANES) if em.dmas[q][l]]
        if fw:
            em.ops[q].append([lambda h: h.nop(), fw, None])
    em.check_deadlock()
    import contextlib
    stack = contextlib.ExitStack()
    sems = {e: stack.enter_context(nc.semaphore("sem_" + e)) for e in ENGINES}
    dma_sems = {f"{q}{l}": stack.enter_context(nc.semaphore(f"dsem_{q}{l}"))
                for q in DMAQ for l in range(NLANES)}
    with nc.Block() as block:
        def emit_engine(e):
            def inner(h):
                last_wait = {}
                for fn, waits, kind in em.ops[e]:
                    for w in waits:
                        if w is None:
                            continue
                        name, val = w
                        if last_wait.get(name, -1) >= val:
                            continue
                        last_wait[name] = val
                        sem = dma_sems[name[4:]] if name.startswith("dma_") else sems[name]
                        h.wait_ge(sem, val)
                    ins = fn(h)
                    if kind == "tick":
                        h.drain()
                        h.nop().then_inc(sems[e], 1)
                    elif kind == "fence":
                        h.drain()
                    elif isinstance(kind, tuple) and kind[0] == "dma":
                        ins.then_inc(dma_sems[f"{e}{kind[1]}"], 16)
            return inner

        block.tensor(emit_engine("tensor"))
        block.vector(emit_engine("vector"))
        block.scalar(emit_engine("scalar"))
        block.gpsimd(emit_engine("gpsimd"))
        block.sync(emit_engine("sync"))
    stack.close()
    return nc


_CACHE = {}


def kernel(**inputs):
    shared, xs = host_prep(inputs)
    if "nc" not in _CACHE:
        nc, em, d, ar, ctxs = build()
        finalize2(nc, em)
        _CACHE["nc"] = nc
    nc = _CACHE["nc"]
    in_maps = []
    for b in range(8):
        m = dict(shared)
        m["x"] = xs[b]
        in_maps.append(m)
    res = run_bass_kernel_spmd(nc, in_maps, core_ids=list(range(8)))
    out = np.stack([np.asarray(res.results[b]["out"]) for b in range(8)])
    return out.astype(np.float32, copy=False)


# version 04
# speedup vs baseline: 1.0855x; 1.0056x over previous
import numpy as np
import ml_dtypes
import concourse.bass as bass
import concourse.mybir as mybir
from concourse.bass_utils import run_bass_kernel_spmd

F32 = mybir.dt.float32
BF16 = mybir.dt.bfloat16
AF = mybir.ActivationFunctionType
ALU = mybir.AluOpType
AX = mybir.AxisListType

P = 128
C = 1024
T = 4096
HID = 4096
TC = 256
NCH = T // TC
NCC = C // P
NHC = HID // P
NTT = TC // P
MAGIC = 12582912.0
ARENA_F32 = 52224

ENGINES = ("tensor", "vector", "scalar", "gpsimd", "sync")
DMAQ = ("sync", "scalar", "gpsimd")
NLANES = 3


class Em:

    def __init__(self):
        self.ops = {e: [] for e in ENGINES}
        self.ticks = {e: 0 for e in ENGINES}
        self.dmas = {q: [0] * NLANES for q in DMAQ}
        self.rr = {q: 0 for q in DMAQ}
        self.pending = {e: [] for e in ENGINES}

    def op(self, eng, fn, waits=(), tick=False, fence=False):
        kind = "tick" if tick else ("fence" if fence else None)
        w = self.pending[eng] + list(waits)
        self.pending[eng] = []
        self.ops[eng].append([fn, w, kind])
        if tick:
            self.ticks[eng] += 1
            return (eng, self.ticks[eng])
        return None

    def dma(self, q, out, in_, waits=(), lane=None, slow=False):
        if lane is None:
            lane = self.rr[q]
            self.rr[q] = (lane + 1) % NLANES
        def fn(h):
            return h.dma_start(out=out, in_=in_,
                               allow_slow_non_contiguous=slow)
        w = self.pending[q] + list(waits)
        self.pending[q] = []
        prev = self.dmas[q][lane]
        if prev:
            w.append((f"dma_{q}{lane}", 16 * prev))
        self.ops[q].append([fn, w, ("dma", lane)])
        self.dmas[q][lane] = prev + 1
        return (f"dma_{q}{lane}", 16 * (prev + 1))

    def stage_barrier(self):
        toks = [(e, self.ticks[e]) for e in ENGINES if self.ticks[e] > 0]
        toks += [(f"dma_{q}{l}", 16 * self.dmas[q][l])
                 for q in DMAQ for l in range(NLANES) if self.dmas[q][l]]
        for e in ENGINES:
            self.pending[e] = [t for t in toks
                               if not (t[0] == e and self.ticks[e] == t[1])]

    def barrier_token(self, eng):
        if self.ticks[eng] == 0:
            return None
        return (eng, self.ticks[eng])

    def check_deadlock(self):
        idx = {e: 0 for e in ENGINES}
        done_ticks = {e: 0 for e in ENGINES}
        done_dmas = {f"{q}{l}": 0 for q in DMAQ for l in range(NLANES)}

        def ok(w):
            if w is None:
                return True
            name, val = w
            if name.startswith("dma_"):
                return 16 * done_dmas[name[4:]] >= val
            return done_ticks[name] >= val

        progress = True
        while progress:
            progress = False
            for e in ENGINES:
                while idx[e] < len(self.ops[e]):
                    fn, waits, kind = self.ops[e][idx[e]]
                    if not all(ok(w) for w in waits):
                        break
                    if kind == "tick":
                        done_ticks[e] += 1
                    elif isinstance(kind, tuple) and kind[0] == "dma":
                        done_dmas[f"{e}{kind[1]}"] += 1
                    idx[e] += 1
                    progress = True
        stuck = {e: (idx[e], len(self.ops[e])) for e in ENGINES
                 if idx[e] < len(self.ops[e])}
        if stuck:
            detail = {}
            for e in stuck:
                fn, waits, kind = self.ops[e][idx[e]]
                detail[e] = (idx[e], waits)
            raise RuntimeError(f"deadlock: stuck at {detail}")

    def emit(self, nc, handles, sems, dma_sems):
        for e in ENGINES:
            h = handles[e]
            last_wait = {}
            for fn, waits, kind in self.ops[e]:
                for w in waits:
                    if w is None:
                        continue
                    name, val = w
                    if last_wait.get(name, -1) >= val:
                        continue
                    last_wait[name] = val
                    sem = dma_sems[name[4:]] if name.startswith("dma_") else sems[name]
                    h.wait_ge(sem, val)
                ins = fn(h)
                if kind == "tick":
                    ins.then_inc(sems[e], 1)
                elif kind == "dma":
                    ins.then_inc(dma_sems[e], 16)


def ceil8(n):
    return (n + 7) & ~7


class Arena:
    def __init__(self, handle):
        self.h = handle
        self.off = 0
        self.marks = []

    def alloc(self, n_f32):
        n_f32 = ceil8(n_f32)
        assert self.off + n_f32 <= ARENA_F32, f"arena overflow {self.off}+{n_f32}"
        off = self.off
        self.off += n_f32
        return off

    def f32(self, n, shape=None):
        off = self.alloc(n)
        ap = self.h.ap()[:, off:off + n]
        if shape is not None:
            ap = ap.rearrange("p (" + " ".join(f"d{i}" for i in range(len(shape))) + ") -> p " +
                              " ".join(f"d{i}" for i in range(len(shape))),
                              **{f"d{i}": s for i, s in enumerate(shape)})
        return ap

    def bf16(self, n, shape=None):
        off = self.alloc((n + 1) // 2)
        ap = self.h.ap().bitcast(BF16)[:, 2 * off:2 * off + n]
        if shape is not None:
            ap = ap.rearrange("p (" + " ".join(f"d{i}" for i in range(len(shape))) + ") -> p " +
                              " ".join(f"d{i}" for i in range(len(shape))),
                              **{f"d{i}": s for i, s in enumerate(shape)})
        return ap

    def mark(self):
        self.marks.append(self.off)

    def reset(self):
        self.off = self.marks.pop()


VEC8 = ["td", "tf", "tmk", "tmv", "tmr", "g1", "b1", "g2", "b2", "ftmk", "ftmr"]


def host_prep(inputs):
    F = np.float32
    g = {k: np.ascontiguousarray(np.asarray(v)) for k, v in inputs.items()}
    C_ = C

    def t2(a):
        return np.ascontiguousarray(np.asarray(a).T)

    def v8(a):
        return np.ascontiguousarray(np.asarray(a).reshape(NCC, P).T)

    shared = {
        "WkT": t2(g["L0_attn_Wk"]), "WvT": t2(g["L0_attn_Wv"]),
        "WrT": t2(g["L0_attn_Wr"]), "WoT": t2(g["L0_attn_Wo"]),
        "fWkT": t2(g["L0_attn_fWk"]), "fWrT": t2(g["L0_attn_fWr"]),
        "fWvT": t2(g["L0_attn_fWv"]),
        "w1T": t2(g["L0_ffn_w1"]), "w2T": t2(g["L0_ffn_w2"]),
        "vecs": np.ascontiguousarray(np.stack([
            v8(g["L0_attn_time_decay"]), v8(g["L0_attn_time_first"]),
            v8(g["L0_attn_tmk"].reshape(C_)), v8(g["L0_attn_tmv"].reshape(C_)),
            v8(g["L0_attn_tmr"].reshape(C_)),
            v8(g["L0_attn_ln1_g"]), v8(g["L0_attn_ln1_b"]),
            v8(g["L0_attn_ln2_g"]), v8(g["L0_attn_ln2_b"]),
            v8(g["L0_attn_f_tmk"].reshape(C_)), v8(g["L0_attn_f_tmr"].reshape(C_)),
        ], axis=1).reshape(P, len(VEC8) * NCC)),
        "ident": np.eye(P, dtype=F),
        "ident16": np.eye(P, dtype=F).astype(ml_dtypes.bfloat16),
        "ones1": np.ones((1, P), dtype=F),
        "convw": np.ascontiguousarray(
            g["L0_ffn_conv_w"].reshape(NHC, P, 3).transpose(1, 0, 2).reshape(P, NHC * 3)),
        "convb": np.ascontiguousarray(
            g["L0_ffn_conv_b"].reshape(NHC, P).T),
    }
    xs = [np.ascontiguousarray(g["L0_attn_x"][b]) for b in range(8)]
    return shared, xs


def build(stages=("s1", "s2a", "s2b", "s3a", "s3b"), debug=()):
    nc = bass.Bass("TRN2", target_bir_lowering=False, debug=False,
                   detect_race_conditions=False)
    em = Em()
    d = {}

    def dram(name, shape, dt=F32, kind="ExternalInput"):
        t = nc.dram_tensor(name, shape, dt, kind=kind).ap()
        d[name] = t
        return t

    def scratch(name, shape, dt=F32):
        kind = "ExternalOutput" if name in debug else "Internal"
        return dram(name, shape, dt, kind=kind)

    x_h = dram("x", [T, C])
    for n, shape in [("WkT", [C, C]), ("WvT", [C, C]), ("WrT", [C, C]),
                     ("WoT", [C, C]), ("fWkT", [C, HID]), ("fWrT", [C, C]),
                     ("fWvT", [HID, C]), ("w1T", [C, HID]), ("w2T", [HID, C])]:
        dram(n, shape)
    dram("vecs", [P, len(VEC8) * NCC])
    dram("ident", [P, P])
    dram("ident16", [P, P], BF16)
    dram("ones1", [1, P])
    dram("convw", [P, NHC * 3])
    dram("convb", [P, NHC])
    x1_h = scratch("x1s", [T, C])
    kk_h = scratch("kks", [NCH, P, NHC * TC], BF16)
    xr2_h = scratch("xr2s", [NCH, P, NCC * TC], BF16)
    sr2_h = scratch("sr2s", [T, C])
    xA_h = scratch("xAs", [T, C])
    gq_h = scratch("gqs", [NCH, P, NHC * TC], BF16)
    inv2_h = scratch("inv2s", [T // P, P])
    w2q_h = scratch("w2qs", [P, NHC * 1024], BF16)
    out_h = dram("out", [T, C], kind="ExternalOutput")

    sb_ctx = nc.sbuf_tensor("arena", [P, ARENA_F32], F32)
    arena_h = sb_ctx.__enter__()
    ar = Arena(arena_h)

    ps_ctx = []
    pbank = []
    for i in range(8):
        c = nc.psum_tensor(f"pbank{i}", [P, 512], F32)
        ps_ctx.append(c)
        pbank.append(c.__enter__())
    pb = pbank[:4]

    identb = ar.f32(P)
    identb16 = ar.bf16(P)

    onesb = ar.f32(P)
    vecsb = ar.f32(len(VEC8) * NCC)
    derv = ar.f32(80)
    state = ar.f32(80)
    statb = ar.f32(32)
    convwb = ar.f32(NHC * 3)
    convbb = ar.f32(NHC)
    scl = ar.f32(32)

    def vcol(name, i=0):
        j = VEC8.index(name) * NCC
        return vecsb[:, j + i:j + i + 1]

    def vgrp(name):
        j = VEC8.index(name) * NCC
        return vecsb[:, j:j + NCC]

    ident_ap = identb

    tok_ident = em.dma("sync", identb, d["ident"][:, :])
    tok_ident16 = em.dma("sync", identb16, d["ident16"][:, :])
    tok_ones = em.dma("sync", onesb[0:1, :], d["ones1"][:, :])
    tok_vecs = em.dma("sync", vecsb, d["vecs"][:, :])
    tok_convw = em.dma("sync", convwb, d["convw"][:, :])
    tok_convb = em.dma("sync", convbb, d["convb"][:, :])

    ew = derv[:, 0:8]; wneg = derv[:, 8:16]; lam = derv[:, 16:24]
    eu = derv[:, 24:32]; allow = derv[:, 32:40]
    SA = state[:, 0:8]; SB = state[:, 8:16]; mrun = state[:, 16:24]
    tmpa = state[:, 24:32]; mnew = state[:, 32:40]; mdelta = state[:, 40:48]
    fac = state[:, 48:56]; negm = state[:, 56:64]; ck = state[:, 64:72]

    t_ew = em.op("scalar", lambda h: h.activation(out=ew, in_=vgrp("td"), func=AF.Exp),
                 waits=[tok_vecs], tick=True)
    t_wneg = em.op("vector", lambda h: h.tensor_scalar_mul(wneg, ew, -1.0),
                   waits=[t_ew], tick=True)
    t_lam = em.op("scalar", lambda h: h.activation(out=lam, in_=wneg, func=AF.Exp),
                  waits=[t_wneg], tick=True)
    t_eu = em.op("scalar", lambda h: h.activation(out=eu, in_=vgrp("tf"), func=AF.Exp),
                 waits=[tok_vecs], tick=True)
    t_allow = em.op("vector", lambda h: h.tensor_scalar(allow, ew, float(TC), 60.0,
                                                        ALU.mult, ALU.min),
                    waits=[t_ew], tick=True)
    em.op("vector", lambda h: h.memset(SA, 0.0))
    em.op("vector", lambda h: h.memset(SB, 0.0))
    t_state0 = em.op("vector", lambda h: h.memset(mrun, -1e30), tick=True)

    ar.mark()
    wbuf = {}
    tok_w = {}
    for n in ["WkT", "WvT", "WrT", "WoT"]:
        wbuf[n] = ar.bf16(NCC * C, shape=[NCC, C])
        srcw = d[n].rearrange("(cc p) co -> p cc co", p=P)
        tok_w[n] = em.dma("gpsimd", wbuf[n], srcw)

    xin = ar.f32(2 * 2048)
    z1tm = ar.f32(2048)
    z1fm = ar.f32(NCC * 257 + 7)
    dbuf = ar.f32(2048)
    xkb = ar.bf16(NCC * TC, shape=[NCC, TC])
    xvb = ar.bf16(NCC * TC, shape=[NCC, TC])
    xrb = ar.bf16(NCC * TC, shape=[NCC, TC])
    ekbuf = ar.f32(2048)
    ekvbuf = ar.f32(2048)
    AAbuf = ar.f32(NCC * 264)
    BBbuf = ar.f32(NCC * 264)
    numbuf = ar.f32(2048)
    denbuf = ar.f32(2048)
    srbuf = ar.f32(2048)
    sryb_f = ar.bf16(NCC * TC)
    x1buf = ar.f32(2 * 2048)

    def view3(ap, a, b):
        return ap.rearrange("p (a b) -> p a b", a=a, b=b)

    xin3 = view3(xin, 4, 1024)
    z1tm3 = view3(z1tm, 2, 1024)
    z1fm3 = view3(z1fm[:, :NCC * 257], NCC, 257)
    dbuf3 = view3(dbuf, NCC, TC)
    ek3 = view3(ekbuf, NCC, TC)
    ekv3 = view3(ekvbuf, NCC, TC)
    AA3 = view3(AAbuf, NCC, 264)
    BB3 = view3(BBbuf, NCC, 264)
    num3 = view3(numbuf, NCC, TC)
    den3 = view3(denbuf, NCC, TC)
    sr3 = view3(srbuf, NCC, TC)
    x13 = view3(x1buf, 4, 1024)
    sryb = view3(sryb_f, NCC, TC)

    t_z1fm0 = em.op("vector", lambda h: h.memset(z1fm3[:, :, 0], 0.0), tick=True)

    if "s1" in stages:
        PH = 99
        SKIP = set()
        toks = {"sr": None, "x1evac": [None, None], "evacz1": None, "ek": None,
                "ekv": None, "x1dma": [None, None], "y": None, "sry": None,
                "mixr": None, "var": None, "den": None, "vd": None}
        for j in range(NCH):
            pj = j % 2
            xa = [xin3[:, pj * 2 + tt, :] for tt in range(2)]
            x1a = [x13[:, pj * 2 + tt, :] for tt in range(2)]
            xin_t = lambda tt, xa=xa: xa[tt]
            x1_t = lambda tt, x1a=x1a: x1a[tt]
            src = x_h[j * TC:(j + 1) * TC, :].rearrange("(tt p) c -> p tt c", p=P)
            wl = [toks["x1evac"][pj]] if toks["x1evac"][pj] else []
            tok_x = em.dma("sync", view3(xin, 4, 1024)[:, pj * 2:pj * 2 + 2, :], src,
                           waits=wl)
            for tt in (range(2) if "1" not in SKIP else []):
                em.op("vector", lambda h, tt=tt, xin_t=xin_t: h.reduce_sum(
                    statb[:, 0 + tt:1 + tt], xin_t(tt), axis=AX.X), waits=[tok_x])
            t_sum = em.op("vector", lambda h: h.nop(), tick=True)
            sq_wl = [tok_x] + ([toks["var"]] if toks["var"] else [])
            for tt in (range(2) if "1" not in SKIP else []):
                em.op("scalar", lambda h, tt=tt, xin_t=xin_t: h.activation(
                    out=z1tm3[:, tt, :], in_=xin_t(tt), func=AF.Square,
                    accum_out=statb[:, 2 + tt:3 + tt]), waits=sq_wl if tt == 0 else [])
            t_sq = em.op("scalar", lambda h: h.nop(), tick=True)
            if PH < 2:
                continue
            if "2" in SKIP:
                t_var = t_rstd = None
            if "2" not in SKIP:
              em.op("vector", lambda h: h.tensor_scalar(
                statb[:, 4:6], statb[:, 2:4], 1.0 / C, 1e-5, ALU.mult, ALU.add),
                waits=[t_sq])
            if "2" not in SKIP:
              em.op("vector", lambda h: h.tensor_scalar_mul(
                statb[:, 6:8], statb[:, 0:2], 1.0 / C), waits=[t_sum])
            if "2" not in SKIP:
              t_m = em.op("vector", lambda h: h.nop(), tick=True)
              em.op("scalar", lambda h: h.activation(
                out=statb[:, 8:10], in_=statb[:, 6:8], func=AF.Square),
                waits=[t_m])
            if "2" not in SKIP:
              t_m2s = em.op("scalar", lambda h: h.nop(), tick=True)
              toks["var"] = t_var = em.op("vector", lambda h: h.tensor_tensor(
                statb[:, 10:12], statb[:, 4:6], statb[:, 8:10], ALU.subtract),
                waits=[t_m2s], tick=True)
            if "2" not in SKIP:
              em.op("scalar", lambda h: h.activation(
                out=statb[:, 14:16], in_=statb[:, 10:12], func=AF.Ln), waits=[t_var],
                fence=True)
            if "2" not in SKIP:
              t_rstd = em.op("scalar", lambda h: h.activation(
                out=statb[:, 12:14], in_=statb[:, 14:16], func=AF.Exp, scale=-0.5),
                tick=True)
            if PH < 3:
                continue
            for tt in (range(2) if "3" not in SKIP else []):
                em.op("vector", lambda h, tt=tt, xin_t=xin_t: h.tensor_scalar(
                    z1tm3[:, tt, :], xin_t(tt), statb[:, 6 + tt:7 + tt],
                    statb[:, 12 + tt:13 + tt], ALU.subtract, ALU.mult),
                    waits=[t_rstd] if t_rstd else [])
            t_apply = em.op("vector", lambda h: h.nop(), tick=True)
            if PH < 4:
                continue
            if j > 0 and "4" not in SKIP:
                em.op("vector", lambda h: h.tensor_copy(
                    z1fm3[:, :, 0], z1fm3[:, :, 256]), waits=[toks["evacz1"]])
            t_bnd = em.op("vector", lambda h: h.nop(), tick=True)
            if PH < 5:
                continue
            twl = [t_apply, t_bnd]
            if toks["sr"]:
                twl.append(toks["sr"])
            if j == 0:
                twl += [tok_ident, t_z1fm0]
            _first = True
            for bank in range(4):
                for cc in (bank, bank + 4):
                    for tt in range(2):
                        em.op("tensor", lambda h, cc=cc, tt=tt: h.transpose(
                            pb[cc % 4][:, (cc // 4) * 256 + tt * 128:
                                       (cc // 4) * 256 + tt * 128 + 128],
                            z1tm3[:, tt, cc * 128:(cc + 1) * 128], ident_ap),
                            waits=twl if _first else [])
                        _first = False
            t_tr = em.op("tensor", lambda h: h.nop(), tick=True)
            if PH < 6:
                continue
            evz_wl = [t_tr] + ([tok_vecs] if j == 0 else [])
            _plain = False
            for ci_, cc in enumerate(range(NCC)):
                if True:
                    em.op("scalar", lambda h, cc=cc: h.activation(
                        out=z1fm3[:, cc, 1:257],
                        in_=pb[cc % 4][:, (cc // 4) * 256:(cc // 4) * 256 + 256],
                        func=AF.Identity, scale=vcol("g1", cc), bias=vcol("b1", cc)),
                        waits=evz_wl if ci_ == 0 else [])
            toks["evacz1"] = t_evz = em.op("scalar", lambda h: h.nop(), tick=True)
            if PH < 7:
                continue
            d_wl = [t_evz] + ([toks["den"]] if toks["den"] else [])
            if j == 0:
                d_wl.append(tok_vecs)
            toks["vd"] = t_d = em.op("vector", lambda h: h.tensor_tensor(
                dbuf3, z1fm3[:, :, 1:257], z1fm3[:, :, 0:256], ALU.subtract),
                waits=d_wl, tick=True)
            if PH < 8:
                continue
            for cc in range(NCC):
                em.op("vector", lambda h, cc=cc: h.scalar_tensor_tensor(
                    xkb[:, cc, :], dbuf3[:, cc, :], vcol("tmk", cc),
                    z1fm3[:, cc, 0:256], ALU.mult, ALU.add),
                    waits=[t_d] if cc == 0 else [])
                em.op("vector", lambda h, cc=cc: h.scalar_tensor_tensor(
                    xvb[:, cc, :], dbuf3[:, cc, :], vcol("tmv", cc),
                    z1fm3[:, cc, 0:256], ALU.mult, ALU.add))
                em.op("vector", lambda h, cc=cc: h.scalar_tensor_tensor(
                    xrb[:, cc, :], dbuf3[:, cc, :], vcol("tmr", cc),
                    z1fm3[:, cc, 0:256], ALU.mult, ALU.add))
            t_mixkv = t_mixr = em.op("vector", lambda h: h.nop(), tick=True)
            toks["mixr"] = t_mixr
            if PH < 9:
                continue
            kwl = [t_mixkv, tok_w["WkT"], t_evz]
            kwl = [w for w in kwl if w]
            if toks["ek"]:
                kwl.append(toks["ek"])
            for co in range(NCC):
                for ci in range(NCC):
                    em.op("tensor", lambda h, co=co, ci=ci: h.matmul(
                        pb[co // 2][:, (co % 2) * 256:(co % 2) * 256 + 256],
                        wbuf["WkT"][:, ci, co * 128:(co + 1) * 128],
                        xkb[:, ci, :], start=(ci == 0), stop=(ci == NCC - 1)),
                        waits=kwl if (co == 0 and ci == 0) else [])
            t_k = em.op("tensor", lambda h: h.nop(), tick=True)
            if PH < 10:
                continue
            for co in range(NCC):
                em.op("vector", lambda h, co=co: h.reduce_max(
                    ck[:, co:co + 1],
                    pb[co // 2][:, (co % 2) * 256:(co % 2) * 256 + 256], axis=AX.X),
                    waits=[t_k, t_allow, t_state0] if co == 0 else [],
                    fence=(co == NCC - 1))
            em.op("vector", lambda h: h.tensor_tensor(tmpa, mrun, allow, ALU.subtract), fence=True)
            em.op("vector", lambda h: h.tensor_tensor(mnew, tmpa, ck, ALU.max), fence=True)
            em.op("vector", lambda h: h.tensor_tensor(mdelta, mrun, mnew, ALU.subtract), fence=True)
            em.op("vector", lambda h: h.tensor_scalar_mul(negm, mnew, -1.0), fence=True)
            t_st1 = em.op("vector", lambda h: h.tensor_copy(mrun, mnew), tick=True)
            t_fac = em.op("scalar", lambda h: h.activation(
                out=fac, in_=mdelta, func=AF.Exp), waits=[t_st1], tick=True)
            em.op("vector", lambda h: h.tensor_tensor(SA, SA, fac, ALU.mult),
                  waits=[t_fac], fence=True)
            t_st2 = em.op("vector", lambda h: h.tensor_tensor(SB, SB, fac, ALU.mult),
                          tick=True)
            if PH < 11:
                continue
            vwl = [t_mixkv, tok_w["WvT"]]
            vwl = [w for w in vwl if w]
            if toks["x1evac"][1 - pj]:
                vwl.append(toks["x1evac"][1 - pj])
            for co in range(NCC):
                for ci in range(NCC):
                    em.op("tensor", lambda h, co=co, ci=ci: h.matmul(
                        pbank[4 + co // 2][:, (co % 2) * 256:(co % 2) * 256 + 256],
                        wbuf["WvT"][:, ci, co * 128:(co + 1) * 128],
                        xvb[:, ci, :], start=(ci == 0), stop=(ci == NCC - 1)),
                        waits=vwl if (co == 0 and ci == 0) else [])
            t_v = em.op("tensor", lambda h: h.nop(), tick=True)
            if PH < 12:
                continue
            ek_wl = [t_st1, t_k] + ([toks["y"]] if toks["y"] else [])
            for co in range(NCC):
                em.op("scalar", lambda h, co=co: h.activation(
                    out=ek3[:, co, :],
                    in_=pb[co // 2][:, (co % 2) * 256:(co % 2) * 256 + 256],
                    func=AF.Exp, bias=negm[:, co:co + 1]),
                    waits=ek_wl if co == 0 else [])
            toks["ek"] = t_ek = em.op("scalar", lambda h: h.nop(), tick=True)
            if PH < 13:
                continue
            for co in range(NCC):
                em.op("vector", lambda h, co=co: h.tensor_tensor(
                    ekv3[:, co, :], ek3[:, co, :],
                    pbank[4 + co // 2][:, (co % 2) * 256:(co % 2) * 256 + 256], ALU.mult),
                    waits=[t_ek, t_v] if co == 0 else [])
            toks["ekv"] = t_ekv = em.op("vector", lambda h: h.nop(), tick=True)
            if PH < 14:
                continue
            em.op("vector", lambda h: h.tensor_copy(AA3[:, :, 0], SA),
                  waits=[t_st2, t_ekv])
            for co in range(NCC):
                em.op("vector", lambda h, co=co: h.tensor_tensor_scan(
                    AA3[:, co, 1:257], lam[:, co:co + 1].broadcast_to([P, TC]),
                    ekv3[:, co, :], SA[:, co:co + 1], ALU.mult, ALU.add),
                    waits=[t_lam] if co == 0 else [])
            t_scanA = em.op("vector", lambda h: h.nop(), tick=True)
            scb_wl = [t_st2, t_ek] + ([toks["den"]] if toks["den"] else [])
            em.op("vector", lambda h: h.tensor_copy(BB3[:, :, 0], SB),
                  waits=scb_wl)
            for co in range(NCC):
                em.op("vector", lambda h, co=co: h.tensor_tensor_scan(
                    BB3[:, co, 1:257], lam[:, co:co + 1].broadcast_to([P, TC]),
                    ek3[:, co, :], SB[:, co:co + 1], ALU.mult, ALU.add),
                    waits=[t_lam] if co == 0 else [])
            t_scanB = em.op("vector", lambda h: h.nop(), tick=True)
            if PH < 15:
                continue
            em.op("vector", lambda h: h.tensor_copy(SA, AA3[:, :, 256]),
                  waits=[t_scanA])
            t_carA = em.op("vector", lambda h: h.nop(), tick=True)
            em.op("vector", lambda h: h.tensor_copy(SB, BB3[:, :, 256]),
                  waits=[t_scanB])
            t_carB = em.op("vector", lambda h: h.nop(), tick=True)
            if PH < 16:
                continue
            num_wl = [t_eu, t_scanA] + ([toks["y"]] if toks["y"] else [])
            for co in range(NCC):
                em.op("vector", lambda h, co=co: h.scalar_tensor_tensor(
                    num3[:, co, :], ekv3[:, co, :], eu[:, co:co + 1],
                    AA3[:, co, 0:256], ALU.mult, ALU.add),
                    waits=num_wl if co == 0 else [])
            t_num = em.op("vector", lambda h: h.nop(), tick=True)
            et_wl = [t_eu, t_ek, t_mixkv] + ([toks["vd"]] if toks["vd"] else [])
            for co in range(NCC):
                em.op("gpsimd", lambda h, co=co: h.tensor_tensor(
                    dbuf3[:, co, :], ek3[:, co, :],
                    eu[:, co:co + 1].broadcast_to([P, TC]), ALU.mult),
                    waits=et_wl if co == 0 else [])
            for co in range(NCC):
                em.op("gpsimd", lambda h, co=co: h.tensor_tensor(
                    den3[:, co, :], dbuf3[:, co, :], BB3[:, co, 0:256], ALU.add),
                    waits=[t_scanB] if co == 0 else [])
            toks["den"] = t_den = em.op("gpsimd", lambda h: h.nop(), tick=True)
            if PH < 17:
                continue
            em.op("scalar", lambda h: h.activation(
                out=ekbuf, in_=denbuf, func=AF.Ln), waits=[t_den], fence=True)
            t_rden = em.op("scalar", lambda h: h.activation(
                out=denbuf, in_=ekbuf, func=AF.Exp, scale=-1.0), tick=True)
            if PH < 18:
                continue
            rwl = [t_mixr, tok_w["WrT"], t_ek]
            rwl = [w for w in rwl if w]
            for co in range(NCC):
                for ci in range(NCC):
                    em.op("tensor", lambda h, co=co, ci=ci: h.matmul(
                        pb[co // 2][:, (co % 2) * 256:(co % 2) * 256 + 256],
                        wbuf["WrT"][:, ci, co * 128:(co + 1) * 128],
                        xrb[:, ci, :], start=(ci == 0), stop=(ci == NCC - 1)),
                        waits=rwl if (co == 0 and ci == 0) else [])
            t_r = em.op("tensor", lambda h: h.nop(), tick=True)
            sr_wl = [t_r] + ([toks["sry"]] if toks["sry"] else [])
            for co in range(NCC):
                em.op("scalar", lambda h, co=co: h.activation(
                    out=sr3[:, co, :],
                    in_=pb[co // 2][:, (co % 2) * 256:(co % 2) * 256 + 256],
                    func=AF.Sigmoid), waits=sr_wl if co == 0 else [])
            toks["sr"] = t_sr = em.op("scalar", lambda h: h.nop(), tick=True)
            if PH < 19:
                continue
            toks["y"] = t_y = em.op("gpsimd", lambda h: h.tensor_tensor(
                ekbuf, numbuf, denbuf, ALU.mult), waits=[t_num, t_rden], tick=True)
            toks["sry"] = t_sry = em.op("gpsimd", lambda h: h.tensor_tensor(
                sryb_f, ekbuf, srbuf, ALU.mult),
                waits=[t_sr], tick=True)
            if PH < 20:
                continue
            wwl = [t_sry, tok_w["WoT"], t_ekv]
            wwl = [w for w in wwl if w]
            first = True
            for tt in range(2):
                for nh in range(2):
                    for ci in range(NCC):
                        em.op("tensor", lambda h, tt=tt, nh=nh, ci=ci: h.matmul(
                            pbank[4 + 2 * tt + nh][:, :],
                            sryb[:, ci, tt * 128:tt * 128 + 128],
                            wbuf["WoT"][:, ci, nh * 512:nh * 512 + 512],
                            start=(ci == 0), stop=(ci == NCC - 1)),
                            waits=wwl if first else [])
                        first = False
            t_wo = em.op("tensor", lambda h: h.nop(), tick=True)
            if PH < 21:
                continue
            x1_wl = [t_wo] + ([toks["x1dma"][pj]] if toks["x1dma"][pj] else [])
            for tt in range(2):
                for nh in range(2):
                    em.op("vector", lambda h, tt=tt, nh=nh, x1_t=x1_t, xin_t=xin_t:
                          h.tensor_tensor(
                              x1_t(tt)[:, nh * 512:nh * 512 + 512],
                              xin_t(tt)[:, nh * 512:nh * 512 + 512],
                              pbank[4 + 2 * tt + nh][:, :], ALU.add),
                          waits=x1_wl if (tt == 0 and nh == 0) else [])
            toks["x1evac"][pj] = t_x1 = em.op(
                "vector", lambda h: h.nop(), tick=True)
            dst = x1_h[j * TC:(j + 1) * TC, :].rearrange("(tt p) c -> p tt c", p=P)
            toks["x1dma"][pj] = em.dma(
                "sync", dst, x13[:, pj * 2:pj * 2 + 2, :], waits=[t_x1])


    if "s2a" in stages:
        em.stage_barrier()
        ar.reset(); ar.mark()
        fwk = ar.bf16(NCC * HID, shape=[NCC, HID])
        tok_fwk = em.dma("gpsimd", fwk, d["fWkT"].rearrange("(cc p) ho -> p cc ho", p=P))
        x1in = ar.f32(2 * 2048)
        x1in3 = view3(x1in, 4, 1024)
        z2tm = ar.f32(2048); z2tm3 = view3(z2tm, 2, 1024)
        z2fm = ar.f32(NCC * 257 + 7); z2fm3 = view3(z2fm[:, :NCC * 257], NCC, 257)
        dbuf2 = ar.f32(2048); dbuf23 = view3(dbuf2, NCC, TC)
        xk2b = ar.bf16(NCC * TC, shape=[NCC, TC])
        xr2b_f = ar.bf16(NCC * TC)
        xr2b = view3(xr2b_f, NCC, TC)
        kkb_f = ar.bf16(NHC * TC)
        kkb = view3(kkb_f, NHC, TC)
        rbuf = ar.f32(8 * TC)
        rbuf3 = view3(rbuf, 8, TC)

        t_z2fm0 = em.op("vector", lambda h: h.memset(z2fm3[:, :, 0], 0.0), tick=True)
        t2 = {"evac": None, "mix": None, "kk": [None] * 8, "x1dma": [None, None],
              "var": None, "kkdma": None, "xrdma": None}
        for j in range(NCH):
            pj = j % 2
            x1a = [x1in3[:, pj * 2 + tt, :] for tt in range(2)]
            srcx1 = x1_h[j * TC:(j + 1) * TC, :].rearrange("(tt p) c -> p tt c", p=P)
            wl = [t2["mix"]] if j >= 2 and t2["mix"] else []
            tok_x1 = em.dma("sync", x1in3[:, pj * 2:pj * 2 + 2, :], srcx1, waits=wl)
            for tt in range(2):
                em.op("vector", lambda h, tt=tt, x1a=x1a: h.reduce_sum(
                    statb[:, 0 + tt:1 + tt], x1a[tt], axis=AX.X),
                    waits=[tok_x1] if tt == 0 else [], tick=(tt == 1))
            t_sum = ("vector", em.ticks["vector"])
            sq_wl = [tok_x1] + ([t2["var"]] if t2["var"] else [])
            for tt in range(2):
                em.op("scalar", lambda h, tt=tt, x1a=x1a: h.activation(
                    out=z2tm3[:, tt, :], in_=x1a[tt], func=AF.Square,
                    accum_out=statb[:, 2 + tt:3 + tt]),
                    waits=sq_wl if tt == 0 else [], tick=(tt == 1))
            t_sq = ("scalar", em.ticks["scalar"])
            em.op("vector", lambda h: h.tensor_scalar(
                statb[:, 4:6], statb[:, 2:4], 1.0 / C, 1e-5, ALU.mult, ALU.add),
                waits=[t_sq], fence=True)
            t_m = em.op("vector", lambda h: h.tensor_scalar_mul(
                statb[:, 6:8], statb[:, 0:2], 1.0 / C), waits=[t_sum],
                fence=True, tick=True)
            em.op("scalar", lambda h: h.activation(
                out=statb[:, 8:10], in_=statb[:, 6:8], func=AF.Square),
                waits=[t_m], fence=True, tick=True)
            t_m2s = ("scalar", em.ticks["scalar"])
            t2["var"] = t_var = em.op("vector", lambda h: h.tensor_tensor(
                statb[:, 10:12], statb[:, 4:6], statb[:, 8:10], ALU.subtract),
                waits=[t_m2s], tick=True)
            em.op("scalar", lambda h: h.activation(
                out=statb[:, 14:16], in_=statb[:, 10:12], func=AF.Ln),
                waits=[t_var], fence=True)
            t_rstd = em.op("scalar", lambda h: h.activation(
                out=statb[:, 12:14], in_=statb[:, 14:16], func=AF.Exp, scale=-0.5),
                tick=True)
            for tt in range(2):
                em.op("vector", lambda h, tt=tt, x1a=x1a: h.tensor_scalar(
                    z2tm3[:, tt, :], x1a[tt], statb[:, 6 + tt:7 + tt],
                    statb[:, 12 + tt:13 + tt], ALU.subtract, ALU.mult),
                    waits=[t_rstd], tick=(tt == 1))
            t_apply = ("vector", em.ticks["vector"])
            if j > 0:
                em.op("vector", lambda h: h.tensor_copy(
                    z2fm3[:, :, 0], z2fm3[:, :, 256]), waits=[t2["evac"]],
                    fence=True, tick=True)
            else:
                em.op("vector", lambda h: h.nop(), waits=[t_z2fm0], tick=True)
            t_bnd = ("vector", em.ticks["vector"])
            twl = [t_apply, t_bnd]
            if t2["kk"][7]:
                twl.append(t2["kk"][7])
            if j == 0:
                twl.append(tok_ident)
            first = True
            for bank in range(4):
                for cc in (bank, bank + 4):
                    for tt in range(2):
                        em.op("tensor", lambda h, cc=cc, tt=tt: h.transpose(
                            pb[cc % 4][:, (cc // 4) * 256 + tt * 128:
                                       (cc // 4) * 256 + tt * 128 + 128],
                            z2tm3[:, tt, cc * 128:(cc + 1) * 128], ident_ap),
                            waits=twl if first else [],
                            tick=(bank == 3 and cc == 7 and tt == 1))
                        first = False
            t_tr = ("tensor", em.ticks["tensor"])
            evz_wl = [t_tr] + ([tok_vecs] if j == 0 else [])
            for cc in range(NCC):
                em.op("scalar", lambda h, cc=cc: h.activation(
                    out=z2fm3[:, cc, 1:257],
                    in_=pb[cc % 4][:, (cc // 4) * 256:(cc // 4) * 256 + 256],
                    func=AF.Identity, scale=vcol("g2", cc), bias=vcol("b2", cc)),
                    waits=evz_wl if cc == 0 else [], tick=(cc == NCC - 1))
            t2["evac"] = t_evz = ("scalar", em.ticks["scalar"])
            dwl = [t_evz] + ([t2["xrdma"]] if t2["xrdma"] else [])
            em.op("vector", lambda h: h.tensor_tensor(
                dbuf23, z2fm3[:, :, 1:257], z2fm3[:, :, 0:256], ALU.subtract),
                waits=dwl, tick=True)
            t_d = ("vector", em.ticks["vector"])
            for cc in range(NCC):
                em.op("vector", lambda h, cc=cc: h.scalar_tensor_tensor(
                    xk2b[:, cc, :], dbuf23[:, cc, :], vcol("ftmk", cc),
                    z2fm3[:, cc, 0:256], ALU.mult, ALU.add),
                    waits=[t_d] if cc == 0 else [])
                em.op("vector", lambda h, cc=cc: h.scalar_tensor_tensor(
                    xr2b[:, cc, :], dbuf23[:, cc, :], vcol("ftmr", cc),
                    z2fm3[:, cc, 0:256], ALU.mult, ALU.add),
                    tick=(cc == NCC - 1))
            t2["mix"] = t_mix = ("vector", em.ticks["vector"])
            wl = [t_mix] + ([t2["xrdma"]] if t2["xrdma"] else [])
            t2["xrdma"] = em.dma("scalar", xr2_h[j, :, :], xr2b_f, waits=[t_mix])
            for hg in range(8):
                first = True
                bb = (hg % 2) * 2
                for hs in range(4):
                    hc = hg * 4 + hs
                    mwl = [t_mix, tok_fwk]
                    if t2["kk"][(hg - 2) % 8]:
                        mwl.append(t2["kk"][(hg - 2) % 8])
                    if hg <= 1:
                        mwl.append(t_evz)
                    for cc in range(NCC):
                        em.op("tensor", lambda h, hc=hc, hs=hs, cc=cc, bb=bb: h.matmul(
                            pb[bb + hs // 2][:, (hs % 2) * 256:(hs % 2) * 256 + 256],
                            fwk[:, cc, hc * 128:(hc + 1) * 128],
                            xk2b[:, cc, :], start=(cc == 0), stop=(cc == NCC - 1)),
                            waits=mwl if first else [],
                            tick=(hs == 3 and cc == NCC - 1))
                        first = False
                t_fk = ("tensor", em.ticks["tensor"])
                for hs in range(4):
                    em.op("scalar", lambda h, hs=hs, hg=hg, bb=bb: h.activation(
                        out=rbuf3[:, (hg % 2) * 4 + hs, :],
                        in_=pb[bb + hs // 2][:, (hs % 2) * 256:(hs % 2) * 256 + 256],
                        func=AF.Relu), waits=[t_fk] if hs == 0 else [],
                        tick=(hs == 3))
                t_rl = ("scalar", em.ticks["scalar"])
                kwl2 = [t_rl] + ([t2["kkdma"]] if (hg == 0 and t2["kkdma"]) else [])
                for hs in range(4):
                    hc = hg * 4 + hs
                    em.op("vector", lambda h, hc=hc, hs=hs, hg=hg, bb=bb: h.tensor_tensor(
                        kkb[:, hc, :],
                        pb[bb + hs // 2][:, (hs % 2) * 256:(hs % 2) * 256 + 256],
                        rbuf3[:, (hg % 2) * 4 + hs, :], ALU.mult),
                        waits=kwl2 if hs == 0 else [], tick=(hs == 3))
                t2["kk"][hg % 8] = ("vector", em.ticks["vector"])
            wl = [t2["kk"][7]] + ([t2["kkdma"]] if t2["kkdma"] else [])
            t2["kkdma"] = em.dma("sync", kk_h[j, :, :], kkb_f, waits=wl)

    if "s2b" in stages:
        em.stage_barrier()
        ar.reset(); ar.mark()
        fwv = ar.bf16(NHC * C, shape=[NHC, C])
        tok_fwv = em.dma("gpsimd", fwv, d["fWvT"].rearrange("(hc p) co -> p hc co", p=P))
        fwr = ar.bf16(NCC * C, shape=[NCC, C])
        tok_fwr = em.dma("gpsimd", fwr, d["fWrT"].rearrange("(cc p) co -> p cc co", p=P))
        kkin_f = [ar.bf16(NHC * TC) for _ in range(2)]
        kkin = [view3(b, NHC, TC) for b in kkin_f]
        xrin_f = [ar.bf16(NCC * TC) for _ in range(2)]
        xrin = [view3(b, NCC, TC) for b in xrin_f]
        xin2 = ar.f32(2048); xin23 = view3(xin2, 2, 1024)
        x1in2 = ar.f32(2048); x1in23 = view3(x1in2, 2, 1024)
        sr2buf = ar.f32(2048); sr2b3 = view3(sr2buf, 2, 1024)
        rkvbuf = ar.f32(2048); rkvb3 = view3(rkvbuf, 2, 1024)
        accbuf = ar.f32(2048); accb3 = view3(accbuf, 2, 1024)
        xAbuf = ar.f32(2 * 2048); xAb3 = view3(xAbuf, 4, 1024)

        t3 = {"r2": None, "kv": None, "sr2": None, "rkv": None, "var": None,
              "xAdma": [None, None], "acc": None, "apply": None}
        for j in range(NCH):
            pj = j % 2
            kwl = [t3["kv"]] if t3["kv"] else []
            tok_kk = em.dma("sync", kkin_f[pj], kk_h[j, :, :], waits=kwl)
            rwl = [t3["r2"]] if t3["r2"] else []
            tok_xr = em.dma("scalar", xrin_f[pj], xr2_h[j, :, :], waits=rwl)
            awl = [t3["acc"]] if t3["acc"] else []
            tok_x = em.dma("sync", xin23,
                           x_h[j * TC:(j + 1) * TC, :].rearrange("(tt p) c -> p tt c", p=P),
                           waits=awl)
            tok_x1 = em.dma("scalar", x1in23,
                            x1_h[j * TC:(j + 1) * TC, :].rearrange("(tt p) c -> p tt c", p=P),
                            waits=awl)
            rwl = [tok_xr, tok_fwr]
            if t3["sr2"]:
                rwl.append(t3["sr2"])
            first = True
            for tt in range(2):
                for nh in range(2):
                    for cc in range(NCC):
                        em.op("tensor", lambda h, tt=tt, nh=nh, cc=cc, xr_=xrin[pj]: h.matmul(
                            pbank[2 * tt + nh][:, :],
                            xr_[:, cc, tt * 128:tt * 128 + 128],
                            fwr[:, cc, nh * 512:nh * 512 + 512],
                            start=(cc == 0), stop=(cc == NCC - 1)),
                            waits=rwl if first else [],
                            tick=(tt == 1 and nh == 1 and cc == NCC - 1))
                        first = False
            t_r2 = ("tensor", em.ticks["tensor"])
            t3["r2"] = t_r2
            kvwl = [tok_kk, tok_fwv]
            if t3["rkv"]:
                kvwl.append(t3["rkv"])
            first = True
            for tt in range(2):
                for nh in range(2):
                    for hc in range(NHC):
                        em.op("tensor", lambda h, tt=tt, nh=nh, hc=hc, kk_=kkin[pj]: h.matmul(
                            pbank[4 + 2 * tt + nh][:, :],
                            kk_[:, hc, tt * 128:tt * 128 + 128],
                            fwv[:, hc, nh * 512:nh * 512 + 512],
                            start=(hc == 0), stop=(hc == NHC - 1)),
                            waits=kvwl if first else [],
                            tick=(tt == 1 and nh == 1 and hc == NHC - 1))
                        first = False
            t_kv = ("tensor", em.ticks["tensor"])
            t3["kv"] = t_kv
            for tt in range(2):
                for nh in range(2):
                    em.op("scalar", lambda h, tt=tt, nh=nh: h.activation(
                        out=sr2b3[:, tt, nh * 512:nh * 512 + 512],
                        in_=pbank[2 * tt + nh][:, :], func=AF.Sigmoid),
                        waits=[t_r2] if (tt == 0 and nh == 0) else [],
                        tick=(tt == 1 and nh == 1))
            t3["sr2"] = t_sr2 = ("scalar", em.ticks["scalar"])
            for tt in range(2):
                for nh in range(2):
                    em.op("vector", lambda h, tt=tt, nh=nh: h.tensor_tensor(
                        rkvb3[:, tt, nh * 512:nh * 512 + 512],
                        sr2b3[:, tt, nh * 512:nh * 512 + 512],
                        pbank[4 + 2 * tt + nh][:, :], ALU.mult),
                        waits=[t_sr2, t_kv] if (tt == 0 and nh == 0) else [],
                        tick=(tt == 1 and nh == 1))
            t3["rkv"] = ("vector", em.ticks["vector"])
            for tt in range(2):
                em.op("vector", lambda h, tt=tt: h.tensor_tensor(
                    accb3[:, tt, :], xin23[:, tt, :], x1in23[:, tt, :], ALU.add),
                    waits=[tok_x, tok_x1] if tt == 0 else [])
            sq_wl2 = [t3["var"]] if t3["var"] else []
            for tt in range(2):
                em.op("vector", lambda h, tt=tt: h.scalar_tensor_tensor(
                    accb3[:, tt, :], rkvb3[:, tt, :], 1.0, accb3[:, tt, :],
                    ALU.mult, ALU.add, accum_out=statb[:, 0 + tt:1 + tt]),
                    waits=sq_wl2 if tt == 0 else [], tick=(tt == 1))
            t3["acc"] = t_acc = ("vector", em.ticks["vector"])
            for tt in range(2):
                em.op("scalar", lambda h, tt=tt: h.activation(
                    out=rkvb3[:, tt, :], in_=accb3[:, tt, :], func=AF.Square,
                    accum_out=statb[:, 2 + tt:3 + tt]),
                    waits=[t_acc] if tt == 0 else [], tick=(tt == 1))
            t_sq = ("scalar", em.ticks["scalar"])
            em.op("vector", lambda h: h.tensor_scalar(
                statb[:, 4:6], statb[:, 2:4], 1.0 / C, 1e-5, ALU.mult, ALU.add),
                waits=[t_sq], fence=True)
            t_m = em.op("vector", lambda h: h.tensor_scalar_mul(
                statb[:, 6:8], statb[:, 0:2], 1.0 / C), fence=True, tick=True)
            em.op("scalar", lambda h: h.activation(
                out=statb[:, 8:10], in_=statb[:, 6:8], func=AF.Square),
                waits=[t_m], fence=True, tick=True)
            t_m2s = ("scalar", em.ticks["scalar"])
            t3["var"] = t_var = em.op("vector", lambda h: h.tensor_tensor(
                statb[:, 10:12], statb[:, 4:6], statb[:, 8:10], ALU.subtract),
                waits=[t_m2s], tick=True)
            em.op("scalar", lambda h: h.activation(
                out=statb[:, 14:16], in_=statb[:, 10:12], func=AF.Ln),
                waits=[t_var], fence=True)
            t_rstd = em.op("scalar", lambda h: h.activation(
                out=statb[:, 12:14], in_=statb[:, 14:16], func=AF.Exp, scale=-0.5),
                tick=True)
            xa_wl = [t_rstd] + ([t3["xAdma"][pj]] if t3["xAdma"][pj] else [])
            for tt in range(2):
                em.op("vector", lambda h, tt=tt, pj=pj: h.tensor_scalar(
                    xAb3[:, pj * 2 + tt, :], accb3[:, tt, :], statb[:, 6 + tt:7 + tt],
                    statb[:, 12 + tt:13 + tt], ALU.subtract, ALU.mult),
                    waits=xa_wl if tt == 0 else [], tick=(tt == 1))
            t3["apply"] = t_ap = ("vector", em.ticks["vector"])
            dst = xA_h[j * TC:(j + 1) * TC, :].rearrange("(tt p) c -> p tt c", p=P)
            t3["xAdma"][pj] = em.dma("scalar", dst, xAb3[:, pj * 2:pj * 2 + 2, :],
                                     waits=[t_ap])


    if "s3a" in stages or "s3b" in stages:
        em.stage_barrier()
        ar.reset(); ar.mark()
        w1q = ar.bf16(NCC * HID, shape=[NCC, HID])
        hbuf = ar.f32(NHC * 258 + 2)
        hbuf3 = view3(hbuf[:, :NHC * 258], NHC, 258)
        cbuf = ar.f32(NHC * 256)
        cbuf3 = view3(cbuf, NHC, 256)
        stg = [hbuf[:, 0:4096], hbuf[:, 4096:8192]]
        abscols = cbuf[:, 0:40]
        tot = cbuf[:, 40:48]
        row1 = cbuf[0:1, 48:56]
        swc = scl

        def prep_weight(tag, src_ap, ntile, tile_n, qdst, store_dma,
                        prev_done=None):
            abst = []
            toks_ld = []
            for i in range(ntile):
                wl = [abst[i - 2]] if i >= 2 else (
                    [prev_done] if prev_done else [])
                toks_ld.append(em.dma("sync", stg[i % 2][:, 0:tile_n],
                                      src_ap[:, i, :], waits=wl))
                em.op("scalar", lambda h, i=i: h.activation(
                    out=absdst[:, 0:tile_n], in_=stg[i % 2][:, 0:tile_n],
                    func=AF.Abs, accum_out=abscols[:, i:i + 1]),
                    waits=[toks_ld[i]], tick=True)
                abst.append(("scalar", em.ticks["scalar"]))
            t_abs = abst[-1]
            em.op("vector", lambda h: h.reduce_sum(
                tot[:, 0:1], abscols[:, 0:ntile], axis=AX.X), waits=[t_abs],
                fence=True, tick=True)
            t_tot = ("vector", em.ticks["vector"])
            em.op("tensor", lambda h: h.transpose(
                pbank[0][0:1, 0:128], tot[:, 0:1], ident_ap),
                waits=[t_tot, tok_ident], tick=True)
            t_tr = ("tensor", em.ticks["tensor"])
            em.op("vector", lambda h: h.reduce_sum(
                row1[:, 0:1], pbank[0][0:1, 0:128], axis=AX.X), waits=[t_tr],
                fence=True)
            em.op("vector", lambda h: h.tensor_scalar(
                row1[:, 1:2], row1[:, 0:1], 1.0 / (ntile * tile_n * 128), 1e-5,
                ALU.mult, ALU.max), fence=True)
            em.op("vector", lambda h: h.reciprocal(
                row1[:, 2:3], row1[:, 1:2]), fence=True, tick=True)
            t_sw = ("vector", em.ticks["vector"])
            for (srccol, dstcol) in ((2, 0), (1, 1)):
                em.op("tensor", lambda h, srccol=srccol, dstcol=dstcol: h.matmul(
                    pbank[1][:, dstcol:dstcol + 1], onesb[0:1, 0:128],
                    row1[:, srccol:srccol + 1], start=True, stop=True),
                    waits=[t_sw, tok_ones], tick=True)
            t_bc = ("tensor", em.ticks["tensor"])
            base = 0 if tag == "w1" else 2
            em.op("vector", lambda h, base=base: h.tensor_copy(
                swc[:, base:base + 2], pbank[1][:, 0:2]), waits=[t_bc],
                fence=True, tick=True)
            t_swc = ("vector", em.ticks["vector"])
            toks_ld2 = []
            qt = []
            t_q = None
            for i in range(ntile):
                wl = [qt[i - 2]] if i >= 2 else [t_abs]
                toks_ld2.append(em.dma("sync", stg[i % 2][:, 0:tile_n],
                                       src_ap[:, i, :], waits=wl))
                qwl = [toks_ld2[i], t_swc]
                if store_dma is not None and i >= 2:
                    qwl.append(stok[i % 2])
                em.op("vector", lambda h, i=i, base=base: h.tensor_scalar(
                    stg[i % 2][:, 0:tile_n], stg[i % 2][:, 0:tile_n],
                    swc[:, base:base + 1], MAGIC, ALU.mult, ALU.add),
                    waits=qwl)
                em.op("vector", lambda h, i=i: h.tensor_scalar(
                    stg[i % 2][:, 0:tile_n], stg[i % 2][:, 0:tile_n],
                    MAGIC, -1.0, ALU.subtract, ALU.max))
                em.op("vector", lambda h, i=i, qd=qdst(i): h.tensor_scalar_min(
                    qd, stg[i % 2][:, 0:tile_n], 1.0), tick=True)
                qt.append(("vector", em.ticks["vector"]))
                t_q = qt[-1]
                if store_dma is not None:
                    stok[i % 2] = store_dma(i, t_q)
            return t_q

        stok = [None, None]
        def w2_store(i, tok):
            return em.dma("scalar", w2q_h[:, i * 1024:(i + 1) * 1024],
                          w2qstg[i % 2], waits=[tok])
        w2qstg = [ar.bf16(1024) for _ in range(2)]
        absdst = cbuf[:, 1024:5120]

        t_q1 = prep_weight("w1", d["w1T"].rearrange("(cc p) ho -> p cc ho", p=P),
                           NCC, HID, lambda i: w1q[:, i, :], None)
        t_q2 = prep_weight("w2", d["w2T"].rearrange("(hc p) co -> p hc co", p=P),
                           NHC, C, lambda i: w2qstg[i % 2], w2_store,
                           prev_done=t_q1)

    if "s3a" in stages:
        em.stage_barrier()
        xAin = ar.f32(2 * 2048)
        xAin3 = view3(xAin, 4, 1024)
        xqtm_f = ar.bf16(2048)
        xqtm = view3(xqtm_f, 2, 1024)
        xqfm = [ar.bf16(NCC * 260, shape=[NCC, 260]) for _ in range(2)]
        gqb_f = ar.bf16(NHC * 256)
        gqb = view3(gqb_f, NHC, 256)
        s1row = ar.f32(4104)
        s2row = ar.f32(256)
        hscale = ar.f32(264)
        b2scale = ar.f32(256)
        xqscr = ar.f32(1024)

        em.op("vector", lambda h: h.memset(s1row[0:1, :], 0.0))
        em.op("vector", lambda h: h.memset(xqfm[0][:, :, 0], 0.0))
        t_init3 = em.op("vector", lambda h: h.memset(xqfm[1][:, :, 0], 0.0),
                        tick=True)

        t4 = {"qdone": [None, None], "hburst": [None] * 8, "deq": [None] * 16,
              "gqdma": None, "evacs": [None, None], "hlast": None,
              "convdone": None, "i2dma": None, "si": None}

        def emit_quant(q):
            pq = q % 2
            wl = [t4["qdone"][pq]] if t4["qdone"][pq] else []
            tok_xa = em.dma("sync", xAin3[:, pq * 2:pq * 2 + 2, :],
                            xA_h[q * TC:(q + 1) * TC, :].rearrange(
                                "(tt p) c -> p tt c", p=P), waits=wl)
            for tt in range(2):
                em.op("vector", lambda h, tt=tt, pq=pq: h.reduce_max(
                    scl[:, 8 + tt:9 + tt], xAin3[:, pq * 2 + tt, :], axis=AX.X,
                    apply_absolute_value=True),
                    waits=[tok_xa] if tt == 0 else [], fence=(tt == 1))
            em.op("vector", lambda h: h.tensor_scalar_max(
                scl[:, 10:12], scl[:, 8:10], 1e-5), fence=True)
            em.op("vector", lambda h: h.reciprocal(
                scl[:, 12:14], scl[:, 10:12]), fence=True)
            em.op("vector", lambda h: h.tensor_scalar_mul(
                scl[:, 14:16], scl[:, 12:14], 127.0), fence=True)
            t_sc = em.op("vector", lambda h: h.tensor_scalar(
                scl[:, 16:18], scl[:, 10:12], 1.0 / 127, swc[:, 1:2],
                ALU.mult, ALU.mult), fence=True, tick=True)
            for tt in range(2):
                em.op("vector", lambda h, tt=tt, pq=pq: h.tensor_scalar(
                    xqscr, xAin3[:, pq * 2 + tt, :], scl[:, 14 + tt:15 + tt],
                    MAGIC, ALU.mult, ALU.add), waits=[t_sc] if tt == 0 else [])
                em.op("vector", lambda h, tt=tt: h.tensor_scalar_sub(
                    xqtm[:, tt, :], xqscr, MAGIC), tick=(tt == 1))
            t_xq = ("vector", em.ticks["vector"])
            stwl = [t_sc] + ([t4["qdone"][1 - pq]] if t4["qdone"][1 - pq] else [])
            for tt in range(2):
                em.op("tensor", lambda h, tt=tt: h.transpose(
                    pbank[4][0:1, tt * 128:tt * 128 + 128],
                    scl[:, 16 + tt:17 + tt], ident_ap),
                    waits=stwl if tt == 0 else [], tick=(tt == 1))
            t_rt = ("tensor", em.ticks["tensor"])
            em.op("vector", lambda h, q=q: h.tensor_copy(
                s1row[0:1, q * TC + 1:q * TC + 257], pbank[4][0:1, 0:256]),
                waits=[t_rt], fence=True, tick=True)
            t_s1r = ("vector", em.ticks["vector"])
            twl = [t_xq]
            if t4["hlast"]:
                twl.append(t4["hlast"])
            if t4["deq"][15]:
                twl.append(t4["deq"][15])
            first = True
            for bank in range(4):
                for cc in (bank, bank + 4):
                    for tt in range(2):
                        em.op("tensor", lambda h, cc=cc, tt=tt: h.transpose(
                            pb[cc % 4].bitcast(BF16)[:, (cc // 4) * 512 + tt * 128:
                                                     (cc // 4) * 512 + tt * 128 + 128],
                            xqtm[:, tt, cc * 128:(cc + 1) * 128], identb16),
                            waits=(twl + [tok_ident16]) if first else [],
                            tick=(bank == 3 and cc == 7 and tt == 1))
                        first = False
            t_tr = ("tensor", em.ticks["tensor"])
            ewl = [t_tr]
            if t4["hburst"][7]:
                ewl.append(t4["hburst"][7])
            for cc in range(NCC):
                em.op("scalar", lambda h, cc=cc, pq=pq: h.activation(
                    out=xqfm[pq][:, cc, 1:257],
                    in_=pb[cc % 4].bitcast(BF16)[:, (cc // 4) * 512:
                                                 (cc // 4) * 512 + 256],
                    func=AF.Identity), waits=ewl if cc == 0 else [],
                    tick=(cc == NCC - 1))
            t4["evacs"][pq] = ("scalar", em.ticks["scalar"])
            t4["qdone"][pq] = t_s1r
            return t4["evacs"][pq]

        emit_quant(0)
        for j in range(NCH):
            pj = j % 2
            if j < NCH - 1:
                emit_quant(j + 1)
                em.op("vector", lambda h, pj=pj: h.tensor_copy(
                    xqfm[1 - pj][:, :, 0], xqfm[pj][:, :, 256]),
                    waits=[t4["evacs"][pj], t4["evacs"][1 - pj]], fence=True)
                em.op("vector", lambda h, pj=pj: h.tensor_copy(
                    xqfm[pj][:, :, 257], xqfm[1 - pj][:, :, 1]),
                    fence=True, tick=True)
            else:
                em.op("vector", lambda h, pj=pj: h.memset(
                    xqfm[pj][:, :, 257], 0.0),
                    waits=[t4["evacs"][pj]], fence=True, tick=True)
            t_bnd = ("vector", em.ticks["vector"])
            em.op("tensor", lambda h, j=j: h.matmul(
                pbank[5][:, 0:258], onesb[0:1, 0:128],
                s1row[0:1, j * TC:j * TC + 258], start=True, stop=True),
                waits=[t4["qdone"][pj], tok_ones] +
                      ([t4["b2"]] if t4.get("b2") else []), tick=True)
            t_hb = ("tensor", em.ticks["tensor"])
            em.op("vector", lambda h: h.tensor_copy(
                hscale[:, 0:258], pbank[5][:, 0:258]), waits=[t_hb], tick=True)
            t_hs = ("vector", em.ticks["vector"])
            for hb in range(16):
                first = True
                bb = (hb % 2) * 2
                for hs in range(2):
                    hc = hb * 2 + hs
                    mwl = [t_bnd, t_hs, t_q1]
                    if t4["deq"][(hb - 2) % 16]:
                        mwl.append(t4["deq"][(hb - 2) % 16])
                    if hb <= 1:
                        mwl.append(t4["evacs"][1 - pj] if j < NCH - 1 else
                                   t4["evacs"][pj])
                    for cc in range(NCC):
                        em.op("tensor",
                              lambda h, hc=hc, hs=hs, cc=cc, bb=bb, sl=xqfm[pj]: h.matmul(
                                  pbank[bb + hs][:, 0:258],
                                  w1q[:, cc, hc * 128:(hc + 1) * 128],
                                  sl[:, cc, 0:258],
                                  start=(cc == 0), stop=(cc == NCC - 1)),
                              waits=mwl if first else [],
                              tick=(hs == 1 and cc == NCC - 1))
                        first = False
                t4["hburst"][hb % 8] = t_hm = ("tensor", em.ticks["tensor"])
                if hb == 15:
                    t4["hlast"] = t_hm
                for hs in range(2):
                    hc = hb * 2 + hs
                    em.op("vector", lambda h, hc=hc, hs=hs, bb=bb: h.tensor_tensor(
                        hbuf3[:, hc, :], pbank[bb + hs][:, 0:258], hscale[:, 0:258],
                        ALU.mult), waits=[t_hm] if hs == 0 else [],
                        tick=(hs == 1))
                t4["deq"][(hb) % 16] = ("vector", em.ticks["vector"])
            cwl = [t4["deq"][15]] + ([t4["si"]] if t4["si"] else [])
            for hc in range(NHC):
                em.op("vector", lambda h, hc=hc: h.tensor_scalar_mul(
                    cbuf3[:, hc, :], hbuf3[:, hc, 0:256],
                    convwb[:, hc * 3:hc * 3 + 1]),
                    waits=cwl if hc == 0 else [])
                em.op("vector", lambda h, hc=hc: h.scalar_tensor_tensor(
                    cbuf3[:, hc, :], hbuf3[:, hc, 1:257],
                    convwb[:, hc * 3 + 1:hc * 3 + 2], cbuf3[:, hc, :],
                    ALU.mult, ALU.add))
                em.op("vector", lambda h, hc=hc: h.scalar_tensor_tensor(
                    cbuf3[:, hc, :], hbuf3[:, hc, 2:258],
                    convwb[:, hc * 3 + 2:hc * 3 + 3], cbuf3[:, hc, :],
                    ALU.mult, ALU.add), tick=(hc == NHC - 1))
            t4["convdone"] = t_cv = ("vector", em.ticks["vector"])
            for hc in range(NHC):
                em.op("scalar", lambda h, hc=hc: h.activation(
                    out=hbuf3[:, hc, 0:256], in_=cbuf3[:, hc, :], func=AF.Silu,
                    bias=convbb[:, hc:hc + 1]), waits=[t_cv] if hc == 0 else [],
                    tick=(hc == NHC - 1))
            t4["si"] = t_si = ("scalar", em.ticks["scalar"])
            em.op("vector", lambda h: h.reduce_max(
                cbuf3[:, 30, :], hbuf3[:, :, 0:256].transpose([0, 2, 1]),
                axis=AX.X, apply_absolute_value=True), waits=[t_si],
                fence=True, tick=True)
            t_tm = ("vector", em.ticks["vector"])
            for tt in range(2):
                em.op("tensor", lambda h, tt=tt: h.transpose(
                    pbank[6][:, tt * 128:tt * 128 + 128],
                    cbuf3[:, 30, tt * 128:tt * 128 + 128], ident_ap),
                    waits=[t_tm] if tt == 0 else [], tick=(tt == 1))
            t_tt = ("tensor", em.ticks["tensor"])
            for tt in range(2):
                em.op("vector", lambda h, tt=tt: h.reduce_max(
                    scl[:, 20 + tt:21 + tt], pbank[6][:, tt * 128:tt * 128 + 128],
                    axis=AX.X), waits=[t_tt] if tt == 0 else [], fence=(tt == 1))
            em.op("vector", lambda h: h.tensor_scalar_max(
                scl[:, 22:24], scl[:, 20:22], 1e-5), fence=True)
            em.op("vector", lambda h: h.reciprocal(
                scl[:, 24:26], scl[:, 22:24]), fence=True)
            em.op("vector", lambda h: h.tensor_scalar_mul(
                scl[:, 26:28], scl[:, 24:26], 127.0), fence=True)
            t_i2 = em.op("vector", lambda h: h.tensor_scalar(
                scl[:, 28:30], scl[:, 22:24], 1.0 / 127, swc[:, 3:4],
                ALU.mult, ALU.mult), fence=True, tick=True)
            i2wl = [t_i2] + ([t4["i2dma"]] if t4["i2dma"] else [])
            t4["i2dma"] = em.dma("scalar",
                                 inv2_h[2 * j:2 * j + 2, :].rearrange("t p -> p t"),
                                 scl[:, 28:30], waits=i2wl, slow=True)
            for tt in range(2):
                em.op("tensor", lambda h, tt=tt: h.transpose(
                    pbank[7][0:1, tt * 128:tt * 128 + 128],
                    scl[:, 26 + tt:27 + tt], ident_ap),
                    waits=[t_i2] if tt == 0 else [], tick=(tt == 1))
            t_s2t = ("tensor", em.ticks["tensor"])
            em.op("vector", lambda h: h.tensor_copy(
                s2row[0:1, 0:256], pbank[7][0:1, 0:256]), waits=[t_s2t],
                fence=True, tick=True)
            t_s2r = ("vector", em.ticks["vector"])
            em.op("tensor", lambda h: h.matmul(
                pbank[7][:, 256:512], onesb[0:1, 0:128],
                s2row[0:1, 0:256], start=True, stop=True), waits=[t_s2r],
                tick=True)
            t_b2m = ("tensor", em.ticks["tensor"])
            em.op("vector", lambda h: h.tensor_copy(
                b2scale, pbank[7][:, 256:512]), waits=[t_b2m], tick=True)
            t_b2 = ("vector", em.ticks["vector"])
            t4["b2"] = t_b2
            gwl = [t_b2] + ([t4["gqdma"]] if t4["gqdma"] else [])
            for hc in range(NHC):
                em.op("vector", lambda h, hc=hc: h.tensor_tensor(
                    cbuf3[:, hc, :], hbuf3[:, hc, 0:256], b2scale, ALU.mult),
                    waits=gwl if hc == 0 else [], tick=(hc == NHC - 1))
            t_gm = ("vector", em.ticks["vector"])
            em.op("vector", lambda h: h.tensor_scalar(
                gqb_f, cbuf, MAGIC, MAGIC, ALU.add, ALU.subtract),
                waits=[t_gm], tick=True)
            t_gq = ("vector", em.ticks["vector"])
            t4["gqdma"] = em.dma("sync", gq_h[j, :, :], gqb_f, waits=[t_gq])

    if "s3b" in stages:
        em.stage_barrier()
        ar.reset(); ar.mark()
        w2qb = ar.bf16(NHC * C, shape=[NHC, C])
        tok_w2 = em.dma("gpsimd", w2qb, w2q_h[:, :].rearrange(
            "p (hc co) -> p hc co", hc=NHC))
        gqin_f = [ar.bf16(NHC * TC) for _ in range(2)]
        gqin = [view3(b, NHC, TC) for b in gqin_f]
        xAin2 = ar.f32(2048); xAin23 = view3(xAin2, 2, 1024)
        acc5b = ar.f32(2048); acc5b3 = view3(acc5b, 2, 1024)
        sqscr = ar.f32(2048); sqscr3 = view3(sqscr, 2, 1024)
        outb = ar.f32(2 * 2048); outb3 = view3(outb, 4, 1024)

        t5 = {"mm": None, "acc": None, "var": None, "odma": [None, None],
              "ap": None, "accs": [None, None]}
        for j in range(NCH):
            pj = j % 2
            gwl = [t5["mm"]] if t5["mm"] else []
            tok_gq = em.dma("sync", gqin_f[pj], gq_h[j, :, :], waits=gwl)
            awl = [t5["acc"]] if t5["acc"] else []
            tok_xa = em.dma("scalar", xAin23,
                            xA_h[j * TC:(j + 1) * TC, :].rearrange(
                                "(tt p) c -> p tt c", p=P), waits=awl)
            iwl = [t5["acc"]] if t5["acc"] else []
            tok_i2 = em.dma("scalar", scl[:, 0:2],
                            inv2_h[2 * j:2 * j + 2, :].rearrange("t p -> p t"),
                            waits=iwl, slow=True)
            first = True
            pbase = (j % 2) * 4
            accw = t5["accs"][pj] if t5.get("accs") else None
            for tt in range(2):
                for nh in range(2):
                    for hc in range(NHC):
                        em.op("tensor",
                              lambda h, tt=tt, nh=nh, hc=hc, gq_=gqin[pj],
                              pbase=pbase: h.matmul(
                                  pbank[pbase + 2 * tt + nh][:, :],
                                  gq_[:, hc, tt * 128:tt * 128 + 128],
                                  w2qb[:, hc, nh * 512:nh * 512 + 512],
                                  start=(hc == 0), stop=(hc == NHC - 1)),
                              waits=([tok_gq, tok_w2] +
                                     ([accw] if accw else []))
                              if first else [],
                              tick=(tt == 1 and nh == 1 and hc == NHC - 1))
                        first = False
            t5["mm"] = t_mm = ("tensor", em.ticks["tensor"])
            for tt in range(2):
                for nh in range(2):
                    em.op("vector", lambda h, tt=tt, nh=nh, pbase=pbase:
                          h.scalar_tensor_tensor(
                              acc5b3[:, tt, nh * 512:nh * 512 + 512],
                              pbank[pbase + 2 * tt + nh][:, :], scl[:, 0 + tt:1 + tt],
                              xAin23[:, tt, nh * 512:nh * 512 + 512],
                              ALU.mult, ALU.add,
                              accum_out=scl[:, 4 + 2 * tt + nh:5 + 2 * tt + nh]),
                          waits=[t_mm, tok_xa, tok_i2] if (tt == 0 and nh == 0)
                          else [], tick=(tt == 1 and nh == 1))
            t5["acc"] = t_acc = ("vector", em.ticks["vector"])
            t5["accs"][pj] = t_acc
            em.op("vector", lambda h: h.tensor_tensor(
                statb[:, 0:1], scl[:, 4:5], scl[:, 5:6], ALU.add), waits=[t_acc],
                fence=True)
            em.op("vector", lambda h: h.tensor_tensor(
                statb[:, 1:2], scl[:, 6:7], scl[:, 7:8], ALU.add), fence=True,
                tick=True)
            t_sum = ("vector", em.ticks["vector"])
            sq_wl3 = [t_acc] + ([t5["var"]] if t5["var"] else [])
            for tt in range(2):
                em.op("scalar", lambda h, tt=tt: h.activation(
                    out=sqscr3[:, tt, :], in_=acc5b3[:, tt, :], func=AF.Square,
                    accum_out=statb[:, 2 + tt:3 + tt]),
                    waits=sq_wl3 if tt == 0 else [], tick=(tt == 1))
            t_sq = ("scalar", em.ticks["scalar"])
            em.op("vector", lambda h: h.tensor_scalar(
                statb[:, 4:6], statb[:, 2:4], 1.0 / C, 1e-5, ALU.mult, ALU.add),
                waits=[t_sq], fence=True)
            t_m = em.op("vector", lambda h: h.tensor_scalar_mul(
                statb[:, 6:8], statb[:, 0:2], 1.0 / C), waits=[t_sum],
                fence=True, tick=True)
            em.op("scalar", lambda h: h.activation(
                out=statb[:, 8:10], in_=statb[:, 6:8], func=AF.Square),
                waits=[t_m], fence=True, tick=True)
            t_m2s = ("scalar", em.ticks["scalar"])
            t5["var"] = t_var = em.op("vector", lambda h: h.tensor_tensor(
                statb[:, 10:12], statb[:, 4:6], statb[:, 8:10], ALU.subtract),
                waits=[t_m2s], tick=True)
            em.op("scalar", lambda h: h.activation(
                out=statb[:, 14:16], in_=statb[:, 10:12], func=AF.Ln),
                waits=[t_var], fence=True)
            t_rstd = em.op("scalar", lambda h: h.activation(
                out=statb[:, 12:14], in_=statb[:, 14:16], func=AF.Exp, scale=-0.5),
                tick=True)
            owl = [t_rstd] + ([t5["odma"][pj]] if t5["odma"][pj] else [])
            for tt in range(2):
                em.op("vector", lambda h, tt=tt, pj=pj: h.tensor_scalar(
                    outb3[:, pj * 2 + tt, :], acc5b3[:, tt, :],
                    statb[:, 6 + tt:7 + tt], statb[:, 12 + tt:13 + tt],
                    ALU.subtract, ALU.mult), waits=owl if tt == 0 else [],
                    tick=(tt == 1))
            t_ap = ("vector", em.ticks["vector"])
            dst = out_h[j * TC:(j + 1) * TC, :].rearrange("(tt p) c -> p tt c", p=P)
            t5["odma"][pj] = em.dma("sync", dst, outb3[:, pj * 2:pj * 2 + 2, :],
                                    waits=[t_ap])

    return nc, em, d, ar, (sb_ctx, ps_ctx)


def finalize2(nc, em):
    for q in DMAQ:
        fw = [(f"dma_{q}{l}", 16 * em.dmas[q][l])
              for l in range(NLANES) if em.dmas[q][l]]
        if fw:
            em.ops[q].append([lambda h: h.nop(), fw, None])
    em.check_deadlock()
    import contextlib
    stack = contextlib.ExitStack()
    sems = {e: stack.enter_context(nc.semaphore("sem_" + e)) for e in ENGINES}
    dma_sems = {f"{q}{l}": stack.enter_context(nc.semaphore(f"dsem_{q}{l}"))
                for q in DMAQ for l in range(NLANES)}
    with nc.Block() as block:
        def emit_engine(e):
            def inner(h):
                last_wait = {}
                for fn, waits, kind in em.ops[e]:
                    for w in waits:
                        if w is None:
                            continue
                        name, val = w
                        if last_wait.get(name, -1) >= val:
                            continue
                        last_wait[name] = val
                        sem = dma_sems[name[4:]] if name.startswith("dma_") else sems[name]
                        h.wait_ge(sem, val)
                    ins = fn(h)
                    if kind == "tick":
                        h.drain()
                        h.nop().then_inc(sems[e], 1)
                    elif kind == "fence":
                        h.drain()
                    elif isinstance(kind, tuple) and kind[0] == "dma":
                        ins.then_inc(dma_sems[f"{e}{kind[1]}"], 16)
            return inner

        block.tensor(emit_engine("tensor"))
        block.vector(emit_engine("vector"))
        block.scalar(emit_engine("scalar"))
        block.gpsimd(emit_engine("gpsimd"))
        block.sync(emit_engine("sync"))
    stack.close()
    return nc


_CACHE = {}


def kernel(**inputs):
    shared, xs = host_prep(inputs)
    if "nc" not in _CACHE:
        nc, em, d, ar, ctxs = build()
        finalize2(nc, em)
        _CACHE["nc"] = nc
    nc = _CACHE["nc"]
    in_maps = []
    for b in range(8):
        m = dict(shared)
        m["x"] = xs[b]
        in_maps.append(m)
    res = run_bass_kernel_spmd(nc, in_maps, core_ids=list(range(8)))
    out = np.stack([np.asarray(res.results[b]["out"]) for b in range(8)])
    return out.astype(np.float32, copy=False)
